# version 11
# baseline (speedup 1.0000x reference)
import contextlib
import numpy as np
import ml_dtypes
import concourse.bass as bass
import concourse.mybir as mybir
from concourse.bass_utils import run_bass_kernel_spmd

F32 = mybir.dt.float32
BF16 = mybir.dt.bfloat16
AF = mybir.ActivationFunctionType
ALU = mybir.AluOpType
AX = mybir.AxisListType

S = 8192
D = 1024
NH = 16
HD = 64
CHK = 64
TILE = 512
NT = S // TILE
NCK = TILE // CHK
HALF = 4096
DI = 2816
EPS = 1e-5

ENGS = ("sync", "tensor", "vector", "scalar", "gpsimd")


class Bld:
    def __init__(self, nc, es):
        self.nc = nc
        self.es = es
        self.esem = {e: es.enter_context(nc.semaphore("se_" + e)) for e in ENGS}
        self.ecnt = {e: 0 for e in ENGS}
        self.ops = {e: [] for e in ENGS}
        self.waited = {e: {} for e in ENGS}
        self.wr = {}
        self.rd = {}
        self.chans = {}

    def chan(self, name):
        if name not in self.chans:
            sem = self.es.enter_context(self.nc.semaphore("ch_" + name))
            self.chans[name] = [sem, 0, "c_" + name]
        return name

    def _wait(self, eng, tok):
        nm, sem, v = tok
        if self.waited[eng].get(nm, 0) >= v:
            return
        self.waited[eng][nm] = v
        self.ops[eng].append(("w", sem, v))

    def hazards(self, eng, r=(), w=()):
        for k in r:
            t = self.wr.get(k)
            if t is not None:
                self._wait(eng, t)
        for k in w:
            t = self.wr.get(k)
            if t is not None:
                self._wait(eng, t)
            for t2 in self.rd.get(k, ()):
                self._wait(eng, t2)

    def op(self, eng, fn, r=(), w=(), chan=None, raw=False, chanwait=True):
        if chan is not None:
            c = self.chans[chan]
            if chanwait and c[1] > 0:
                self._wait(eng, (c[2], c[0], c[1]))
        self.hazards(eng, r, w)
        if raw:
            self.ops[eng].append(("r", fn))
            return None
        if chan is None:
            self.ecnt[eng] += 1
            sem, v, nm, isd = self.esem[eng], self.ecnt[eng], "e_" + eng, False
        else:
            c = self.chans[chan]
            c[1] += 16
            sem, v, nm, isd = c[0], c[1], c[2], True
        tok = (nm, sem, v)
        self.ops[eng].append(("i", fn, sem, isd))
        for k in r:
            self.rd.setdefault(k, []).append(tok)
        for k in w:
            self.wr[k] = tok
            self.rd[k] = []
        return tok

    def settok(self, tok, r=(), w=()):
        for k in r:
            self.rd.setdefault(k, []).append(tok)
        for k in w:
            self.wr[k] = tok
            self.rd[k] = []

    def dma_group(self, eng, chan, items):
        for fn, r, w in items:
            self.hazards(eng, r, w)
        last = None
        for i, (fn, r, w) in enumerate(items):
            last = self.op(eng, fn, chan=chan, chanwait=(i == 0))
        allr = [k for _, r, _ in items for k in r]
        allw = [k for _, _, w in items for k in w]
        self.settok(last, r=allr, w=allw)

    def mmg(self, bank_key, items):
        self.hazards("tensor", w=[bank_key])
        n = len(items)
        union = []
        for _, rk in items:
            for k in rk:
                if k not in union:
                    union.append(k)
        for i, (fn, rk) in enumerate(items):
            if i < n - 1:
                self.op("tensor", fn, r=rk, raw=True)
            else:
                tok = self.op("tensor", fn, r=rk, w=[bank_key])
                extra = [k for k in union if k not in rk]
                self.settok(tok, r=extra)

    def barrier(self):
        toks = [(nm, sem, cnt) for (sem, cnt, nm) in self.chans.values()]
        toks += [("e_" + e, self.esem[e], self.ecnt[e]) for e in ENGS]
        for e in ENGS:
            for tk in toks:
                if tk[0] != "e_" + e and tk[2] > 0:
                    self._wait(e, tk)
        self.wr = {}
        self.rd = {}

    def finalize(self, block):
        for e in ENGS:
            ops = self.ops[e]

            def make(ops):
                def body(h):
                    for it in ops:
                        if it[0] == "w":
                            h.wait_ge(it[1], it[2])
                        elif it[0] == "r":
                            it[1](h)
                        else:
                            it[1](h).then_inc(it[2], 16 if it[3] else 1)
                return body

            getattr(block, e)(make(ops))


def build_nc(upto=4, s3mode=9):
    nc = bass.Bass("TRN2", target_bir_lowering=False)
    es = contextlib.ExitStack()

    def din(name, shape):
        return nc.dram_tensor(name, list(shape), F32, kind="ExternalInput").ap()

    def dsc(name, shape):
        return nc.dram_tensor(name, list(shape), F32, kind="Internal").ap()

    x0T = din("x0T", (D, S))
    xbT = din("xbT", (D, S))
    xf_in = din("xf", (HALF, D))
    xfT = din("xfT", (D, HALF))
    m16 = din("m16", (NH, S))
    W_inT = din("W_inT", (D, 3088))
    convw = din("convw", (3072, 4))
    convb = din("convb", (3072, 1))
    A_param = din("A_param", (NH, 1))
    dt_bias = din("dt_bias", (NH, 1))
    W_outT = din("W_outT", (D, D))
    wg_in = din("wg", (DI, D))
    wu_in = din("wu", (DI, D))
    wd_in = din("wd", (D, DI))
    ident_in = din("ident_in", (128, 128))
    identb_in = nc.dram_tensor("identb_in", [128, 128], BF16, kind="ExternalInput").ap()
    umask_in = din("umask_in", (128, 128))
    onesc_in = din("onesc_in", (128, 1))
    ones1_in = din("ones1_in", (1, 128))
    cst_in = din("cst_in", (128, 2))
    out_h = nc.dram_tensor("out", [HALF, D], F32, kind="ExternalOutput").ap()

    def dscb(name, shape):
        return nc.dram_tensor(name, list(shape), BF16, kind="Internal").ap()

    cvB = dsc("cvB", (D, S))
    cvX = dsc("cvX", (D, S))
    cvC = dsc("cvC", (D, HALF))
    wgqT_h = dscb("wgqT_h", (D, DI))
    wgqT_l = dscb("wgqT_l", (D, DI))
    wuqT_h = dscb("wuqT_h", (D, DI))
    wuqT_l = dscb("wuqT_l", (D, DI))
    wdqT_h = dscb("wdqT_h", (DI, D))
    wdqT_l = dscb("wdqT_l", (DI, D))
    WinT_h = dscb("WinT_h", (D, 3088))
    WinT_l = dscb("WinT_l", (D, 3088))
    WoT_h = dscb("WoT_h", (D, D))
    WoT_l = dscb("WoT_l", (D, D))
    rows_em_d = dsc("rows_em", (NH, S))
    rows_ep_d = dsc("rows_ep", (NH, S))
    rows_nv_d = dsc("rows_nv", (NH, S))
    YnT = dsc("YnT", (D, HALF))

    AR = 42496
    arena = es.enter_context(nc.sbuf_tensor("arena", [128, AR], F32)).ap()
    ident = es.enter_context(nc.sbuf_tensor("ident", [128, 128], F32)).ap()
    identb = es.enter_context(nc.sbuf_tensor("identb", [128, 128], BF16)).ap()
    umask = es.enter_context(nc.sbuf_tensor("umask", [128, 128], F32)).ap()
    onesc = es.enter_context(nc.sbuf_tensor("onesc", [128, 1], F32)).ap()
    ones1 = es.enter_context(nc.sbuf_tensor("ones1", [1, 128], F32)).ap()
    apar = es.enter_context(nc.sbuf_tensor("apar", [NH, 1], F32)).ap()
    dtb = es.enter_context(nc.sbuf_tensor("dtb", [NH, 1], F32)).ap()
    cwb = es.enter_context(nc.sbuf_tensor("cwb", [128, 5 * 24], F32)).ap()
    swr = es.enter_context(nc.sbuf_tensor("swr", [NH, 1], F32)).ap()
    gtab = es.enter_context(nc.sbuf_tensor("gtab", [NH, 160], F32)).ap()
    carry = es.enter_context(nc.sbuf_tensor("carry", [NH, 8], F32)).ap()
    mrow = es.enter_context(nc.sbuf_tensor("mrow", [NH, TILE], F32)).ap()
    srow = es.enter_context(nc.sbuf_tensor("srow", [1, 8 * TILE], F32)).ap()
    rowsb = es.enter_context(nc.sbuf_tensor("rowsb", [NH, 8 * TILE], F32)).ap()
    fstat = es.enter_context(nc.sbuf_tensor("fstat", [128, 16], F32)).ap()
    cst = es.enter_context(nc.sbuf_tensor("cst", [128, 2], F32)).ap()
    wdt_s = es.enter_context(nc.sbuf_tensor("wdt_sb", [128, 128], F32)).ap()

    pb = [es.enter_context(nc.psum_tensor("pbank%d" % i, [128, 512], F32)).ap()
          for i in range(8)]
    PB = ["pb%d" % i for i in range(8)]

    b = Bld(nc, es)

    R_dt, R_A, R_cum, R_em, R_ep, R_S, R_t0, R_t1 = range(8)

    def rslot(i):
        return rowsb[0:16, i * TILE:(i + 1) * TILE]

    def sslot(i):
        return srow[0:1, i * TILE:(i + 1) * TILE]

    ldc = b.chan("const")
    citems = [
        (lambda h: h.dma_start(out=ident, in_=ident_in), (), ["ident"]),
        (lambda h: h.dma_start(out=identb, in_=identb_in), (), ["identb"]),
        (lambda h: h.dma_start(out=umask, in_=umask_in), (), ["umask"]),
        (lambda h: h.dma_start(out=onesc, in_=onesc_in), (), ["onesc"]),
        (lambda h: h.dma_start(out=ones1, in_=ones1_in), (), ["ones1"]),
        (lambda h: h.dma_start(out=apar, in_=A_param), (), ["apar"]),
        (lambda h: h.dma_start(out=dtb, in_=dt_bias), (), ["dtb"]),
        (lambda h: h.dma_start(out=cst, in_=cst_in), (), ["cst"]),
    ]
    for fc in range(24):
        citems.append((lambda h, fc=fc: h.dma_start(
            out=cwb[:, 5 * fc:5 * fc + 4], in_=convw[fc * 128:(fc + 1) * 128, :]),
            (), ["cwb"]))
        citems.append((lambda h, fc=fc: h.dma_start(
            out=cwb[:, 5 * fc + 4:5 * fc + 5], in_=convb[fc * 128:(fc + 1) * 128, :]),
            (), ["cwb"]))
    b.dma_group("sync", ldc, citems)

    wQ = arena[:, 30848:33936]
    wS = arena[:, 33936:36752]
    wq1 = arena[:, 36752:39568]
    wsc = arena[:, 39568:39952]
    wSb = wS.bitcast(BF16)
    wstb = wq1.bitcast(BF16)

    c_wdt = b.chan("wdtl")
    c_w = b.chan("s0w")
    c_st = b.chan("s0st")

    b.dma_group("sync", c_wdt, [
        (lambda h, kc=kc: h.dma_start(
            out=wdt_s[:, kc * 16:(kc + 1) * 16],
            in_=W_inT[kc * 128:(kc + 1) * 128, 3072:3088]), (), ["wdt_s"])
        for kc in range(8)])
    b.mmg(PB[0], [
        (lambda h, kc=kc: h.matmul(pb[0][0:1, 0:16], onesc,
                                   wdt_s[:, kc * 16:(kc + 1) * 16],
                                   start=(kc == 0), stop=(kc == 7)),
         ["wdt_s", "onesc"]) for kc in range(8)])
    b.op("vector", lambda h: h.tensor_copy(gtab[0:1, 128:144], pb[0][0:1, 0:16]),
         r=[PB[0]], w=["swrow"])
    b.op("tensor", lambda h: h.transpose(pb[1][0:16, 0:1], gtab[0:1, 128:144], ident[0:1, 0:1]),
         r=["swrow", "ident"], w=[PB[1]])
    b.op("vector", lambda h: h.tensor_copy(swr, pb[1][0:16, 0:1]), r=[PB[1]], w=["swr"])

    def ladder(v_ap, q_ap, vkey, qkey, eng, claim=(), tmp_ap=None, tmpkey=None,
               tmpclaim=()):
        b.op(eng, lambda h: h.tensor_single_scalar(q_ap, v_ap, 0.25, ALU.is_gt),
             r=[vkey], w=[qkey] + list(claim))
        if eng == "vector":
            for c in (0.75, 1.25, 1.75):
                b.op(eng, lambda h, c=c: h.scalar_tensor_tensor(
                    q_ap, v_ap, c, q_ap, ALU.is_gt, ALU.add), r=[vkey, qkey], w=[qkey])
            b.op(eng, lambda h: h.tensor_single_scalar(q_ap, q_ap, 0.5, ALU.mult),
                 r=[qkey], w=[qkey])
            for c in (2.5, 3.5, 5.0, 5.0):
                b.op(eng, lambda h, c=c: h.scalar_tensor_tensor(
                    q_ap, v_ap, c, q_ap, ALU.is_gt, ALU.add), r=[vkey, qkey], w=[qkey])
        else:
            first = True
            for c in (0.75, 1.25, 1.75):
                b.op(eng, lambda h, c=c: h.tensor_single_scalar(
                    tmp_ap, v_ap, c, ALU.is_gt),
                    r=[vkey], w=[tmpkey] + (list(tmpclaim) if first else []))
                first = False
                b.op(eng, lambda h: h.tensor_tensor(q_ap, q_ap, tmp_ap, ALU.add),
                     r=[qkey, tmpkey], w=[qkey])
            b.op(eng, lambda h: h.tensor_single_scalar(q_ap, q_ap, 0.5, ALU.mult),
                 r=[qkey], w=[qkey])
            for c in (2.5, 3.5, 5.0, 5.0):
                b.op(eng, lambda h, c=c: h.tensor_single_scalar(
                    tmp_ap, v_ap, c, ALU.is_gt), r=[vkey], w=[tmpkey])
                b.op(eng, lambda h: h.tensor_tensor(q_ap, q_ap, tmp_ap, ALU.add),
                     r=[qkey, tmpkey], w=[qkey])

    def quant_tm(x_ap, xkey, sgn_ap, skey, q_ap, qkey, scl_ap, sclkey, n, eng="vector",
                 claim=(), sclaim=(), sclclaim=(), tmp_ap=None, tmpkey=None,
                 tmpclaim=()):
        nb = n // 16
        if eng == "gpsimd" and tmp_ap is None:
            eng = "vector"
        b.op("vector", lambda h: h.tensor_reduce(
            scl_ap[:, 0:nb], x_ap.rearrange("p (bb k) -> p bb k", k=16),
            AX.X, ALU.max, apply_absolute_value=True),
            r=[xkey], w=[sclkey] + list(sclclaim))
        b.op(eng, lambda h: h.tensor_scalar(
            scl_ap[:, 0:nb], scl_ap[:, 0:nb], 1.0 / 6.0, 1e-12, ALU.mult, ALU.max),
            r=[sclkey], w=[sclkey])
        b.op("vector", lambda h: h.reciprocal(scl_ap[:, nb:2 * nb], scl_ap[:, 0:nb]),
             r=[sclkey], w=[sclkey + "i"])
        b.op("scalar", lambda h: h.activation(sgn_ap, x_ap, AF.Sign),
             r=[xkey], w=[skey] + list(sclaim))
        b.op("scalar", lambda h: h.activation(x_ap, x_ap, AF.Abs), r=[xkey, skey], w=[xkey])
        b.op(eng, lambda h: h.tensor_tensor(
            x_ap.rearrange("p (bb k) -> p bb k", k=16),
            x_ap.rearrange("p (bb k) -> p bb k", k=16),
            scl_ap[:, nb:2 * nb].unsqueeze(2).to_broadcast([128, nb, 16]), ALU.mult),
            r=[xkey, sclkey + "i"], w=[xkey])
        ladder(x_ap, q_ap, xkey, qkey, eng, claim=claim,
               tmp_ap=tmp_ap, tmpkey=tmpkey, tmpclaim=tmpclaim)
        b.op(eng, lambda h: h.tensor_tensor(q_ap, q_ap, sgn_ap, ALU.mult),
             r=[qkey, skey], w=[qkey])
        b.op(eng, lambda h: h.tensor_tensor(
            q_ap.rearrange("p (bb k) -> p bb k", k=16),
            q_ap.rearrange("p (bb k) -> p bb k", k=16),
            scl_ap[:, 0:nb].unsqueeze(2).to_broadcast([128, nb, 16]), ALU.mult),
            r=[qkey, sclkey], w=[qkey])

    def s0_weight(w_in_ap, wT_h, wT_l, nrows, ncols):
        nblk = ncols // 128
        for rc in range(nrows // 128):
            yield
            b.op("sync", lambda h, rc=rc: h.dma_start(
                out=wQ[:, 0:ncols], in_=w_in_ap[rc * 128:(rc + 1) * 128, :]),
                w=["wQ"], chan=c_w)
            quant_tm(wQ[:, 0:ncols], "wQ", wS[:, 0:ncols], "wS",
                     wq1[:, 0:ncols], "wq1", wsc, "wsc", ncols)
            b.op("vector", lambda h: h.tensor_copy(wSb[:, 0:ncols], wq1[:, 0:ncols]),
                 r=["wq1"], w=["wS", "whl"])
            b.op("vector", lambda h: h.tensor_tensor(
                wSb[:, ncols:2 * ncols], wq1[:, 0:ncols], wSb[:, 0:ncols], ALU.subtract),
                r=["wq1", "whl"], w=["whl2"])
            yield
            for hl in range(2):
                hk = "whl" if hl == 0 else "whl2"
                dst_d = wT_h if hl == 0 else wT_l
                for tb in range(nblk):
                    bank, pk = pb[tb % 2], PB[tb % 2]
                    bkb = bank.bitcast(BF16)
                    b.op("tensor", lambda h, tb=tb, hl=hl, bkb=bkb: h.transpose(
                        bkb[:, 0:128], wSb[:, hl * ncols + tb * 128:hl * ncols + (tb + 1) * 128],
                        identb), r=[hk, "identb"], w=[pk])
                    if tb % 2 == 0:
                        b.op("vector", lambda h, tb=tb, bkb=bkb: h.tensor_copy(
                            wstb[:, tb * 128:(tb + 1) * 128], bkb[:, 0:128]),
                            r=[pk], w=["wst%d" % tb, "wq1"])
                    else:
                        b.op("scalar", lambda h, tb=tb, bkb=bkb: h.copy(
                            wstb[:, tb * 128:(tb + 1) * 128], bkb[:, 0:128]),
                            r=[pk], w=["wst%d" % tb, "wq1"])
                for tb in range(nblk):
                    b.op("scalar", lambda h, tb=tb, rc=rc, dst_d=dst_d: h.dma_start(
                        out=dst_d[tb * 128:(tb + 1) * 128, rc * 128:(rc + 1) * 128],
                        in_=wstb[:, tb * 128:(tb + 1) * 128]),
                        r=["wst%d" % tb, "wq1"], chan=c_st)

    def s0_plain(w_in_ap, wT_h, wT_l, nrows, ncols):
        for rc in range(nrows // 128):
            yield
            b.op("sync", lambda h, rc=rc: h.dma_start(
                out=wQ[:, 0:ncols], in_=w_in_ap[rc * 128:(rc + 1) * 128, :]),
                w=["wQ"], chan=c_w)
            b.op("vector", lambda h: h.tensor_copy(wSb[:, 0:ncols], wQ[:, 0:ncols]),
                 r=["wQ"], w=["wS", "whl"])
            b.op("vector", lambda h: h.tensor_tensor(
                wstb[:, 0:ncols], wQ[:, 0:ncols], wSb[:, 0:ncols], ALU.subtract),
                r=["wQ", "whl"], w=["wq1", "whl2"])
            b.op("scalar", lambda h, rc=rc: h.dma_start(
                out=wT_h[rc * 128:(rc + 1) * 128, :], in_=wSb[:, 0:ncols]),
                r=["whl"], chan=c_st)
            b.op("scalar", lambda h, rc=rc: h.dma_start(
                out=wT_l[rc * 128:(rc + 1) * 128, :], in_=wstb[:, 0:ncols]),
                r=["whl2"], chan=c_st)

    def s0_steps():
        yield from s0_plain(W_inT, WinT_h, WinT_l, D, 3088)
        yield from s0_weight(wg_in, wgqT_h, wgqT_l, DI, D)
        yield from s0_weight(wu_in, wuqT_h, wuqT_l, DI, D)
        yield from s0_weight(wd_in, wdqT_h, wdqT_l, D, DI)
        yield from s0_plain(W_outT, WoT_h, WoT_l, D, D)
    s0_gen = s0_steps()

    a_x = arena[:, 0:4096]
    a_sq = arena[:, 4096:4608]
    a_cv = arena[:, 4608:4608 + 16 * 515]
    a_cvC = arena[:, 12848:12848 + 8 * 515]
    a_cd = [arena[:, 16968:17480], arena[:, 17480:17992]]
    a_wb = [arena[:, 17992:19016], arena[:, 19016:20040]]
    a_xn = arena[:, 20040:24136]
    a_bc = arena[:, 24136:25160]
    a_ct = arena[:, 25160:26696]
    a_xnhl = arena[:, 26696:30792]
    a_xnb = a_xnhl.bitcast(BF16)
    a_wbb = [a_wb[0].bitcast(BF16), a_wb[1].bitcast(BF16)]

    c_x = [b.chan("xs0"), b.chan("xs1")]
    c_wp = [b.chan("wp0"), b.chan("wp1")]
    c_m = b.chan("mrowc")
    c_sp = [b.chan("sp0"), b.chan("sp1")]
    c_rw = b.chan("rowsw")

    def stats_and_dt(key):
        b.mmg(PB[6], [
            (lambda h, fc=fc: h.matmul(pb[6][0:1, 0:512], onesc,
                                       a_x[:, fc * 512:(fc + 1) * 512],
                                       start=(fc == 0), stop=(fc == 7)),
             [key + "x", "onesc"]) for fc in range(8)])
        b.hazards("tensor", w=[PB[7]])
        for fc in range(8):
            b.op("scalar", lambda h, fc=fc: h.activation(
                a_sq[:, 0:512], a_x[:, fc * 512:(fc + 1) * 512], AF.Square),
                r=[key + "x"], w=["sqs"])
            b.op("tensor", lambda h, fc=fc: h.matmul(
                pb[7][0:1, 0:512], onesc, a_sq[:, 0:512],
                start=(fc == 0), stop=(fc == 7)),
                r=["sqs", "onesc"], w=([PB[7]] if fc == 7 else ()))
        b.op("vector", lambda h: h.tensor_single_scalar(
            sslot(0), pb[6][0:1, 0:512], 1.0 / D, ALU.mult), r=[PB[6]], w=["S0k"])
        b.op("scalar", lambda h: h.activation(sslot(1), sslot(0), AF.Square),
             r=["S0k"], w=["S1k"])
        b.op("vector", lambda h: h.scalar_tensor_tensor(
            sslot(1), pb[7][0:1, 0:512], 1.0 / D, sslot(1), ALU.mult, ALU.subtract),
            r=[PB[7], "S1k"], w=["S1k"])
        b.op("scalar", lambda h: h.activation(sslot(1), sslot(1), AF.Sqrt, bias=cst[0:1, 0:1]),
             r=["S1k"], w=["S1k"])
        b.op("vector", lambda h: h.reciprocal(sslot(2), sslot(1)), r=["S1k"], w=["S2k"])
        b.op("vector", lambda h: h.tensor_tensor(sslot(3), sslot(0), sslot(2), ALU.mult),
             r=["S0k", "S2k"], w=["S3k"])
        b.mmg(PB[5], [
            (lambda h, kc=kc: h.matmul(pb[5][0:16, 0:512], wdt_s[:, kc * 16:(kc + 1) * 16],
                                       a_x[:, kc * 512:(kc + 1) * 512],
                                       start=(kc == 0), stop=(kc == 7)),
             ["wdt_s", key + "x"]) for kc in range(8)])
        b.op("tensor", lambda h: h.matmul(pb[4][0:16, 0:512], ones1[0:1, 0:16], sslot(2),
                                          start=True, stop=True),
             r=["S2k", "ones1"], w=[PB[4]])
        b.op("scalar", lambda h: h.copy(rslot(R_t1), pb[5][0:16, 0:512]),
             r=[PB[5]], w=["Rt1"])
        b.op("vector", lambda h: h.tensor_tensor(
            rslot(R_t0), rslot(R_t1), pb[4][0:16, 0:512], ALU.mult),
            r=["Rt1", PB[4]], w=["Rt0"])
        b.op("tensor", lambda h: h.matmul(pb[4][0:16, 0:512], ones1[0:1, 0:16], sslot(3),
                                          start=True, stop=True),
             r=["S3k", "ones1"], w=[PB[4]])
        b.op("vector", lambda h: h.scalar_tensor_tensor(
            rslot(R_dt), pb[4][0:16, 0:512], swr[:, 0:1], rslot(R_t0), ALU.mult, ALU.subtract),
            r=[PB[4], "swr", "Rt0"], w=["Rdt"])
        b.op("scalar", lambda h: h.activation(rslot(R_t1), rslot(R_dt), AF.Exp,
                                              bias=dtb[:, 0:1], scale=-1.0),
             r=["Rdt", "dtb"], w=["Rt1"])
        b.op("scalar", lambda h: h.activation(rslot(R_dt), rslot(R_t1), AF.Ln, bias=cst[0:16, 1:2]),
             r=["Rt1"], w=["Rdt"])
        b.op("vector", lambda h: h.tensor_scalar_mul(rslot(R_A), rslot(R_dt), apar[:, 0:1]),
             r=["Rdt", "apar"], w=["RA"])
        b.op("vector", lambda h: h.tensor_tensor(rslot(R_A), rslot(R_A), mrow, ALU.mult),
             r=["RA", "mrowk"], w=["RA"])

    if upto >= 1:
        b.op("vector", lambda h: h.memset(carry[:, 2:3], 0.0), w=["c_D"])
        b.op("vector", lambda h: h.memset(carry[:, 3:4], 1.0), w=["c_u"])
        for t in range(NT):
            b.op("sync", lambda h, t=t: h.dma_start(
                out=a_x.rearrange("p (c f) -> p c f", f=512),
                in_=xbT[:, t * TILE:(t + 1) * TILE].rearrange("(c p) f -> p c f", p=128)),
                w=["s1x"], chan=c_x[t % 2])
            b.op("sync", lambda h, t=t: h.dma_start(
                out=mrow, in_=m16[:, t * TILE:(t + 1) * TILE]), w=["mrowk"], chan=c_m)
            stats_and_dt("s1")
            b.op("scalar", lambda h: h.activation(rslot(R_em), rslot(R_A), AF.Exp),
                 r=["RA"], w=["Rem"])
            b.op("vector", lambda h: h.memset(rslot(R_t1), 1.0), w=["Rt1"])
            b.op("vector", lambda h: h.tensor_scalar_mul(rslot(R_t0), rslot(R_t1), carry[:, 3:4]),
                 r=["Rt1", "c_u"], w=["Rt0"])
            b.op("vector", lambda h: h.tensor_tensor(rslot(R_t0), rslot(R_t0), mrow, ALU.mult),
                 r=["Rt0", "mrowk"], w=["Rt0"])
            b.op("vector", lambda h: h.tensor_tensor_scan(
                rslot(R_S), rslot(R_em), rslot(R_t0), carry[:, 2:3], ALU.mult, ALU.add),
                r=["Rem", "Rt0", "c_D"], w=["RS"])
            b.op("vector", lambda h: h.tensor_single_scalar(
                rslot(R_t1), rslot(R_S), 1e-12, ALU.max), r=["RS"], w=["Rt1"])
            b.op("vector", lambda h: h.reciprocal(rslot(R_dt), rslot(R_t1)),
                 r=["Rt1"], w=["Rdt"])
            b.op("vector", lambda h: h.tensor_scalar_mul(rslot(R_dt), rslot(R_dt), carry[:, 3:4]),
                 r=["Rdt", "c_u"], w=["Rdt"])
            b.op("scalar", lambda h, t=t: h.dma_start(
                out=rows_nv_d[:, t * TILE:(t + 1) * TILE], in_=rslot(R_dt)),
                r=["Rdt"], chan=c_rw)
            b.op("vector", lambda h: h.tensor_copy(carry[:, 4:5], rslot(R_S)[:, TILE - 1:TILE]),
                 r=["RS"], w=["c_t"])
            b.op("vector", lambda h: h.tensor_single_scalar(
                carry[:, 5:6], carry[:, 4:5], 1e-12, ALU.max), r=["c_t"], w=["c_r"])
            b.op("vector", lambda h: h.reciprocal(carry[:, 5:6], carry[:, 5:6]),
                 r=["c_r"], w=["c_r"])
            b.op("vector", lambda h: h.reduce_max(carry[:, 6:7], mrow, AX.X),
                 r=["mrowk"], w=["c_m1"])
            b.op("vector", lambda h: h.tensor_tensor(carry[:, 5:6], carry[:, 5:6],
                                                     carry[:, 6:7], ALU.mult),
                 r=["c_r", "c_m1"], w=["c_r"])
            b.op("vector", lambda h: h.tensor_scalar(carry[:, 6:7], carry[:, 6:7],
                                                     -1.0, 1.0, ALU.mult, ALU.add),
                 r=["c_m1"], w=["c_m1"])
            b.op("vector", lambda h: h.tensor_tensor(carry[:, 5:6], carry[:, 5:6],
                                                     carry[:, 6:7], ALU.add),
                 r=["c_r", "c_m1"], w=["c_r"])
            b.op("vector", lambda h: h.tensor_tensor(carry[:, 2:3], carry[:, 4:5],
                                                     carry[:, 5:6], ALU.mult),
                 r=["c_t", "c_r"], w=["c_D"])
            b.op("vector", lambda h: h.tensor_tensor(carry[:, 3:4], carry[:, 3:4],
                                                     carry[:, 5:6], ALU.mult),
                 r=["c_u", "c_r"], w=["c_u"])
        b.barrier()

    def proj_conv(spill_col, ofs, winf, wkeyf, cwcf, spill):
        for j, o in enumerate(ofs):
            wbb, wk = a_wbb[j % 2], "wb%d" % (j % 2)
            b.dma_group("sync", c_wp[j % 2], [
                (lambda h, o=o, wbb=wbb: h.dma_start(
                    out=wbb[:, 0:1024].rearrange("p (c f) -> p c f", f=128),
                    in_=WinT_h[:, o * 128:(o + 1) * 128].rearrange("(c p) f -> p c f", p=128)),
                 [], [wk]),
                (lambda h, o=o, wbb=wbb: h.dma_start(
                    out=wbb[:, 1024:2048].rearrange("p (c f) -> p c f", f=128),
                    in_=WinT_l[:, o * 128:(o + 1) * 128].rearrange("(c p) f -> p c f", p=128)),
                 [], [wk])])
            bank, pk = pb[j % 2], PB[j % 2]
            items = []
            for wo, xo in ((0, 0), (0, 4096), (1024, 0)):
                for kc in range(8):
                    first = (wo == 0 and xo == 0 and kc == 0)
                    last = (wo == 1024 and kc == 7)
                    xk = ("xnh%d" % kc) if xo == 0 else ("xnl%d" % kc)
                    items.append((lambda h, kc=kc, wo=wo, xo=xo, wbb=wbb, bank=bank,
                                  first=first, last=last: h.matmul(
                        bank, wbb[:, wo + kc * 128:wo + (kc + 1) * 128],
                        a_xnb[:, xo + kc * 512:xo + (kc + 1) * 512],
                        start=first, stop=last), [wk, xk]))
            b.mmg(pk, items)
            win, wkey = winf(j), wkeyf(j)
            if j % 2 == 0:
                b.op("vector", lambda h, win=win, bank=bank: h.tensor_copy(win[:, 3:515], bank),
                     r=[pk], w=[wkey])
            else:
                b.op("scalar", lambda h, win=win, bank=bank: h.copy(win[:, 3:515], bank),
                     r=[pk], w=[wkey])
            cd, cdk = a_cd[j % 2], "cd%d" % (j % 2)
            cc = cwcf(j)
            b.op("scalar", lambda h, win=win, cd=cd, cc=cc: h.activation(
                cd, win[:, 0:512], AF.Copy, scale=cwb[:, 5 * cc:5 * cc + 1]),
                r=[wkey, "cwb"], w=[cdk])
            for tap in range(1, 4):
                b.op("scalar", lambda h, win=win, cc=cc, tap=tap: h.activation(
                    a_ct[:, (tap - 1) * 512:tap * 512], win[:, tap:tap + 512], AF.Copy,
                    scale=cwb[:, 5 * cc + tap:5 * cc + tap + 1]),
                    r=[wkey, "cwb"], w=["ctmp%d" % tap])
            b.op("vector", lambda h, cd=cd, cc=cc: h.scalar_tensor_tensor(
                a_ct[:, 0:512], a_ct[:, 0:512], cwb[:, 5 * cc + 4:5 * cc + 5],
                a_ct[:, 512:1024], ALU.add, ALU.add),
                r=["ctmp1", "ctmp2", "cwb"], w=["ctmp1"])
            b.op("vector", lambda h, cd=cd: h.tensor_tensor(
                cd, cd, a_ct[:, 1024:1536], ALU.add), r=[cdk, "ctmp3"], w=[cdk])
            b.op("vector", lambda h, cd=cd: h.tensor_tensor(
                cd, cd, a_ct[:, 0:512], ALU.add), r=[cdk, "ctmp1"], w=[cdk])
            b.op("gpsimd", lambda h, win=win: h.tensor_copy(win[:, 0:3], win[:, 512:515]),
                 r=[wkey], w=[wkey])
            b.op("scalar", lambda h, cd=cd, j=j, spill=spill, spill_col=spill_col: h.dma_start(
                out=spill[j * 128:(j + 1) * 128, spill_col * TILE:(spill_col + 1) * TILE],
                in_=cd), r=[cdk], chan=c_sp[j % 2])

    if upto >= 2:
        for _ in range(9):
            next(s0_gen, None)
        b.barrier()
        b.op("vector", lambda h: h.memset(carry[:, 0:1], 0.0), w=["c_cum"])
        for fc in range(16):
            b.op("gpsimd", lambda h, fc=fc: h.memset(a_cv[:, fc * 515:fc * 515 + 3], 0.0),
                 w=["cw%d" % fc])
        for fc in range(8):
            b.op("gpsimd", lambda h, fc=fc: h.memset(a_cvC[:, fc * 515:fc * 515 + 3], 0.0),
                 w=["cwC%d" % fc])
        for t in range(NT):
            b.op("sync", lambda h, t=t: h.dma_start(
                out=a_x.rearrange("p (c f) -> p c f", f=512),
                in_=x0T[:, t * TILE:(t + 1) * TILE].rearrange("(c p) f -> p c f", p=128)),
                w=["s2x"], chan=c_x[t % 2])
            b.op("sync", lambda h, t=t: h.dma_start(
                out=mrow, in_=m16[:, t * TILE:(t + 1) * TILE]), w=["mrowk"], chan=c_m)
            stats_and_dt("s2")
            b.op("tensor", lambda h: h.matmul(pb[2], ones1, sslot(0), start=True, stop=True),
                 r=["S0k", "ones1"], w=[PB[2]])
            b.op("tensor", lambda h: h.matmul(pb[3], ones1, sslot(2), start=True, stop=True),
                 r=["S2k", "ones1"], w=[PB[3]])
            b.op("scalar", lambda h: h.copy(a_bc[:, 0:512], pb[2]), r=[PB[2]], w=["bcmu"])
            b.op("scalar", lambda h: h.copy(a_bc[:, 512:1024], pb[3]), r=[PB[3]], w=["bcis"])
            for fc in range(8):
                ee = "vector" if fc % 2 == 0 else "gpsimd"
                b.op(ee, lambda h, fc=fc: h.tensor_tensor(
                    a_xn[:, fc * 512:(fc + 1) * 512], a_x[:, fc * 512:(fc + 1) * 512],
                    a_bc[:, 0:512], ALU.subtract), r=["s2x", "bcmu"], w=["xn%d" % fc])
                b.op(ee, lambda h, fc=fc: h.tensor_tensor(
                    a_xn[:, fc * 512:(fc + 1) * 512], a_xn[:, fc * 512:(fc + 1) * 512],
                    a_bc[:, 512:1024], ALU.mult), r=["xn%d" % fc, "bcis"], w=["xn%d" % fc])
                b.op(ee, lambda h, fc=fc: h.tensor_copy(
                    a_xnb[:, fc * 512:(fc + 1) * 512], a_xn[:, fc * 512:(fc + 1) * 512]),
                    r=["xn%d" % fc], w=["xnh%d" % fc])
                b.op(ee, lambda h, fc=fc: h.tensor_tensor(
                    a_xnb[:, 4096 + fc * 512:4096 + (fc + 1) * 512],
                    a_xn[:, fc * 512:(fc + 1) * 512],
                    a_xnb[:, fc * 512:(fc + 1) * 512], ALU.subtract),
                    r=["xn%d" % fc, "xnh%d" % fc], w=["xnl%d" % fc])
            proj_conv(t, list(range(8, 16)),
                      lambda j: a_cv[:, j * 515:(j + 1) * 515], lambda j: "cw%d" % j,
                      lambda j: 8 + j, cvB)
            for _ in range(3):
                next(s0_gen, None)
            proj_conv(t, list(range(16, 24)),
                      lambda j: a_cv[:, (8 + j) * 515:(9 + j) * 515],
                      lambda j: "cw%d" % (8 + j), lambda j: 16 + j, cvX)
            if t == 7:
                for j, o in enumerate(range(0, 8)):
                    wbb, wk = a_wbb[j % 2], "wb%d" % (j % 2)
                    b.dma_group("sync", c_wp[j % 2], [
                        (lambda h, o=o, wbb=wbb: h.dma_start(
                            out=wbb[:, 0:1024].rearrange("p (c f) -> p c f", f=128),
                            in_=WinT_h[:, o * 128:(o + 1) * 128].rearrange(
                                "(c p) f -> p c f", p=128)),
                         [], [wk]),
                        (lambda h, o=o, wbb=wbb: h.dma_start(
                            out=wbb[:, 1024:2048].rearrange("p (c f) -> p c f", f=128),
                            in_=WinT_l[:, o * 128:(o + 1) * 128].rearrange(
                                "(c p) f -> p c f", p=128)),
                         [], [wk])])
                    bank, pk = pb[j % 2], PB[j % 2]
                    items = []
                    for wo, xo in ((0, 0), (0, 4096), (1024, 0)):
                        for kc in range(8):
                            first = (wo == 0 and xo == 0 and kc == 0)
                            last = (wo == 1024 and kc == 7)
                            xk = ("xnh%d" % kc) if xo == 0 else ("xnl%d" % kc)
                            items.append((lambda h, kc=kc, wo=wo, xo=xo, wbb=wbb,
                                          bank=bank, first=first, last=last: h.matmul(
                                bank[:, 0:3], wbb[:, wo + kc * 128:wo + (kc + 1) * 128],
                                a_xnb[:, xo + kc * 512 + 509:xo + kc * 512 + 512],
                                start=first, stop=last), [wk, xk]))
                    b.mmg(pk, items)
                    b.op("vector", lambda h, j=j, bank=bank: h.tensor_copy(
                        a_cvC[:, j * 515:j * 515 + 3], bank[:, 0:3]),
                        r=[pk], w=["cwC%d" % j])
            for _ in range(3):
                next(s0_gen, None)
            if t >= 8:
                proj_conv(t - 8, list(range(0, 8)),
                          lambda j: a_cvC[:, j * 515:(j + 1) * 515],
                          lambda j: "cwC%d" % j, lambda j: j, cvC)
            b.op("vector", lambda h: h.tensor_tensor_scan(
                rslot(R_cum), rslot(R_A), rslot(R_A), carry[:, 0:1], ALU.add, ALU.bypass),
                r=["RA", "c_cum"], w=["Rcum"])
            b.op("vector", lambda h: h.tensor_copy(carry[:, 0:1], rslot(R_cum)[:, TILE - 1:TILE]),
                 r=["Rcum"], w=["c_cum"])
            b.op("scalar", lambda h: h.activation(rslot(R_em), rslot(R_cum), AF.Exp, scale=-1.0),
                 r=["Rcum"], w=["Rem"])
            b.op("vector", lambda h: h.tensor_tensor(rslot(R_em), rslot(R_em), mrow, ALU.mult),
                 r=["Rem", "mrowk"], w=["Rem"])
            b.op("scalar", lambda h: h.activation(rslot(R_ep), rslot(R_cum), AF.Exp),
                 r=["Rcum"], w=["Rep"])
            b.op("scalar", lambda h, t=t: h.dma_start(
                out=rows_em_d[:, t * TILE:(t + 1) * TILE], in_=rslot(R_em)),
                r=["Rem"], chan=c_rw)
            b.op("scalar", lambda h, t=t: h.dma_start(
                out=rows_ep_d[:, t * TILE:(t + 1) * TILE], in_=rslot(R_ep)),
                r=["Rep"], chan=c_rw)
            for _ in range(2):
                next(s0_gen, None)
        for _ in s0_gen:
            pass
        b.barrier()

    if upto < 2:
        for _ in s0_gen:
            pass
        b.barrier()
    if upto >= 3:
        CH2 = 128
        NCK2 = TILE // CH2
        s_B64 = arena[0:64, 0:8192]
        s_X64 = arena[0:64, 8192:16384]
        s_C64 = arena[0:64, 16384:24576]
        s_ms = arena[:, 24576:26624]
        s_xt = arena[:, 26624:27648]
        s_Bt = arena[:, 27648:28672]
        s_ACC = arena[0:64, 28672:29696]
        s_Y = arena[:, 29696:30720]
        s_emt = arena[:, 30720:30784]
        s_ft = arena[:, 30784:30848]
        s_Yne = arena[0:64, 30848:34944]
        s_Yno = arena[0:64, 34944:39040]

        c_b64 = [b.chan("b64_0"), b.chan("b64_1")]
        c_x64 = [b.chan("x64_0"), b.chan("x64_1")]
        c_c64 = [b.chan("c64_0"), b.chan("c64_1")]
        c_rem = b.chan("rowem")
        c_rep = b.chan("rowep")
        c_rnv = b.chan("rownv")
        c_yn = [b.chan("yn0"), b.chan("yn1")]

        b.op("vector", lambda h: h.memset(s_ACC[:, 0:512], 0.0), w=["ACC0"])
        b.op("vector", lambda h: h.memset(s_ACC[:, 512:1024], 0.0), w=["ACC1"])

        YnT4 = YnT.rearrange("(g two p) f -> two p g f", two=2, p=64)

        for t in range(NT):
            isY = t >= 8
            b.op("sync", lambda h, t=t: h.dma_start(
                out=s_B64.rearrange("p (g f) -> p g f", f=512),
                in_=cvB[:, t * TILE:(t + 1) * TILE].rearrange("(g p) f -> p g f", p=64)),
                w=["B64"], chan=c_b64[t % 2])
            b.op("sync", lambda h, t=t: h.dma_start(
                out=s_X64.rearrange("p (g f) -> p g f", f=512),
                in_=cvX[:, t * TILE:(t + 1) * TILE].rearrange("(g p) f -> p g f", p=64)),
                w=["X64"], chan=c_x64[t % 2])
            if isY:
                b.op("sync", lambda h, t=t: h.dma_start(
                    out=s_C64.rearrange("p (g f) -> p g f", f=512),
                    in_=cvC[:, (t - 8) * TILE:(t - 7) * TILE].rearrange("(g p) f -> p g f", p=64)),
                    w=["C64"], chan=c_c64[t % 2])
            b.op("sync", lambda h, t=t: h.dma_start(
                out=rslot(R_em), in_=rows_em_d[:, t * TILE:(t + 1) * TILE]),
                w=["Rem"], chan=c_rem)
            if isY:
                b.op("sync", lambda h, t=t: h.dma_start(
                    out=rslot(R_ep), in_=rows_ep_d[:, t * TILE:(t + 1) * TILE]),
                    w=["Rep"], chan=c_rep)
                b.op("sync", lambda h, t=t: h.dma_start(
                    out=rslot(R_t0), in_=rows_nv_d[:, t * TILE:(t + 1) * TILE]),
                    w=["Rt0"], chan=c_rnv)
                b.op("vector", lambda h: h.tensor_tensor(
                    rslot(R_ep), rslot(R_ep), rslot(R_t0), ALU.mult),
                    r=["Rep", "Rt0"], w=["Rep"])
            for blk in range(4):
                bank, pk = pb[6 + blk % 2], PB[6 + blk % 2]
                b.op("tensor", lambda h, blk=blk, bank=bank: h.transpose(
                    bank[:, 0:16], rslot(R_em)[:, blk * 128:(blk + 1) * 128],
                    ident[0:16, 0:16]), r=["Rem", "ident"], w=[pk])
                b.op("vector", lambda h, blk=blk, bank=bank: h.tensor_copy(
                    s_emt[:, blk * 16:(blk + 1) * 16], bank[:, 0:16]),
                    r=[pk], w=["emt"])
                if isY:
                    b.op("tensor", lambda h, blk=blk, bank=bank: h.transpose(
                        bank[:, 0:16], rslot(R_ep)[:, blk * 128:(blk + 1) * 128],
                        ident[0:16, 0:16]), r=["Rep", "ident"], w=[pk])
                    b.op("vector", lambda h, blk=blk, bank=bank: h.tensor_copy(
                        s_ft[:, blk * 16:(blk + 1) * 16], bank[:, 0:16]),
                        r=[pk], w=["ft"])
            for ci in range(NCK2):
                for hh in range(NH):
                    bank, pk = pb[6 + hh % 2], PB[6 + hh % 2]
                    xsl = s_X64[:, hh * 512 + ci * 128:hh * 512 + (ci + 1) * 128]
                    b.op("tensor", lambda h, xsl=xsl, bank=bank: h.transpose(
                        bank[:, 0:64], xsl, ident[0:64, 0:64]),
                        r=["X64", "ident"], w=[pk])
                    if hh % 2 == 0:
                        b.op("vector", lambda h, hh=hh, bank=bank, ci=ci: h.tensor_scalar_mul(
                            s_xt[:, hh * 64:(hh + 1) * 64], bank[:, 0:64],
                            s_emt[:, ci * 16 + hh:ci * 16 + hh + 1]),
                            r=[pk, "emt"], w=["xt%d" % hh])
                    else:
                        b.op("scalar", lambda h, hh=hh, bank=bank, ci=ci: h.activation(
                            s_xt[:, hh * 64:(hh + 1) * 64], bank[:, 0:64], AF.Copy,
                            scale=s_emt[:, ci * 16 + hh:ci * 16 + hh + 1]),
                            r=[pk, "emt"], w=["xt%d" % hh])
                    bsl = s_B64[:, hh * 512 + ci * 128:hh * 512 + (ci + 1) * 128]
                    b.op("tensor", lambda h, bsl=bsl, bank=bank: h.transpose(
                        bank[:, 0:64], bsl, ident[0:64, 0:64]),
                        r=["B64", "ident"], w=[pk])
                    if hh % 2 == 0:
                        b.op("vector", lambda h, hh=hh, bank=bank: h.tensor_copy(
                            s_Bt[:, hh * 64:(hh + 1) * 64], bank[:, 0:64]),
                            r=[pk], w=["Bt%d" % hh])
                    else:
                        b.op("scalar", lambda h, hh=hh, bank=bank: h.copy(
                            s_Bt[:, hh * 64:(hh + 1) * 64], bank[:, 0:64]),
                            r=[pk], w=["Bt%d" % hh])
                if isY:
                    for bk in range(4):
                        items = []
                        for j in range(4):
                            hh = bk * 4 + j
                            items.append((lambda h, hh=hh, j=j, bk=bk, ci=ci,
                                          first=(j == 0), lastf=(j == 3): h.matmul(
                                pb[bk][:, j * 128:(j + 1) * 128],
                                s_B64[:, hh * 512 + ci * 128:hh * 512 + (ci + 1) * 128],
                                s_C64[:, hh * 512 + ci * 128:hh * 512 + (ci + 1) * 128],
                                start=first, stop=lastf), ["B64", "C64"]))
                        b.mmg(PB[bk], items)
                    for bk in range(4):
                        b.op("vector", lambda h, bk=bk: h.tensor_tensor(
                            s_ms[:, bk * 512:(bk + 1) * 512].rearrange("p (g f) -> p g f", f=128),
                            pb[bk].rearrange("p (g f) -> p g f", f=128),
                            umask.unsqueeze(1).to_broadcast([128, 4, 128]), ALU.mult),
                            r=[PB[bk], "umask"], w=["ms%d" % bk])
                    for half in range(2):
                        items = []
                        for j in range(8):
                            hh = half * 8 + j
                            off = j * 64
                            items.append((lambda h, hh=hh, off=off, half=half,
                                          first=(j == 0): h.matmul(
                                pb[4 + half][:, off:off + 64],
                                s_ms[:, hh * 128:(hh + 1) * 128],
                                s_xt[:, hh * 64:(hh + 1) * 64],
                                start=first, stop=False),
                                ["ms%d" % (hh // 4), "xt%d" % hh]))
                            items.append((lambda h, hh=hh, off=off, ci=ci, half=half,
                                          lastf=(j == 7): h.matmul(
                                pb[4 + half][:, off:off + 64],
                                s_C64[:, hh * 512 + ci * 128:hh * 512 + (ci + 1) * 128],
                                s_ACC[:, hh * 64:(hh + 1) * 64],
                                start=False, stop=lastf),
                                ["C64", "ACC%d" % half]))
                        b.mmg(PB[4 + half], items)
                    for hh in range(NH):
                        half, off = hh // 8, (hh % 8) * 64
                        if half == 0:
                            b.op("vector", lambda h, hh=hh, half=half, off=off, ci=ci: h.tensor_scalar_mul(
                                s_Y[:, hh * 64:(hh + 1) * 64], pb[4 + half][:, off:off + 64],
                                s_ft[:, ci * 16 + hh:ci * 16 + hh + 1]),
                                r=[PB[4 + half], "ft"], w=["sy%d" % hh])
                        else:
                            b.op("scalar", lambda h, hh=hh, half=half, off=off, ci=ci: h.activation(
                                s_Y[:, hh * 64:(hh + 1) * 64], pb[4 + half][:, off:off + 64],
                                AF.Copy, scale=s_ft[:, ci * 16 + hh:ci * 16 + hh + 1]),
                                r=[PB[4 + half], "ft"], w=["sy%d" % hh])
                    for hh in range(NH):
                        bank, pk = pb[6 + hh % 2], PB[6 + hh % 2]
                        b.op("tensor", lambda h, hh=hh, bank=bank: h.transpose(
                            bank[0:64, 0:128], s_Y[:, hh * 64:(hh + 1) * 64],
                            ident), r=["sy%d" % hh, "ident"], w=[pk])
                        dst_t = s_Yne if hh % 2 == 0 else s_Yno
                        fcq = hh // 2
                        dst = dst_t[:, fcq * 512 + ci * 128:fcq * 512 + (ci + 1) * 128]
                        if hh % 2 == 0:
                            b.op("vector", lambda h, dst=dst, bank=bank: h.tensor_copy(
                                dst, bank[0:64, 0:128]), r=[pk], w=["yn%d" % hh])
                        else:
                            b.op("scalar", lambda h, dst=dst, bank=bank: h.copy(
                                dst, bank[0:64, 0:128]), r=[pk], w=["yn%d" % hh])
                for half in range(2):
                    items = []
                    for j in range(8):
                        hh = half * 8 + j
                        off = j * 64
                        items.append((lambda h, hh=hh, off=off, half=half,
                                      first=(j == 0), lastf=(j == 7): h.matmul(
                            pb[2 + half][0:64, off:off + 64],
                            s_Bt[:, hh * 64:(hh + 1) * 64],
                            s_xt[:, hh * 64:(hh + 1) * 64],
                            start=first, stop=lastf),
                            ["Bt%d" % hh, "xt%d" % hh]))
                    b.mmg(PB[2 + half], items)
                for half in range(2):
                    b.op("vector", lambda h, half=half: h.tensor_tensor(
                        s_ACC[:, half * 512:(half + 1) * 512],
                        s_ACC[:, half * 512:(half + 1) * 512],
                        pb[2 + half][0:64, 0:512], ALU.add),
                        r=[PB[2 + half], "ACC%d" % half], w=["ACC%d" % half])
            if isY:
                b.op("scalar", lambda h, t=t: h.dma_start(
                    out=YnT4[0][:, :, (t - 8) * TILE:(t - 7) * TILE],
                    in_=s_Yne.rearrange("p (g f) -> p g f", f=512)),
                    r=["yn%d" % i2 for i2 in range(0, 16, 2)], chan=c_yn[0])
                b.op("scalar", lambda h, t=t: h.dma_start(
                    out=YnT4[1][:, :, (t - 8) * TILE:(t - 7) * TILE],
                    in_=s_Yno.rearrange("p (g f) -> p g f", f=512)),
                    r=["yn%d" % i2 for i2 in range(1, 16, 2)], chan=c_yn[1])
        b.barrier()

    if upto >= 4:
        f_xf = arena[:, 0:4096]
        f_q = arena[:, 4096:8192]
        f_sgn = arena[:, 8192:12288]
        f_scl = arena[:, 12288:12800]
        f_xqf2 = [arena[:, 12800:16896], arena[:, 16896:20992]]
        f_h = arena[:, 20992:32256]
        f_wq = arena[:, 32256:36352]
        f_ws = [arena[:, 36352:37376], arena[:, 37376:38400]]
        f_res = arena[:, 38400:42496]
        f_hscl = f_res[:, 0:352]
        f_sil = f_res[:, 352:2400]
        f_xfb = f_xf.bitcast(BF16)
        f_qb = f_q.bitcast(BF16)
        f_xqf2b = [f_xqf2[0].bitcast(BF16), f_xqf2[1].bitcast(BF16)]
        f_hb = f_h.bitcast(BF16)
        f_wqb = f_wq.bitcast(BF16)
        f_wsb = [f_ws[0].bitcast(BF16), f_ws[1].bitcast(BF16)]

        c_xf = [b.chan("xfc0"), b.chan("xfc1")]
        c_wq = [b.chan("wqc0"), b.chan("wqc1")]
        c_ynr = [b.chan("ynr0"), b.chan("ynr1")]
        c_ws = [b.chan("wsc0"), b.chan("wsc1")]
        c_out = b.chan("outc")
        c_res = b.chan("resc")

        SPANS = [(0, 512), (512, 512), (1024, 512), (1536, 512), (2048, 512), (2560, 256)]

        def emit_A(t):
            b.op("sync", lambda h, t=t: h.dma_start(
                out=f_xf.rearrange("p (s d) -> p s d", d=1024),
                in_=xf_in[t * TILE:(t + 1) * TILE, :].rearrange("(s p) d -> p s d", p=128)),
                w=["fx", "fhqh", "fhql", "fhtmp"] + ["fxq%d" % s2 for s2 in range(4)]
                  + ["fxqh%d" % s2 for s2 in range(4)]
                  + ["fxql%d" % s2 for s2 in range(4)], chan=c_xf[t % 2])
            for st in range(4):
                xs = f_xf[:, st * 1024:(st + 1) * 1024]
                dum = f_q[:, st * 1024:st * 1024 + 512]
                b.op("scalar", lambda h, xs=xs, dum=dum, st=st: h.activation(
                    dum, xs[:, 0:512], AF.Square, accum_out=fstat[:, st:st + 1]),
                    r=["fx"], w=["fs2a%d" % st, "fxn%d" % st, "fhs"])
                b.op("scalar", lambda h, xs=xs, dum=dum, st=st: h.activation(
                    dum, xs[:, 512:1024], AF.Square, accum_out=fstat[:, 4 + st:5 + st]),
                    r=["fx", "fs2a%d" % st], w=["fs2b%d" % st, "fxn%d" % st])
                b.op("vector", lambda h, xs=xs, st=st: h.reduce_sum(
                    fstat[:, 8 + st:9 + st], xs, AX.X), r=["fx"], w=["fs1%d" % st])
                b.op("vector", lambda h, st=st: h.tensor_single_scalar(
                    fstat[:, 8 + st:9 + st], fstat[:, 8 + st:9 + st], -1.0 / D, ALU.mult),
                    r=["fs1%d" % st], w=["fs1%d" % st])
                b.op("vector", lambda h, st=st: h.tensor_tensor(
                    fstat[:, 12 + st:13 + st], fstat[:, st:st + 1],
                    fstat[:, 4 + st:5 + st], ALU.add),
                    r=["fs2a%d" % st, "fs2b%d" % st], w=["fv%d" % st])
                b.op("vector", lambda h, st=st: h.tensor_single_scalar(
                    fstat[:, 12 + st:13 + st], fstat[:, 12 + st:13 + st], 1.0 / D, ALU.mult),
                    r=["fv%d" % st], w=["fv%d" % st])
                b.op("vector", lambda h, st=st: h.scalar_tensor_tensor(
                    fstat[:, 12 + st:13 + st], fstat[:, 8 + st:9 + st],
                    fstat[:, 8 + st:9 + st], fstat[:, 12 + st:13 + st],
                    ALU.mult, ALU.subtract),
                    r=["fs1%d" % st, "fv%d" % st], w=["fv%d" % st])
                b.op("scalar", lambda h, st=st: h.activation(
                    fstat[:, 12 + st:13 + st], fstat[:, 12 + st:13 + st], AF.Sqrt,
                    bias=cst[:, 0:1], scale=-1.0), r=["fv%d" % st], w=["fv%d" % st])
                b.op("vector", lambda h, st=st: h.reciprocal(
                    fstat[:, 12 + st:13 + st], fstat[:, 12 + st:13 + st]),
                    r=["fv%d" % st], w=["fistd%d" % st])
                b.op("vector", lambda h, xs=xs, st=st: h.tensor_scalar(
                    f_q[:, st * 1024:(st + 1) * 1024], xs,
                    fstat[:, 8 + st:9 + st], fstat[:, 12 + st:13 + st], ALU.add, ALU.mult),
                    r=["fx", "fs1%d" % st, "fistd%d" % st],
                    w=["fxn%d" % st, "fhs"])
                quant_tm(f_q[:, st * 1024:(st + 1) * 1024], "fxn%d" % st,
                         f_sgn[:, st * 1024:(st + 1) * 1024], "fsgn%d" % st,
                         f_xf[:, st * 1024:(st + 1) * 1024], "fxq%d" % st,
                         f_scl[:, st * 128:(st + 1) * 128], "fscl%d" % st, 1024,
                         eng="gpsimd" if st % 2 else "vector",
                         claim=["fx", "fhqh", "fhql"], sclaim=["fhqq", "fyn"])
                qe = "gpsimd" if st % 2 else "vector"
                b.op(qe, lambda h, st=st: h.tensor_copy(
                    f_qb[:, st * 2048:st * 2048 + 1024],
                    f_xf[:, st * 1024:(st + 1) * 1024]),
                    r=["fxq%d" % st, "fhs"], w=["fxqh%d" % st, "fxn%d" % st])
                b.op(qe, lambda h, st=st: h.tensor_tensor(
                    f_qb[:, st * 2048 + 1024:st * 2048 + 2048],
                    f_xf[:, st * 1024:(st + 1) * 1024],
                    f_qb[:, st * 2048:st * 2048 + 1024], ALU.subtract),
                    r=["fxq%d" % st, "fxqh%d" % st], w=["fxql%d" % st, "fxn%d" % st])

        def emit_XT(t):
            p = t % 2
            for st in range(4):
                for kc in range(8):
                    bank, pk = pb[6 + kc % 2], PB[6 + kc % 2]
                    bkb = bank.bitcast(BF16)
                    for hl in range(2):
                        xk = ("fxqh%d" % st) if hl == 0 else ("fxql%d" % st)
                        dk = ("xqf%d_%d_%d" % (kc, st, p)) if hl == 0 \
                            else ("xqfl%d_%d_%d" % (kc, st, p))
                        dof = hl * 4096
                        b.op("tensor", lambda h, st=st, kc=kc, hl=hl, bkb=bkb: h.transpose(
                            bkb[:, hl * 128:hl * 128 + 128],
                            f_qb[:, st * 2048 + hl * 1024 + kc * 128:
                                 st * 2048 + hl * 1024 + (kc + 1) * 128], identb),
                            r=[xk, "identb", "fhs"], w=[pk])
                        if kc % 2 == 0:
                            b.op("vector", lambda h, kc=kc, st=st, hl=hl, dof=dof,
                                 bkb=bkb, p=p: h.tensor_copy(
                                f_xqf2b[p][:, dof + kc * 512 + st * 128:
                                           dof + kc * 512 + (st + 1) * 128],
                                bkb[:, hl * 128:hl * 128 + 128]), r=[pk], w=[dk])
                        else:
                            b.op("scalar", lambda h, kc=kc, st=st, hl=hl, dof=dof,
                                 bkb=bkb, p=p: h.copy(
                                f_xqf2b[p][:, dof + kc * 512 + st * 128:
                                           dof + kc * 512 + (st + 1) * 128],
                                bkb[:, hl * 128:hl * 128 + 128]), r=[pk], w=[dk])

        def emit_Dpre(t):
            b.op("sync", lambda h, t=t: h.dma_start(
                out=f_sgn.rearrange("p (c f) -> p c f", f=512),
                in_=YnT[:, t * TILE:(t + 1) * TILE].rearrange("(c p) f -> p c f", p=128)),
                w=["fyn"] + ["fsgn%d" % s2 for s2 in range(4)], chan=c_ynr[t % 2])
            for di in range(8):
                ee = "vector" if di % 2 == 0 else "gpsimd"
                b.op(ee, lambda h, di=di: h.tensor_copy(
                    f_wqb[:, di * 512:(di + 1) * 512],
                    f_sgn[:, di * 512:(di + 1) * 512]),
                    r=["fyn"], w=["fynh%d" % di, "fwq"])
                b.op(ee, lambda h, di=di: h.tensor_tensor(
                    f_wqb[:, 4096 + di * 512:4096 + (di + 1) * 512],
                    f_sgn[:, di * 512:(di + 1) * 512],
                    f_wqb[:, di * 512:(di + 1) * 512], ALU.subtract),
                    r=["fyn", "fynh%d" % di], w=["fynl%d" % di, "fwq"])

        def emit_MID(t):
            p = t % 2
            for (soff, w) in SPANS:
                for wi, (wsh, wsl) in enumerate(((wgqT_h, wgqT_l), (wuqT_h, wuqT_l))):
                    b.dma_group("sync", c_wq[wi], [
                        (lambda h, wsh=wsh, soff=soff, w=w: h.dma_start(
                            out=f_wqb[:, 0:8 * w].rearrange("p (c f) -> p c f", f=w),
                            in_=wsh[:, soff:soff + w].rearrange("(c p) f -> p c f", p=128)),
                         [], ["fwq"]),
                        (lambda h, wsl=wsl, soff=soff, w=w: h.dma_start(
                            out=f_wqb[:, 4096:4096 + 8 * w].rearrange("p (c f) -> p c f", f=w),
                            in_=wsl[:, soff:soff + w].rearrange("(c p) f -> p c f", p=128)),
                         [], ["fwq"])])
                    for st in range(4):
                        items = []
                        for pi, (wof, xof) in enumerate(((0, 0), (0, 4096), (4096, 0))):
                            for kc in range(8):
                                first = (pi == 0 and kc == 0)
                                last = (pi == 2 and kc == 7)
                                xk = ("xqf%d_%d_%d" % (kc, st, p)) if xof == 0 \
                                    else ("xqfl%d_%d_%d" % (kc, st, p))
                                items.append((lambda h, st=st, kc=kc, w=w, p=p,
                                              wof=wof, xof=xof, first=first, last=last:
                                              h.matmul(
                                    pb[st][:, 0:w],
                                    f_xqf2b[p][:, xof + kc * 512 + st * 128:
                                               xof + kc * 512 + (st + 1) * 128],
                                    f_wqb[:, wof + kc * w:wof + (kc + 1) * w],
                                    start=first, stop=last), ["fwq", xk]))
                        b.mmg(PB[st], items)
                        if wi == 0:
                            b.op("scalar", lambda h, st=st, w=w: h.activation(
                                f_sil[:, st * 512:st * 512 + w], pb[st][:, 0:w], AF.Silu),
                                r=[PB[st]], w=["fsil%d" % st, "fres"])
                        else:
                            b.op("vector", lambda h, st=st, w=w, soff=soff: h.tensor_tensor(
                                f_h[:, st * 2816 + soff:st * 2816 + soff + w],
                                f_sil[:, st * 512:st * 512 + w], pb[st][:, 0:w], ALU.mult),
                                r=["fsil%d" % st, PB[st]], w=["fh%d" % st])

        def emit_C(t):
            for st in range(4):
                hwin = f_h[:, st * 2816:(st + 1) * 2816]
                quant_tm(hwin, "fh%d" % st,
                         f_q[:, 0:2816], "fhs",
                         f_sgn[:, 0:2816], "fhqq",
                         f_hscl, "fhscl", 2816,
                         eng="gpsimd" if st % 2 else "vector",
                         claim=["fyn", "fsgn0", "fsgn1", "fsgn2"],
                         sclaim=["fxn0", "fxn1", "fxn2", "fhs",
                                 "fxqh0", "fxqh1", "fxqh2",
                                 "fxql0", "fxql1", "fxql2"],
                         sclclaim=["fres"])
                qe = "gpsimd" if st % 2 else "vector"
                b.op(qe, lambda h: h.tensor_copy(
                    f_xfb[:, 0:2816], f_sgn[:, 0:2816]),
                    r=["fhqq"], w=["fhqh", "fhtmp", "fx", "fxq0", "fxq1", "fxq2",
                                   "fxqh0", "fxqh1", "fxql0", "fxql1"])
                b.op(qe, lambda h: h.tensor_tensor(
                    f_xfb[:, 2816:5632], f_sgn[:, 0:2816],
                    f_xfb[:, 0:2816], ALU.subtract),
                    r=["fhqq", "fhqh"], w=["fhql", "fx", "fxq1", "fxq2", "fxq3",
                                           "fxqh2", "fxqh3", "fxql2", "fxql3"])
                for kc in range(22):
                    bank, pk = pb[4 + kc % 2], PB[4 + kc % 2]
                    bkb = bank.bitcast(BF16)
                    for hl in range(2):
                        hk = "fhqh" if hl == 0 else "fhql"
                        dk = ("hqf%d_%d" % (kc, st)) if hl == 0 \
                            else ("hqfl%d_%d" % (kc, st))
                        b.op("tensor", lambda h, kc=kc, hl=hl, bkb=bkb: h.transpose(
                            bkb[:, hl * 128:hl * 128 + 128],
                            f_xfb[:, hl * 2816 + kc * 128:hl * 2816 + (kc + 1) * 128],
                            identb), r=[hk, "identb"], w=[pk])
                        if kc % 2 == 0:
                            b.op("vector", lambda h, st=st, kc=kc, hl=hl, bkb=bkb:
                                 h.tensor_copy(
                                f_hb[:, st * 5632 + hl * 2816 + kc * 128:
                                     st * 5632 + hl * 2816 + (kc + 1) * 128],
                                bkb[:, hl * 128:hl * 128 + 128]), r=[pk],
                                w=[dk, "fh%d" % st])
                        else:
                            b.op("scalar", lambda h, st=st, kc=kc, hl=hl, bkb=bkb:
                                 h.copy(
                                f_hb[:, st * 5632 + hl * 2816 + kc * 128:
                                     st * 5632 + hl * 2816 + (kc + 1) * 128],
                                bkb[:, hl * 128:hl * 128 + 128]), r=[pk],
                                w=[dk, "fh%d" % st])

        def emit_D(t):
            b.op("gpsimd", lambda h, t=t: h.dma_start(
                out=f_res.rearrange("p (c f) -> p c f", f=512),
                in_=xfT[:, t * TILE:(t + 1) * TILE].rearrange("(c p) f -> p c f", p=128)),
                w=["fres", "fhscl", "fhscli"] + ["fsil%d" % s2 for s2 in range(4)],
                chan=c_res)
            for dm in range(8):
                b.hazards("tensor", w=[PB[dm]])
            for di in range(8):
                wsb, wsk = f_wsb[di % 2], "fws%d" % (di % 2)
                b.dma_group("sync", c_ws[di % 2], [
                    (lambda h, di=di, wsb=wsb: h.dma_start(
                        out=wsb[:, 0:1024], in_=WoT_h[di * 128:(di + 1) * 128, :]),
                     [], [wsk]),
                    (lambda h, di=di, wsb=wsb: h.dma_start(
                        out=wsb[:, 1024:2048], in_=WoT_l[di * 128:(di + 1) * 128, :]),
                     [], [wsk])])
                for pi, (wof, yof) in enumerate(((0, 0), (0, 4096), (1024, 0))):
                    ynk = ("fynh%d" % di) if yof == 0 else ("fynl%d" % di)
                    for dm in range(8):
                        last = (dm == 7 and pi == 2)
                        b.op("tensor", lambda h, di=di, dm=dm, pi=pi, wof=wof, yof=yof,
                             wsb=wsb: h.matmul(
                            pb[dm], wsb[:, wof + dm * 128:wof + (dm + 1) * 128],
                            f_wqb[:, yof + di * 512:yof + (di + 1) * 512],
                            start=(di == 0 and pi == 0), stop=False,
                            skip_group_check=True),
                            r=([wsk, ynk, "fwq"] if last else [wsk, ynk]),
                            raw=not last)
            f_hb4 = f_hb.rearrange("p (st x j) -> p st x j", st=4, j=128)
            for kc in range(22):
                wsb, wsk = f_wsb[kc % 2], "fws%d" % (kc % 2)
                b.dma_group("sync", c_ws[kc % 2], [
                    (lambda h, kc=kc, wsb=wsb: h.dma_start(
                        out=wsb[:, 0:1024], in_=wdqT_h[kc * 128:(kc + 1) * 128, :]),
                     [], [wsk]),
                    (lambda h, kc=kc, wsb=wsb: h.dma_start(
                        out=wsb[:, 1024:2048], in_=wdqT_l[kc * 128:(kc + 1) * 128, :]),
                     [], [wsk])])
                hkeys = (["hqf%d_%d" % (kc, s2) for s2 in range(4)]
                         + ["hqfl%d_%d" % (kc, s2) for s2 in range(4)])
                fin = kc == 21
                for dm in range(8):
                    kclast = dm == 7
                    for pi, (wof, xc) in enumerate(((0, kc), (0, 22 + kc), (1024, kc))):
                        plast = pi == 2
                        b.op("tensor", lambda h, kc=kc, dm=dm, wof=wof, xc=xc, wsb=wsb,
                             fin=fin, plast=plast: h.matmul(
                            pb[dm],
                            wsb[:, wof + dm * 128:wof + (dm + 1) * 128],
                            f_hb4[:, :, xc, :],
                            start=False, stop=(fin and plast), skip_group_check=True),
                            r=[wsk] + hkeys,
                            w=([PB[dm]] if (fin and plast) else ()),
                            raw=not ((fin or kclast) and plast))

        def emit_TAIL(t):
            for dm in range(8):
                fhk = ["fh0"] if dm < 5 else (["fh0", "fh1"] if dm == 5 else ["fh1"])
                b.op("vector", lambda h, dm=dm: h.tensor_tensor(
                    f_h[:, dm * 512:(dm + 1) * 512], pb[dm],
                    f_res[:, dm * 512:(dm + 1) * 512], ALU.add),
                    r=[PB[dm], PB[7], "fres"], w=["ofm%d" % dm] + fhk)
            for st in range(4):
                for dm in range(8):
                    bank, pk = pb[4 + dm % 2], PB[4 + dm % 2]
                    fhk2 = ["fh0"] if dm < 5 else (["fh0", "fh1"] if dm == 5 else ["fh1"])
                    b.op("tensor", lambda h, st=st, dm=dm, bank=bank: h.transpose(
                        bank[:, 0:128],
                        f_h[:, dm * 512 + st * 128:dm * 512 + (st + 1) * 128], ident),
                        r=["ofm%d" % dm, "ident"] + fhk2, w=[pk])
                    b.op("vector", lambda h, st=st, dm=dm, bank=bank: h.tensor_copy(
                        f_h[:, 4096 + dm * 128:4096 + (dm + 1) * 128], bank[:, 0:128]),
                        r=[pk], w=["otm%d" % dm, "fh1"])
                b.op("gpsimd", lambda h, t=t, st=st: h.dma_start(
                    out=out_h[t * TILE + st * 128:t * TILE + (st + 1) * 128, :],
                    in_=f_h[:, 4096:5120]),
                    r=["otm%d" % d2 for d2 in range(8)] + ["fh1"], chan=c_out)

        emit_A(0)
        emit_XT(0)
        for t in range(8):
            if t >= 1:
                emit_TAIL(t - 1)
            if t + 1 <= 7:
                emit_A(t + 1)
            emit_MID(t)
            emit_Dpre(t)
            if t + 1 <= 7:
                emit_XT(t + 1)
            emit_C(t)
            emit_D(t)
        emit_TAIL(7)

    b.barrier()

    with nc.Block() as block:
        b.finalize(block)
    es.close()
    return nc


_NC_CACHE = {}


def _get_nc():
    if "nc" not in _NC_CACHE:
        _NC_CACHE["nc"] = build_nc()
    return _NC_CACHE["nc"]


def make_in_maps(inputs):
    x = np.asarray(inputs["L0_attn_x"], np.float32)
    W_in = np.asarray(inputs["L0_attn_W_in"], np.float32)
    conv_w = np.asarray(inputs["L0_attn_conv_w"], np.float32)
    conv_b = np.asarray(inputs["L0_attn_conv_b"], np.float32)
    A_param = np.asarray(inputs["L0_attn_A_param"], np.float32)
    dt_bias = np.asarray(inputs["L0_attn_dt_bias"], np.float32)
    W_out = np.asarray(inputs["L0_attn_W_out"], np.float32)
    wg = np.asarray(inputs["L0_ffn_w_gate"], np.float32)
    wu = np.asarray(inputs["L0_ffn_w_up"], np.float32)
    wd = np.asarray(inputs["L0_ffn_w_down"], np.float32)

    W_inT = np.ascontiguousarray(W_in.T)
    W_outT = np.ascontiguousarray(W_out.T)
    conv_b2 = np.ascontiguousarray(conv_b[:, None])
    A2 = np.ascontiguousarray(A_param[:, None])
    dtb2 = np.ascontiguousarray(dt_bias[:, None])
    ident = np.eye(128, dtype=np.float32)
    identb = np.eye(128, dtype=ml_dtypes.bfloat16)
    um = np.triu(np.ones((128, 128), np.float32))
    onesc = np.ones((128, 1), np.float32)
    ones1 = np.ones((1, 128), np.float32)
    cstv = np.zeros((128, 2), np.float32)
    cstv[:, 0] = EPS
    cstv[:, 1] = 1.0

    x0 = x[0]
    x0T_full = np.ascontiguousarray(x0.T)
    pad0 = np.zeros((D, HALF), np.float32)
    x0T_sh0 = np.ascontiguousarray(np.concatenate([pad0, x0T_full[:, :HALF]], axis=1))
    m_h0 = np.zeros((NH, S), np.float32)
    m_h0[:, HALF:] = 1.0
    m_h1 = np.ones((NH, S), np.float32)

    in_maps = []
    for c in range(8):
        bb, hh = c // 2, c % 2
        xb = x[bb]
        xbT_full = np.ascontiguousarray(xb.T)
        if hh == 0:
            x0T_c = x0T_sh0
            xbT_c = np.ascontiguousarray(
                np.concatenate([pad0, xbT_full[:, :HALF]], axis=1))
            m_c = m_h0
        else:
            x0T_c = x0T_full
            xbT_c = xbT_full
            m_c = m_h1
        xf = np.ascontiguousarray(xb[hh * HALF:(hh + 1) * HALF])
        xfT = np.ascontiguousarray(xf.T)
        in_maps.append({
            "x0T": x0T_c, "xbT": xbT_c, "xf": xf, "xfT": xfT, "m16": m_c,
            "W_inT": W_inT, "convw": conv_w, "convb": conv_b2,
            "A_param": A2, "dt_bias": dtb2, "W_outT": W_outT,
            "wg": wg, "wu": wu, "wd": wd,
            "ident_in": ident, "identb_in": identb,
            "umask_in": um, "onesc_in": onesc, "ones1_in": ones1,
            "cst_in": cstv,
        })
    return in_maps


def kernel(**inputs):
    nc = _get_nc()
    in_maps = make_in_maps(inputs)
    res = run_bass_kernel_spmd(nc, in_maps, core_ids=list(range(8)))
    out = np.zeros((4, S, D), np.float32)
    for c in range(8):
        bb, hh = c // 2, c % 2
        out[bb, hh * HALF:(hh + 1) * HALF] = res.results[c]["out"]
    return out


# version 12
# speedup vs baseline: 1.0288x; 1.0288x over previous
import contextlib
import numpy as np
import ml_dtypes
import concourse.bass as bass
import concourse.mybir as mybir
from concourse.bass_utils import run_bass_kernel_spmd

F32 = mybir.dt.float32
BF16 = mybir.dt.bfloat16
AF = mybir.ActivationFunctionType
ALU = mybir.AluOpType
AX = mybir.AxisListType

S = 8192
D = 1024
NH = 16
HD = 64
CHK = 64
TILE = 512
NT = S // TILE
NCK = TILE // CHK
HALF = 4096
DI = 2816
EPS = 1e-5

ENGS = ("sync", "tensor", "vector", "scalar", "gpsimd")


class Bld:
    def __init__(self, nc, es):
        self.nc = nc
        self.es = es
        self.esem = {e: es.enter_context(nc.semaphore("se_" + e)) for e in ENGS}
        self.ecnt = {e: 0 for e in ENGS}
        self.ops = {e: [] for e in ENGS}
        self.waited = {e: {} for e in ENGS}
        self.wr = {}
        self.rd = {}
        self.chans = {}

    def chan(self, name):
        if name not in self.chans:
            sem = self.es.enter_context(self.nc.semaphore("ch_" + name))
            self.chans[name] = [sem, 0, "c_" + name]
        return name

    def _wait(self, eng, tok):
        nm, sem, v = tok
        if self.waited[eng].get(nm, 0) >= v:
            return
        self.waited[eng][nm] = v
        self.ops[eng].append(("w", sem, v))

    def hazards(self, eng, r=(), w=()):
        for k in r:
            t = self.wr.get(k)
            if t is not None:
                self._wait(eng, t)
        for k in w:
            t = self.wr.get(k)
            if t is not None:
                self._wait(eng, t)
            for t2 in self.rd.get(k, ()):
                self._wait(eng, t2)

    def op(self, eng, fn, r=(), w=(), chan=None, raw=False, chanwait=True):
        if chan is not None:
            c = self.chans[chan]
            if chanwait and c[1] > 0:
                self._wait(eng, (c[2], c[0], c[1]))
        self.hazards(eng, r, w)
        if raw:
            self.ops[eng].append(("r", fn))
            return None
        if chan is None:
            self.ecnt[eng] += 1
            sem, v, nm, isd = self.esem[eng], self.ecnt[eng], "e_" + eng, False
        else:
            c = self.chans[chan]
            c[1] += 16
            sem, v, nm, isd = c[0], c[1], c[2], True
        tok = (nm, sem, v)
        self.ops[eng].append(("i", fn, sem, isd))
        for k in r:
            self.rd.setdefault(k, []).append(tok)
        for k in w:
            self.wr[k] = tok
            self.rd[k] = []
        return tok

    def settok(self, tok, r=(), w=()):
        for k in r:
            self.rd.setdefault(k, []).append(tok)
        for k in w:
            self.wr[k] = tok
            self.rd[k] = []

    def dma_group(self, eng, chan, items):
        for fn, r, w in items:
            self.hazards(eng, r, w)
        last = None
        for i, (fn, r, w) in enumerate(items):
            last = self.op(eng, fn, chan=chan, chanwait=(i == 0))
        allr = [k for _, r, _ in items for k in r]
        allw = [k for _, _, w in items for k in w]
        self.settok(last, r=allr, w=allw)

    def mmg(self, bank_key, items):
        self.hazards("tensor", w=[bank_key])
        n = len(items)
        union = []
        for _, rk in items:
            for k in rk:
                if k not in union:
                    union.append(k)
        for i, (fn, rk) in enumerate(items):
            if i < n - 1:
                self.op("tensor", fn, r=rk, raw=True)
            else:
                tok = self.op("tensor", fn, r=rk, w=[bank_key])
                extra = [k for k in union if k not in rk]
                self.settok(tok, r=extra)

    def barrier(self):
        toks = [(nm, sem, cnt) for (sem, cnt, nm) in self.chans.values()]
        toks += [("e_" + e, self.esem[e], self.ecnt[e]) for e in ENGS]
        for e in ENGS:
            for tk in toks:
                if tk[0] != "e_" + e and tk[2] > 0:
                    self._wait(e, tk)
        self.wr = {}
        self.rd = {}

    def finalize(self, block):
        for e in ENGS:
            ops = self.ops[e]

            def make(ops):
                def body(h):
                    for it in ops:
                        if it[0] == "w":
                            h.wait_ge(it[1], it[2])
                        elif it[0] == "r":
                            it[1](h)
                        else:
                            it[1](h).then_inc(it[2], 16 if it[3] else 1)
                return body

            getattr(block, e)(make(ops))


def build_nc(upto=4, s3mode=9):
    nc = bass.Bass("TRN2", target_bir_lowering=False)
    es = contextlib.ExitStack()

    def din(name, shape):
        return nc.dram_tensor(name, list(shape), F32, kind="ExternalInput").ap()

    def dsc(name, shape):
        return nc.dram_tensor(name, list(shape), F32, kind="Internal").ap()

    x0T = din("x0T", (D, S))
    xbT = din("xbT", (D, S))
    xf_in = din("xf", (HALF, D))
    xfT = din("xfT", (D, HALF))
    m16 = din("m16", (NH, S))
    W_inT = din("W_inT", (D, 3088))
    convw = din("convw", (3072, 4))
    convb = din("convb", (3072, 1))
    A_param = din("A_param", (NH, 1))
    dt_bias = din("dt_bias", (NH, 1))
    W_outT = din("W_outT", (D, D))
    wg_in = din("wg", (DI, D))
    wu_in = din("wu", (DI, D))
    wd_in = din("wd", (D, DI))
    ident_in = din("ident_in", (128, 128))
    identb_in = nc.dram_tensor("identb_in", [128, 128], BF16, kind="ExternalInput").ap()
    umask_in = din("umask_in", (128, 128))
    onesc_in = din("onesc_in", (128, 1))
    ones1_in = din("ones1_in", (1, 128))
    cst_in = din("cst_in", (128, 2))
    out_h = nc.dram_tensor("out", [HALF, D], F32, kind="ExternalOutput").ap()

    def dscb(name, shape):
        return nc.dram_tensor(name, list(shape), BF16, kind="Internal").ap()

    cvB = dsc("cvB", (D, S))
    cvX = dsc("cvX", (D, S))
    cvC = dsc("cvC", (D, HALF))
    wgqT_h = dscb("wgqT_h", (D, DI))
    wgqT_l = dscb("wgqT_l", (D, DI))
    wuqT_h = dscb("wuqT_h", (D, DI))
    wuqT_l = dscb("wuqT_l", (D, DI))
    wdqT_h = dscb("wdqT_h", (DI, D))
    wdqT_l = dscb("wdqT_l", (DI, D))
    WinT_h = dscb("WinT_h", (D, 3088))
    WinT_l = dscb("WinT_l", (D, 3088))
    WoT_h = dscb("WoT_h", (D, D))
    WoT_l = dscb("WoT_l", (D, D))
    rows_em_d = dsc("rows_em", (NH, S))
    rows_ep_d = dsc("rows_ep", (NH, S))
    rows_nv_d = dsc("rows_nv", (NH, S))
    YnT = dsc("YnT", (D, HALF))

    AR = 42496
    arena = es.enter_context(nc.sbuf_tensor("arena", [128, AR], F32)).ap()
    ident = es.enter_context(nc.sbuf_tensor("ident", [128, 128], F32)).ap()
    identb = es.enter_context(nc.sbuf_tensor("identb", [128, 128], BF16)).ap()
    umask = es.enter_context(nc.sbuf_tensor("umask", [128, 128], F32)).ap()
    onesc = es.enter_context(nc.sbuf_tensor("onesc", [128, 1], F32)).ap()
    ones1 = es.enter_context(nc.sbuf_tensor("ones1", [1, 128], F32)).ap()
    apar = es.enter_context(nc.sbuf_tensor("apar", [NH, 1], F32)).ap()
    dtb = es.enter_context(nc.sbuf_tensor("dtb", [NH, 1], F32)).ap()
    cwb = es.enter_context(nc.sbuf_tensor("cwb", [128, 5 * 24], F32)).ap()
    swr = es.enter_context(nc.sbuf_tensor("swr", [NH, 1], F32)).ap()
    gtab = es.enter_context(nc.sbuf_tensor("gtab", [NH, 160], F32)).ap()
    carry = es.enter_context(nc.sbuf_tensor("carry", [NH, 8], F32)).ap()
    mrow = es.enter_context(nc.sbuf_tensor("mrow", [NH, TILE], F32)).ap()
    srow = es.enter_context(nc.sbuf_tensor("srow", [1, 8 * TILE], F32)).ap()
    rowsb = es.enter_context(nc.sbuf_tensor("rowsb", [NH, 8 * TILE], F32)).ap()
    fstat = es.enter_context(nc.sbuf_tensor("fstat", [128, 16], F32)).ap()
    cst = es.enter_context(nc.sbuf_tensor("cst", [128, 2], F32)).ap()
    wdt_s = es.enter_context(nc.sbuf_tensor("wdt_sb", [128, 128], F32)).ap()

    pb = [es.enter_context(nc.psum_tensor("pbank%d" % i, [128, 512], F32)).ap()
          for i in range(8)]
    PB = ["pb%d" % i for i in range(8)]

    b = Bld(nc, es)

    R_dt, R_A, R_cum, R_em, R_ep, R_S, R_t0, R_t1 = range(8)

    def rslot(i):
        return rowsb[0:16, i * TILE:(i + 1) * TILE]

    def sslot(i):
        return srow[0:1, i * TILE:(i + 1) * TILE]

    ldc = b.chan("const")
    citems = [
        (lambda h: h.dma_start(out=ident, in_=ident_in), (), ["ident"]),
        (lambda h: h.dma_start(out=identb, in_=identb_in), (), ["identb"]),
        (lambda h: h.dma_start(out=umask, in_=umask_in), (), ["umask"]),
        (lambda h: h.dma_start(out=onesc, in_=onesc_in), (), ["onesc"]),
        (lambda h: h.dma_start(out=ones1, in_=ones1_in), (), ["ones1"]),
        (lambda h: h.dma_start(out=apar, in_=A_param), (), ["apar"]),
        (lambda h: h.dma_start(out=dtb, in_=dt_bias), (), ["dtb"]),
        (lambda h: h.dma_start(out=cst, in_=cst_in), (), ["cst"]),
    ]
    for fc in range(24):
        citems.append((lambda h, fc=fc: h.dma_start(
            out=cwb[:, 5 * fc:5 * fc + 4], in_=convw[fc * 128:(fc + 1) * 128, :]),
            (), ["cwb"]))
        citems.append((lambda h, fc=fc: h.dma_start(
            out=cwb[:, 5 * fc + 4:5 * fc + 5], in_=convb[fc * 128:(fc + 1) * 128, :]),
            (), ["cwb"]))
    b.dma_group("sync", ldc, citems)

    wQ = arena[:, 30848:33936]
    wS = arena[:, 33936:36752]
    wq1 = arena[:, 36752:39568]
    wsc = arena[:, 39568:39952]
    wSb = wS.bitcast(BF16)
    wstb = wq1.bitcast(BF16)

    c_wdt = b.chan("wdtl")
    c_w = b.chan("s0w")
    c_st = b.chan("s0st")

    b.dma_group("sync", c_wdt, [
        (lambda h, kc=kc: h.dma_start(
            out=wdt_s[:, kc * 16:(kc + 1) * 16],
            in_=W_inT[kc * 128:(kc + 1) * 128, 3072:3088]), (), ["wdt_s"])
        for kc in range(8)])
    b.mmg(PB[0], [
        (lambda h, kc=kc: h.matmul(pb[0][0:1, 0:16], onesc,
                                   wdt_s[:, kc * 16:(kc + 1) * 16],
                                   start=(kc == 0), stop=(kc == 7)),
         ["wdt_s", "onesc"]) for kc in range(8)])
    b.op("vector", lambda h: h.tensor_copy(gtab[0:1, 128:144], pb[0][0:1, 0:16]),
         r=[PB[0]], w=["swrow"])
    b.op("tensor", lambda h: h.transpose(pb[1][0:16, 0:1], gtab[0:1, 128:144], ident[0:1, 0:1]),
         r=["swrow", "ident"], w=[PB[1]])
    b.op("vector", lambda h: h.tensor_copy(swr, pb[1][0:16, 0:1]), r=[PB[1]], w=["swr"])

    def ladder(v_ap, q_ap, vkey, qkey, eng, claim=(), tmp_ap=None, tmpkey=None,
               tmpclaim=()):
        b.op(eng, lambda h: h.tensor_single_scalar(q_ap, v_ap, 0.25, ALU.is_gt),
             r=[vkey], w=[qkey] + list(claim))
        if eng == "vector":
            for c in (0.75, 1.25, 1.75):
                b.op(eng, lambda h, c=c: h.scalar_tensor_tensor(
                    q_ap, v_ap, c, q_ap, ALU.is_gt, ALU.add), r=[vkey, qkey], w=[qkey])
            b.op(eng, lambda h: h.tensor_single_scalar(q_ap, q_ap, 0.5, ALU.mult),
                 r=[qkey], w=[qkey])
            for c in (2.5, 3.5, 5.0, 5.0):
                b.op(eng, lambda h, c=c: h.scalar_tensor_tensor(
                    q_ap, v_ap, c, q_ap, ALU.is_gt, ALU.add), r=[vkey, qkey], w=[qkey])
        else:
            first = True
            for c in (0.75, 1.25, 1.75):
                b.op(eng, lambda h, c=c: h.tensor_single_scalar(
                    tmp_ap, v_ap, c, ALU.is_gt),
                    r=[vkey], w=[tmpkey] + (list(tmpclaim) if first else []))
                first = False
                b.op(eng, lambda h: h.tensor_tensor(q_ap, q_ap, tmp_ap, ALU.add),
                     r=[qkey, tmpkey], w=[qkey])
            b.op(eng, lambda h: h.tensor_single_scalar(q_ap, q_ap, 0.5, ALU.mult),
                 r=[qkey], w=[qkey])
            for c in (2.5, 3.5, 5.0, 5.0):
                b.op(eng, lambda h, c=c: h.tensor_single_scalar(
                    tmp_ap, v_ap, c, ALU.is_gt), r=[vkey], w=[tmpkey])
                b.op(eng, lambda h: h.tensor_tensor(q_ap, q_ap, tmp_ap, ALU.add),
                     r=[qkey, tmpkey], w=[qkey])

    def quant_tm(x_ap, xkey, sgn_ap, skey, q_ap, qkey, scl_ap, sclkey, n, eng="vector",
                 claim=(), sclaim=(), sclclaim=(), tmp_ap=None, tmpkey=None,
                 tmpclaim=()):
        nb = n // 16
        if eng == "gpsimd" and tmp_ap is None:
            eng = "vector"
        b.op("vector", lambda h: h.tensor_reduce(
            scl_ap[:, 0:nb], x_ap.rearrange("p (bb k) -> p bb k", k=16),
            AX.X, ALU.max, apply_absolute_value=True),
            r=[xkey], w=[sclkey] + list(sclclaim))
        b.op(eng, lambda h: h.tensor_scalar(
            scl_ap[:, 0:nb], scl_ap[:, 0:nb], 1.0 / 6.0, 1e-12, ALU.mult, ALU.max),
            r=[sclkey], w=[sclkey])
        b.op("vector", lambda h: h.reciprocal(scl_ap[:, nb:2 * nb], scl_ap[:, 0:nb]),
             r=[sclkey], w=[sclkey + "i"])
        b.op("scalar", lambda h: h.activation(sgn_ap, x_ap, AF.Sign),
             r=[xkey], w=[skey] + list(sclaim))
        b.op("scalar", lambda h: h.activation(x_ap, x_ap, AF.Abs), r=[xkey, skey], w=[xkey])
        b.op(eng, lambda h: h.tensor_tensor(
            x_ap.rearrange("p (bb k) -> p bb k", k=16),
            x_ap.rearrange("p (bb k) -> p bb k", k=16),
            scl_ap[:, nb:2 * nb].unsqueeze(2).to_broadcast([128, nb, 16]), ALU.mult),
            r=[xkey, sclkey + "i"], w=[xkey])
        ladder(x_ap, q_ap, xkey, qkey, eng, claim=claim,
               tmp_ap=tmp_ap, tmpkey=tmpkey, tmpclaim=tmpclaim)
        b.op(eng, lambda h: h.tensor_tensor(q_ap, q_ap, sgn_ap, ALU.mult),
             r=[qkey, skey], w=[qkey])
        b.op(eng, lambda h: h.tensor_tensor(
            q_ap.rearrange("p (bb k) -> p bb k", k=16),
            q_ap.rearrange("p (bb k) -> p bb k", k=16),
            scl_ap[:, 0:nb].unsqueeze(2).to_broadcast([128, nb, 16]), ALU.mult),
            r=[qkey, sclkey], w=[qkey])

    def s0_weight(w_in_ap, wT_h, wT_l, nrows, ncols):
        nblk = ncols // 128
        for rc in range(nrows // 128):
            yield
            b.op("sync", lambda h, rc=rc: h.dma_start(
                out=wQ[:, 0:ncols], in_=w_in_ap[rc * 128:(rc + 1) * 128, :]),
                w=["wQ"], chan=c_w)
            quant_tm(wQ[:, 0:ncols], "wQ", wS[:, 0:ncols], "wS",
                     wq1[:, 0:ncols], "wq1", wsc, "wsc", ncols)
            b.op("vector", lambda h: h.tensor_copy(wSb[:, 0:ncols], wq1[:, 0:ncols]),
                 r=["wq1"], w=["wS", "whl"])
            b.op("vector", lambda h: h.tensor_tensor(
                wSb[:, ncols:2 * ncols], wq1[:, 0:ncols], wSb[:, 0:ncols], ALU.subtract),
                r=["wq1", "whl"], w=["whl2"])
            yield
            for hl in range(2):
                hk = "whl" if hl == 0 else "whl2"
                dst_d = wT_h if hl == 0 else wT_l
                for tb in range(nblk):
                    bank, pk = pb[tb % 2], PB[tb % 2]
                    bkb = bank.bitcast(BF16)
                    b.op("tensor", lambda h, tb=tb, hl=hl, bkb=bkb: h.transpose(
                        bkb[:, 0:128], wSb[:, hl * ncols + tb * 128:hl * ncols + (tb + 1) * 128],
                        identb), r=[hk, "identb"], w=[pk])
                    if tb % 2 == 0:
                        b.op("vector", lambda h, tb=tb, bkb=bkb: h.tensor_copy(
                            wstb[:, tb * 128:(tb + 1) * 128], bkb[:, 0:128]),
                            r=[pk], w=["wst%d" % tb, "wq1"])
                    else:
                        b.op("scalar", lambda h, tb=tb, bkb=bkb: h.copy(
                            wstb[:, tb * 128:(tb + 1) * 128], bkb[:, 0:128]),
                            r=[pk], w=["wst%d" % tb, "wq1"])
                for tb in range(nblk):
                    b.op("scalar", lambda h, tb=tb, rc=rc, dst_d=dst_d: h.dma_start(
                        out=dst_d[tb * 128:(tb + 1) * 128, rc * 128:(rc + 1) * 128],
                        in_=wstb[:, tb * 128:(tb + 1) * 128]),
                        r=["wst%d" % tb, "wq1"], chan=c_st)

    def s0_plain(w_in_ap, wT_h, wT_l, nrows, ncols):
        for rc in range(nrows // 128):
            yield
            b.op("sync", lambda h, rc=rc: h.dma_start(
                out=wQ[:, 0:ncols], in_=w_in_ap[rc * 128:(rc + 1) * 128, :]),
                w=["wQ"], chan=c_w)
            b.op("vector", lambda h: h.tensor_copy(wSb[:, 0:ncols], wQ[:, 0:ncols]),
                 r=["wQ"], w=["wS", "whl"])
            b.op("vector", lambda h: h.tensor_tensor(
                wstb[:, 0:ncols], wQ[:, 0:ncols], wSb[:, 0:ncols], ALU.subtract),
                r=["wQ", "whl"], w=["wq1", "whl2"])
            b.op("scalar", lambda h, rc=rc: h.dma_start(
                out=wT_h[rc * 128:(rc + 1) * 128, :], in_=wSb[:, 0:ncols]),
                r=["whl"], chan=c_st)
            b.op("scalar", lambda h, rc=rc: h.dma_start(
                out=wT_l[rc * 128:(rc + 1) * 128, :], in_=wstb[:, 0:ncols]),
                r=["whl2"], chan=c_st)

    def s0_steps():
        yield from s0_plain(W_inT, WinT_h, WinT_l, D, 3088)
        yield from s0_weight(wg_in, wgqT_h, wgqT_l, DI, D)
        yield from s0_weight(wu_in, wuqT_h, wuqT_l, DI, D)
        yield from s0_weight(wd_in, wdqT_h, wdqT_l, D, DI)
        yield from s0_plain(W_outT, WoT_h, WoT_l, D, D)
    s0_gen = s0_steps()

    a_x = arena[:, 0:4096]
    a_sq = arena[:, 4096:4608]
    a_cv = arena[:, 4608:4608 + 16 * 515]
    a_cvC = arena[:, 12848:12848 + 8 * 515]
    a_cd = [arena[:, 16968:17480], arena[:, 17480:17992]]
    a_wb = [arena[:, 17992:19016], arena[:, 19016:20040]]
    a_xn = arena[:, 20040:24136]
    a_bc = arena[:, 24136:25160]
    a_ct = arena[:, 25160:26696]
    a_xnhl = arena[:, 26696:30792]
    a_xnb = a_xnhl.bitcast(BF16)
    a_wbb = [a_wb[0].bitcast(BF16), a_wb[1].bitcast(BF16)]

    c_x = [b.chan("xs0"), b.chan("xs1")]
    c_wp = [b.chan("wp0"), b.chan("wp1")]
    c_m = b.chan("mrowc")
    c_sp = [b.chan("sp0"), b.chan("sp1")]
    c_rw = b.chan("rowsw")

    def stats_and_dt(key):
        b.mmg(PB[6], [
            (lambda h, fc=fc: h.matmul(pb[6][0:1, 0:512], onesc,
                                       a_x[:, fc * 512:(fc + 1) * 512],
                                       start=(fc == 0), stop=(fc == 7)),
             [key + "x", "onesc"]) for fc in range(8)])
        b.hazards("tensor", w=[PB[7]])
        for fc in range(8):
            b.op("scalar", lambda h, fc=fc: h.activation(
                a_sq[:, 0:512], a_x[:, fc * 512:(fc + 1) * 512], AF.Square),
                r=[key + "x"], w=["sqs"])
            b.op("tensor", lambda h, fc=fc: h.matmul(
                pb[7][0:1, 0:512], onesc, a_sq[:, 0:512],
                start=(fc == 0), stop=(fc == 7)),
                r=["sqs", "onesc"], w=([PB[7]] if fc == 7 else ()))
        b.op("vector", lambda h: h.tensor_single_scalar(
            sslot(0), pb[6][0:1, 0:512], 1.0 / D, ALU.mult), r=[PB[6]], w=["S0k"])
        b.op("scalar", lambda h: h.activation(sslot(1), sslot(0), AF.Square),
             r=["S0k"], w=["S1k"])
        b.op("vector", lambda h: h.scalar_tensor_tensor(
            sslot(1), pb[7][0:1, 0:512], 1.0 / D, sslot(1), ALU.mult, ALU.subtract),
            r=[PB[7], "S1k"], w=["S1k"])
        b.op("scalar", lambda h: h.activation(sslot(1), sslot(1), AF.Sqrt, bias=cst[0:1, 0:1]),
             r=["S1k"], w=["S1k"])
        b.op("vector", lambda h: h.reciprocal(sslot(2), sslot(1)), r=["S1k"], w=["S2k"])
        b.op("vector", lambda h: h.tensor_tensor(sslot(3), sslot(0), sslot(2), ALU.mult),
             r=["S0k", "S2k"], w=["S3k"])
        b.mmg(PB[5], [
            (lambda h, kc=kc: h.matmul(pb[5][0:16, 0:512], wdt_s[:, kc * 16:(kc + 1) * 16],
                                       a_x[:, kc * 512:(kc + 1) * 512],
                                       start=(kc == 0), stop=(kc == 7)),
             ["wdt_s", key + "x"]) for kc in range(8)])
        b.op("tensor", lambda h: h.matmul(pb[4][0:16, 0:512], ones1[0:1, 0:16], sslot(2),
                                          start=True, stop=True),
             r=["S2k", "ones1"], w=[PB[4]])
        b.op("scalar", lambda h: h.copy(rslot(R_t1), pb[5][0:16, 0:512]),
             r=[PB[5]], w=["Rt1"])
        b.op("vector", lambda h: h.tensor_tensor(
            rslot(R_t0), rslot(R_t1), pb[4][0:16, 0:512], ALU.mult),
            r=["Rt1", PB[4]], w=["Rt0"])
        b.op("tensor", lambda h: h.matmul(pb[4][0:16, 0:512], ones1[0:1, 0:16], sslot(3),
                                          start=True, stop=True),
             r=["S3k", "ones1"], w=[PB[4]])
        b.op("vector", lambda h: h.scalar_tensor_tensor(
            rslot(R_dt), pb[4][0:16, 0:512], swr[:, 0:1], rslot(R_t0), ALU.mult, ALU.subtract),
            r=[PB[4], "swr", "Rt0"], w=["Rdt"])
        b.op("scalar", lambda h: h.activation(rslot(R_t1), rslot(R_dt), AF.Exp,
                                              bias=dtb[:, 0:1], scale=-1.0),
             r=["Rdt", "dtb"], w=["Rt1"])
        b.op("scalar", lambda h: h.activation(rslot(R_dt), rslot(R_t1), AF.Ln, bias=cst[0:16, 1:2]),
             r=["Rt1"], w=["Rdt"])
        b.op("vector", lambda h: h.tensor_scalar_mul(rslot(R_A), rslot(R_dt), apar[:, 0:1]),
             r=["Rdt", "apar"], w=["RA"])
        b.op("vector", lambda h: h.tensor_tensor(rslot(R_A), rslot(R_A), mrow, ALU.mult),
             r=["RA", "mrowk"], w=["RA"])

    if upto >= 1:
        b.op("vector", lambda h: h.memset(carry[:, 2:3], 0.0), w=["c_D"])
        b.op("vector", lambda h: h.memset(carry[:, 3:4], 1.0), w=["c_u"])
        for t in range(NT):
            b.op("sync", lambda h, t=t: h.dma_start(
                out=a_x.rearrange("p (c f) -> p c f", f=512),
                in_=xbT[:, t * TILE:(t + 1) * TILE].rearrange("(c p) f -> p c f", p=128)),
                w=["s1x"], chan=c_x[t % 2])
            b.op("sync", lambda h, t=t: h.dma_start(
                out=mrow, in_=m16[:, t * TILE:(t + 1) * TILE]), w=["mrowk"], chan=c_m)
            stats_and_dt("s1")
            b.op("scalar", lambda h: h.activation(rslot(R_em), rslot(R_A), AF.Exp),
                 r=["RA"], w=["Rem"])
            b.op("vector", lambda h: h.memset(rslot(R_t1), 1.0), w=["Rt1"])
            b.op("vector", lambda h: h.tensor_scalar_mul(rslot(R_t0), rslot(R_t1), carry[:, 3:4]),
                 r=["Rt1", "c_u"], w=["Rt0"])
            b.op("vector", lambda h: h.tensor_tensor(rslot(R_t0), rslot(R_t0), mrow, ALU.mult),
                 r=["Rt0", "mrowk"], w=["Rt0"])
            b.op("vector", lambda h: h.tensor_tensor_scan(
                rslot(R_S), rslot(R_em), rslot(R_t0), carry[:, 2:3], ALU.mult, ALU.add),
                r=["Rem", "Rt0", "c_D"], w=["RS"])
            b.op("vector", lambda h: h.tensor_single_scalar(
                rslot(R_t1), rslot(R_S), 1e-12, ALU.max), r=["RS"], w=["Rt1"])
            b.op("vector", lambda h: h.reciprocal(rslot(R_dt), rslot(R_t1)),
                 r=["Rt1"], w=["Rdt"])
            b.op("vector", lambda h: h.tensor_scalar_mul(rslot(R_dt), rslot(R_dt), carry[:, 3:4]),
                 r=["Rdt", "c_u"], w=["Rdt"])
            b.op("scalar", lambda h, t=t: h.dma_start(
                out=rows_nv_d[:, t * TILE:(t + 1) * TILE], in_=rslot(R_dt)),
                r=["Rdt"], chan=c_rw)
            b.op("vector", lambda h: h.tensor_copy(carry[:, 4:5], rslot(R_S)[:, TILE - 1:TILE]),
                 r=["RS"], w=["c_t"])
            b.op("vector", lambda h: h.tensor_single_scalar(
                carry[:, 5:6], carry[:, 4:5], 1e-12, ALU.max), r=["c_t"], w=["c_r"])
            b.op("vector", lambda h: h.reciprocal(carry[:, 5:6], carry[:, 5:6]),
                 r=["c_r"], w=["c_r"])
            b.op("vector", lambda h: h.reduce_max(carry[:, 6:7], mrow, AX.X),
                 r=["mrowk"], w=["c_m1"])
            b.op("vector", lambda h: h.tensor_tensor(carry[:, 5:6], carry[:, 5:6],
                                                     carry[:, 6:7], ALU.mult),
                 r=["c_r", "c_m1"], w=["c_r"])
            b.op("vector", lambda h: h.tensor_scalar(carry[:, 6:7], carry[:, 6:7],
                                                     -1.0, 1.0, ALU.mult, ALU.add),
                 r=["c_m1"], w=["c_m1"])
            b.op("vector", lambda h: h.tensor_tensor(carry[:, 5:6], carry[:, 5:6],
                                                     carry[:, 6:7], ALU.add),
                 r=["c_r", "c_m1"], w=["c_r"])
            b.op("vector", lambda h: h.tensor_tensor(carry[:, 2:3], carry[:, 4:5],
                                                     carry[:, 5:6], ALU.mult),
                 r=["c_t", "c_r"], w=["c_D"])
            b.op("vector", lambda h: h.tensor_tensor(carry[:, 3:4], carry[:, 3:4],
                                                     carry[:, 5:6], ALU.mult),
                 r=["c_u", "c_r"], w=["c_u"])
        b.barrier()

    def proj_conv(spill_col, ofs, winf, wkeyf, cwcf, spill):
        for j, o in enumerate(ofs):
            wbb, wk = a_wbb[j % 2], "wb%d" % (j % 2)
            b.dma_group("sync", c_wp[j % 2], [
                (lambda h, o=o, wbb=wbb: h.dma_start(
                    out=wbb[:, 0:1024].rearrange("p (c f) -> p c f", f=128),
                    in_=WinT_h[:, o * 128:(o + 1) * 128].rearrange("(c p) f -> p c f", p=128)),
                 [], [wk]),
                (lambda h, o=o, wbb=wbb: h.dma_start(
                    out=wbb[:, 1024:2048].rearrange("p (c f) -> p c f", f=128),
                    in_=WinT_l[:, o * 128:(o + 1) * 128].rearrange("(c p) f -> p c f", p=128)),
                 [], [wk])])
            bank, pk = pb[j % 2], PB[j % 2]
            items = []
            for wo, xo in ((0, 0), (0, 4096), (1024, 0)):
                for kc in range(8):
                    first = (wo == 0 and xo == 0 and kc == 0)
                    last = (wo == 1024 and kc == 7)
                    xk = ("xnh%d" % kc) if xo == 0 else ("xnl%d" % kc)
                    items.append((lambda h, kc=kc, wo=wo, xo=xo, wbb=wbb, bank=bank,
                                  first=first, last=last: h.matmul(
                        bank, wbb[:, wo + kc * 128:wo + (kc + 1) * 128],
                        a_xnb[:, xo + kc * 512:xo + (kc + 1) * 512],
                        start=first, stop=last), [wk, xk]))
            b.mmg(pk, items)
            win, wkey = winf(j), wkeyf(j)
            if j % 2 == 0:
                b.op("vector", lambda h, win=win, bank=bank: h.tensor_copy(win[:, 3:515], bank),
                     r=[pk], w=[wkey])
            else:
                b.op("scalar", lambda h, win=win, bank=bank: h.copy(win[:, 3:515], bank),
                     r=[pk], w=[wkey])
            cd, cdk = a_cd[j % 2], "cd%d" % (j % 2)
            cc = cwcf(j)
            b.op("scalar", lambda h, win=win, cd=cd, cc=cc: h.activation(
                cd, win[:, 0:512], AF.Copy, scale=cwb[:, 5 * cc:5 * cc + 1]),
                r=[wkey, "cwb"], w=[cdk])
            for tap in range(1, 4):
                b.op("scalar", lambda h, win=win, cc=cc, tap=tap: h.activation(
                    a_ct[:, (tap - 1) * 512:tap * 512], win[:, tap:tap + 512], AF.Copy,
                    scale=cwb[:, 5 * cc + tap:5 * cc + tap + 1]),
                    r=[wkey, "cwb"], w=["ctmp%d" % tap])
            b.op("vector", lambda h, cd=cd, cc=cc: h.scalar_tensor_tensor(
                a_ct[:, 0:512], a_ct[:, 0:512], cwb[:, 5 * cc + 4:5 * cc + 5],
                a_ct[:, 512:1024], ALU.add, ALU.add),
                r=["ctmp1", "ctmp2", "cwb"], w=["ctmp1"])
            b.op("vector", lambda h, cd=cd: h.tensor_tensor(
                cd, cd, a_ct[:, 1024:1536], ALU.add), r=[cdk, "ctmp3"], w=[cdk])
            b.op("vector", lambda h, cd=cd: h.tensor_tensor(
                cd, cd, a_ct[:, 0:512], ALU.add), r=[cdk, "ctmp1"], w=[cdk])
            b.op("gpsimd", lambda h, win=win: h.tensor_copy(win[:, 0:3], win[:, 512:515]),
                 r=[wkey], w=[wkey])
            b.op("scalar", lambda h, cd=cd, j=j, spill=spill, spill_col=spill_col: h.dma_start(
                out=spill[j * 128:(j + 1) * 128, spill_col * TILE:(spill_col + 1) * TILE],
                in_=cd), r=[cdk], chan=c_sp[j % 2])

    if upto >= 2:
        for _ in range(9):
            next(s0_gen, None)
        b.barrier()
        b.op("vector", lambda h: h.memset(carry[:, 0:1], 0.0), w=["c_cum"])
        for fc in range(16):
            b.op("gpsimd", lambda h, fc=fc: h.memset(a_cv[:, fc * 515:fc * 515 + 3], 0.0),
                 w=["cw%d" % fc])
        for fc in range(8):
            b.op("gpsimd", lambda h, fc=fc: h.memset(a_cvC[:, fc * 515:fc * 515 + 3], 0.0),
                 w=["cwC%d" % fc])
        for t in range(NT):
            b.op("sync", lambda h, t=t: h.dma_start(
                out=a_x.rearrange("p (c f) -> p c f", f=512),
                in_=x0T[:, t * TILE:(t + 1) * TILE].rearrange("(c p) f -> p c f", p=128)),
                w=["s2x"], chan=c_x[t % 2])
            b.op("sync", lambda h, t=t: h.dma_start(
                out=mrow, in_=m16[:, t * TILE:(t + 1) * TILE]), w=["mrowk"], chan=c_m)
            stats_and_dt("s2")
            b.op("tensor", lambda h: h.matmul(pb[2], ones1, sslot(0), start=True, stop=True),
                 r=["S0k", "ones1"], w=[PB[2]])
            b.op("tensor", lambda h: h.matmul(pb[3], ones1, sslot(2), start=True, stop=True),
                 r=["S2k", "ones1"], w=[PB[3]])
            b.op("scalar", lambda h: h.copy(a_bc[:, 0:512], pb[2]), r=[PB[2]], w=["bcmu"])
            b.op("scalar", lambda h: h.copy(a_bc[:, 512:1024], pb[3]), r=[PB[3]], w=["bcis"])
            for fc in range(8):
                ee = "vector" if fc % 2 == 0 else "gpsimd"
                b.op(ee, lambda h, fc=fc: h.tensor_tensor(
                    a_xn[:, fc * 512:(fc + 1) * 512], a_x[:, fc * 512:(fc + 1) * 512],
                    a_bc[:, 0:512], ALU.subtract), r=["s2x", "bcmu"], w=["xn%d" % fc])
                b.op(ee, lambda h, fc=fc: h.tensor_tensor(
                    a_xn[:, fc * 512:(fc + 1) * 512], a_xn[:, fc * 512:(fc + 1) * 512],
                    a_bc[:, 512:1024], ALU.mult), r=["xn%d" % fc, "bcis"], w=["xn%d" % fc])
                b.op(ee, lambda h, fc=fc: h.tensor_copy(
                    a_xnb[:, fc * 512:(fc + 1) * 512], a_xn[:, fc * 512:(fc + 1) * 512]),
                    r=["xn%d" % fc], w=["xnh%d" % fc])
                b.op(ee, lambda h, fc=fc: h.tensor_tensor(
                    a_xnb[:, 4096 + fc * 512:4096 + (fc + 1) * 512],
                    a_xn[:, fc * 512:(fc + 1) * 512],
                    a_xnb[:, fc * 512:(fc + 1) * 512], ALU.subtract),
                    r=["xn%d" % fc, "xnh%d" % fc], w=["xnl%d" % fc])
            proj_conv(t, list(range(8, 16)),
                      lambda j: a_cv[:, j * 515:(j + 1) * 515], lambda j: "cw%d" % j,
                      lambda j: 8 + j, cvB)
            for _ in range(3):
                next(s0_gen, None)
            proj_conv(t, list(range(16, 24)),
                      lambda j: a_cv[:, (8 + j) * 515:(9 + j) * 515],
                      lambda j: "cw%d" % (8 + j), lambda j: 16 + j, cvX)
            if t == 7:
                for j, o in enumerate(range(0, 8)):
                    wbb, wk = a_wbb[j % 2], "wb%d" % (j % 2)
                    b.dma_group("sync", c_wp[j % 2], [
                        (lambda h, o=o, wbb=wbb: h.dma_start(
                            out=wbb[:, 0:1024].rearrange("p (c f) -> p c f", f=128),
                            in_=WinT_h[:, o * 128:(o + 1) * 128].rearrange(
                                "(c p) f -> p c f", p=128)),
                         [], [wk]),
                        (lambda h, o=o, wbb=wbb: h.dma_start(
                            out=wbb[:, 1024:2048].rearrange("p (c f) -> p c f", f=128),
                            in_=WinT_l[:, o * 128:(o + 1) * 128].rearrange(
                                "(c p) f -> p c f", p=128)),
                         [], [wk])])
                    bank, pk = pb[j % 2], PB[j % 2]
                    items = []
                    for wo, xo in ((0, 0), (0, 4096), (1024, 0)):
                        for kc in range(8):
                            first = (wo == 0 and xo == 0 and kc == 0)
                            last = (wo == 1024 and kc == 7)
                            xk = ("xnh%d" % kc) if xo == 0 else ("xnl%d" % kc)
                            items.append((lambda h, kc=kc, wo=wo, xo=xo, wbb=wbb,
                                          bank=bank, first=first, last=last: h.matmul(
                                bank[:, 0:3], wbb[:, wo + kc * 128:wo + (kc + 1) * 128],
                                a_xnb[:, xo + kc * 512 + 509:xo + kc * 512 + 512],
                                start=first, stop=last), [wk, xk]))
                    b.mmg(pk, items)
                    b.op("vector", lambda h, j=j, bank=bank: h.tensor_copy(
                        a_cvC[:, j * 515:j * 515 + 3], bank[:, 0:3]),
                        r=[pk], w=["cwC%d" % j])
            for _ in range(3):
                next(s0_gen, None)
            if t >= 8:
                proj_conv(t - 8, list(range(0, 8)),
                          lambda j: a_cvC[:, j * 515:(j + 1) * 515],
                          lambda j: "cwC%d" % j, lambda j: j, cvC)
            b.op("vector", lambda h: h.tensor_tensor_scan(
                rslot(R_cum), rslot(R_A), rslot(R_A), carry[:, 0:1], ALU.add, ALU.bypass),
                r=["RA", "c_cum"], w=["Rcum"])
            b.op("vector", lambda h: h.tensor_copy(carry[:, 0:1], rslot(R_cum)[:, TILE - 1:TILE]),
                 r=["Rcum"], w=["c_cum"])
            b.op("scalar", lambda h: h.activation(rslot(R_em), rslot(R_cum), AF.Exp, scale=-1.0),
                 r=["Rcum"], w=["Rem"])
            b.op("vector", lambda h: h.tensor_tensor(rslot(R_em), rslot(R_em), mrow, ALU.mult),
                 r=["Rem", "mrowk"], w=["Rem"])
            b.op("scalar", lambda h: h.activation(rslot(R_ep), rslot(R_cum), AF.Exp),
                 r=["Rcum"], w=["Rep"])
            b.op("scalar", lambda h, t=t: h.dma_start(
                out=rows_em_d[:, t * TILE:(t + 1) * TILE], in_=rslot(R_em)),
                r=["Rem"], chan=c_rw)
            b.op("scalar", lambda h, t=t: h.dma_start(
                out=rows_ep_d[:, t * TILE:(t + 1) * TILE], in_=rslot(R_ep)),
                r=["Rep"], chan=c_rw)
            for _ in range(2):
                next(s0_gen, None)
        for _ in s0_gen:
            pass
        b.barrier()

    if upto < 2:
        for _ in s0_gen:
            pass
        b.barrier()
    if upto >= 3:
        CH2 = 128
        NCK2 = TILE // CH2
        s_B64 = arena[0:64, 0:8192]
        s_X64 = arena[0:64, 8192:16384]
        s_C64 = arena[0:64, 16384:24576]
        s_ms = arena[:, 24576:26624]
        s_xt = arena[:, 26624:27648]
        s_Bt = arena[:, 27648:28672]
        s_ACC = arena[0:64, 28672:29696]
        s_Y = arena[:, 29696:30720]
        s_emt = arena[:, 30720:30784]
        s_ft = arena[:, 30784:30848]
        s_Yne = arena[0:64, 30848:34944]
        s_Yno = arena[0:64, 34944:39040]

        c_b64 = [b.chan("b64_0"), b.chan("b64_1")]
        c_x64 = [b.chan("x64_0"), b.chan("x64_1")]
        c_c64 = [b.chan("c64_0"), b.chan("c64_1")]
        c_rem = b.chan("rowem")
        c_rep = b.chan("rowep")
        c_rnv = b.chan("rownv")
        c_yn = [b.chan("yn0"), b.chan("yn1")]

        b.op("vector", lambda h: h.memset(s_ACC[:, 0:512], 0.0), w=["ACC0"])
        b.op("vector", lambda h: h.memset(s_ACC[:, 512:1024], 0.0), w=["ACC1"])

        YnT4 = YnT.rearrange("(g two p) f -> two p g f", two=2, p=64)

        for t in range(NT):
            isY = t >= 8
            b.op("sync", lambda h, t=t: h.dma_start(
                out=s_B64.rearrange("p (g f) -> p g f", f=512),
                in_=cvB[:, t * TILE:(t + 1) * TILE].rearrange("(g p) f -> p g f", p=64)),
                w=["B64"], chan=c_b64[t % 2])
            b.op("sync", lambda h, t=t: h.dma_start(
                out=s_X64.rearrange("p (g f) -> p g f", f=512),
                in_=cvX[:, t * TILE:(t + 1) * TILE].rearrange("(g p) f -> p g f", p=64)),
                w=["X64"], chan=c_x64[t % 2])
            if isY:
                b.op("sync", lambda h, t=t: h.dma_start(
                    out=s_C64.rearrange("p (g f) -> p g f", f=512),
                    in_=cvC[:, (t - 8) * TILE:(t - 7) * TILE].rearrange("(g p) f -> p g f", p=64)),
                    w=["C64"], chan=c_c64[t % 2])
            b.op("sync", lambda h, t=t: h.dma_start(
                out=rslot(R_em), in_=rows_em_d[:, t * TILE:(t + 1) * TILE]),
                w=["Rem"], chan=c_rem)
            if isY:
                b.op("sync", lambda h, t=t: h.dma_start(
                    out=rslot(R_ep), in_=rows_ep_d[:, t * TILE:(t + 1) * TILE]),
                    w=["Rep"], chan=c_rep)
                b.op("sync", lambda h, t=t: h.dma_start(
                    out=rslot(R_t0), in_=rows_nv_d[:, t * TILE:(t + 1) * TILE]),
                    w=["Rt0"], chan=c_rnv)
                b.op("vector", lambda h: h.tensor_tensor(
                    rslot(R_ep), rslot(R_ep), rslot(R_t0), ALU.mult),
                    r=["Rep", "Rt0"], w=["Rep"])
            for blk in range(4):
                bank, pk = pb[6 + blk % 2], PB[6 + blk % 2]
                b.op("tensor", lambda h, blk=blk, bank=bank: h.transpose(
                    bank[:, 0:16], rslot(R_em)[:, blk * 128:(blk + 1) * 128],
                    ident[0:16, 0:16]), r=["Rem", "ident"], w=[pk])
                b.op("vector", lambda h, blk=blk, bank=bank: h.tensor_copy(
                    s_emt[:, blk * 16:(blk + 1) * 16], bank[:, 0:16]),
                    r=[pk], w=["emt"])
                if isY:
                    b.op("tensor", lambda h, blk=blk, bank=bank: h.transpose(
                        bank[:, 0:16], rslot(R_ep)[:, blk * 128:(blk + 1) * 128],
                        ident[0:16, 0:16]), r=["Rep", "ident"], w=[pk])
                    b.op("vector", lambda h, blk=blk, bank=bank: h.tensor_copy(
                        s_ft[:, blk * 16:(blk + 1) * 16], bank[:, 0:16]),
                        r=[pk], w=["ft"])
            for ci in range(NCK2):
                for hh in range(NH):
                    bank, pk = pb[6 + hh % 2], PB[6 + hh % 2]
                    xsl = s_X64[:, hh * 512 + ci * 128:hh * 512 + (ci + 1) * 128]
                    b.op("tensor", lambda h, xsl=xsl, bank=bank: h.transpose(
                        bank[:, 0:64], xsl, ident[0:64, 0:64]),
                        r=["X64", "ident"], w=[pk])
                    if hh % 2 == 0:
                        b.op("vector", lambda h, hh=hh, bank=bank, ci=ci: h.tensor_scalar_mul(
                            s_xt[:, hh * 64:(hh + 1) * 64], bank[:, 0:64],
                            s_emt[:, ci * 16 + hh:ci * 16 + hh + 1]),
                            r=[pk, "emt"], w=["xt%d" % hh])
                    else:
                        b.op("scalar", lambda h, hh=hh, bank=bank, ci=ci: h.activation(
                            s_xt[:, hh * 64:(hh + 1) * 64], bank[:, 0:64], AF.Copy,
                            scale=s_emt[:, ci * 16 + hh:ci * 16 + hh + 1]),
                            r=[pk, "emt"], w=["xt%d" % hh])
                    bsl = s_B64[:, hh * 512 + ci * 128:hh * 512 + (ci + 1) * 128]
                    b.op("tensor", lambda h, bsl=bsl, bank=bank: h.transpose(
                        bank[:, 0:64], bsl, ident[0:64, 0:64]),
                        r=["B64", "ident"], w=[pk])
                    if hh % 2 == 0:
                        b.op("vector", lambda h, hh=hh, bank=bank: h.tensor_copy(
                            s_Bt[:, hh * 64:(hh + 1) * 64], bank[:, 0:64]),
                            r=[pk], w=["Bt%d" % hh])
                    else:
                        b.op("scalar", lambda h, hh=hh, bank=bank: h.copy(
                            s_Bt[:, hh * 64:(hh + 1) * 64], bank[:, 0:64]),
                            r=[pk], w=["Bt%d" % hh])
                if isY:
                    for bk in range(4):
                        items = []
                        for j in range(4):
                            hh = bk * 4 + j
                            items.append((lambda h, hh=hh, j=j, bk=bk, ci=ci,
                                          first=(j == 0), lastf=(j == 3): h.matmul(
                                pb[bk][:, j * 128:(j + 1) * 128],
                                s_B64[:, hh * 512 + ci * 128:hh * 512 + (ci + 1) * 128],
                                s_C64[:, hh * 512 + ci * 128:hh * 512 + (ci + 1) * 128],
                                start=first, stop=lastf), ["B64", "C64"]))
                        b.mmg(PB[bk], items)
                    for bk in range(4):
                        b.op("vector", lambda h, bk=bk: h.tensor_tensor(
                            s_ms[:, bk * 512:(bk + 1) * 512].rearrange("p (g f) -> p g f", f=128),
                            pb[bk].rearrange("p (g f) -> p g f", f=128),
                            umask.unsqueeze(1).to_broadcast([128, 4, 128]), ALU.mult),
                            r=[PB[bk], "umask"], w=["ms%d" % bk])
                    for half in range(2):
                        items = []
                        for j in range(8):
                            hh = half * 8 + j
                            off = j * 64
                            items.append((lambda h, hh=hh, off=off, half=half,
                                          first=(j == 0): h.matmul(
                                pb[4 + half][:, off:off + 64],
                                s_ms[:, hh * 128:(hh + 1) * 128],
                                s_xt[:, hh * 64:(hh + 1) * 64],
                                start=first, stop=False),
                                ["ms%d" % (hh // 4), "xt%d" % hh]))
                            items.append((lambda h, hh=hh, off=off, ci=ci, half=half,
                                          lastf=(j == 7): h.matmul(
                                pb[4 + half][:, off:off + 64],
                                s_C64[:, hh * 512 + ci * 128:hh * 512 + (ci + 1) * 128],
                                s_ACC[:, hh * 64:(hh + 1) * 64],
                                start=False, stop=lastf),
                                ["C64", "ACC%d" % half]))
                        b.mmg(PB[4 + half], items)
                    for hh in range(NH):
                        half, off = hh // 8, (hh % 8) * 64
                        if half == 0:
                            b.op("vector", lambda h, hh=hh, half=half, off=off, ci=ci: h.tensor_scalar_mul(
                                s_Y[:, hh * 64:(hh + 1) * 64], pb[4 + half][:, off:off + 64],
                                s_ft[:, ci * 16 + hh:ci * 16 + hh + 1]),
                                r=[PB[4 + half], "ft"], w=["sy%d" % hh])
                        else:
                            b.op("scalar", lambda h, hh=hh, half=half, off=off, ci=ci: h.activation(
                                s_Y[:, hh * 64:(hh + 1) * 64], pb[4 + half][:, off:off + 64],
                                AF.Copy, scale=s_ft[:, ci * 16 + hh:ci * 16 + hh + 1]),
                                r=[PB[4 + half], "ft"], w=["sy%d" % hh])
                    for hh in range(NH):
                        bank, pk = pb[6 + hh % 2], PB[6 + hh % 2]
                        b.op("tensor", lambda h, hh=hh, bank=bank: h.transpose(
                            bank[0:64, 0:128], s_Y[:, hh * 64:(hh + 1) * 64],
                            ident), r=["sy%d" % hh, "ident"], w=[pk])
                        dst_t = s_Yne if hh % 2 == 0 else s_Yno
                        fcq = hh // 2
                        dst = dst_t[:, fcq * 512 + ci * 128:fcq * 512 + (ci + 1) * 128]
                        if hh % 2 == 0:
                            b.op("vector", lambda h, dst=dst, bank=bank: h.tensor_copy(
                                dst, bank[0:64, 0:128]), r=[pk], w=["yn%d" % hh])
                        else:
                            b.op("scalar", lambda h, dst=dst, bank=bank: h.copy(
                                dst, bank[0:64, 0:128]), r=[pk], w=["yn%d" % hh])
                for half in range(2):
                    items = []
                    for j in range(8):
                        hh = half * 8 + j
                        off = j * 64
                        items.append((lambda h, hh=hh, off=off, half=half,
                                      first=(j == 0), lastf=(j == 7): h.matmul(
                            pb[2 + half][0:64, off:off + 64],
                            s_Bt[:, hh * 64:(hh + 1) * 64],
                            s_xt[:, hh * 64:(hh + 1) * 64],
                            start=first, stop=lastf),
                            ["Bt%d" % hh, "xt%d" % hh]))
                    b.mmg(PB[2 + half], items)
                for half in range(2):
                    b.op("vector", lambda h, half=half: h.tensor_tensor(
                        s_ACC[:, half * 512:(half + 1) * 512],
                        s_ACC[:, half * 512:(half + 1) * 512],
                        pb[2 + half][0:64, 0:512], ALU.add),
                        r=[PB[2 + half], "ACC%d" % half], w=["ACC%d" % half])
            if isY:
                b.op("scalar", lambda h, t=t: h.dma_start(
                    out=YnT4[0][:, :, (t - 8) * TILE:(t - 7) * TILE],
                    in_=s_Yne.rearrange("p (g f) -> p g f", f=512)),
                    r=["yn%d" % i2 for i2 in range(0, 16, 2)], chan=c_yn[0])
                b.op("scalar", lambda h, t=t: h.dma_start(
                    out=YnT4[1][:, :, (t - 8) * TILE:(t - 7) * TILE],
                    in_=s_Yno.rearrange("p (g f) -> p g f", f=512)),
                    r=["yn%d" % i2 for i2 in range(1, 16, 2)], chan=c_yn[1])
        b.barrier()

    if upto >= 4:
        f_xf = arena[:, 0:4096]
        f_q = arena[:, 4096:8192]
        f_sgn = arena[:, 8192:12288]
        f_scl = arena[:, 12288:12800]
        f_xqf2 = [arena[:, 12800:16896], arena[:, 16896:20992]]
        f_h = arena[:, 20992:32256]
        f_wq = arena[:, 32256:36352]
        f_ws = [arena[:, 36352:37376], arena[:, 37376:38400]]
        f_res = arena[:, 38400:42496]
        f_hscl = f_res[:, 0:352]
        f_sil = f_res[:, 352:2400]
        f_xfb = f_xf.bitcast(BF16)
        f_qb = f_q.bitcast(BF16)
        f_xqf2b = [f_xqf2[0].bitcast(BF16), f_xqf2[1].bitcast(BF16)]
        f_hb = f_h.bitcast(BF16)
        f_wqb = f_wq.bitcast(BF16)
        f_wsb = [f_ws[0].bitcast(BF16), f_ws[1].bitcast(BF16)]

        c_xf = [b.chan("xfc0"), b.chan("xfc1")]
        c_wq = [b.chan("wqc0"), b.chan("wqc1")]
        c_ynr = [b.chan("ynr0"), b.chan("ynr1")]
        c_ws = [b.chan("wsc0"), b.chan("wsc1")]
        c_out = b.chan("outc")
        c_res = b.chan("resc")

        SPANS = [(0, 512), (512, 512), (1024, 512), (1536, 512), (2048, 512), (2560, 256)]

        def emit_A(t):
            b.op("sync", lambda h, t=t: h.dma_start(
                out=f_xf.rearrange("p (s d) -> p s d", d=1024),
                in_=xf_in[t * TILE:(t + 1) * TILE, :].rearrange("(s p) d -> p s d", p=128)),
                w=["fx", "fhqh0", "fhqh1", "fhql0", "fhql1", "fhtmp"]
                  + ["fxq%d" % s2 for s2 in range(4)]
                  + ["fxqh%d" % s2 for s2 in range(4)]
                  + ["fxql%d" % s2 for s2 in range(4)], chan=c_xf[t % 2])
            for st in range(4):
                xs = f_xf[:, st * 1024:(st + 1) * 1024]
                dum = f_q[:, st * 1024:st * 1024 + 512]
                b.op("scalar", lambda h, xs=xs, dum=dum, st=st: h.activation(
                    dum, xs[:, 0:512], AF.Square, accum_out=fstat[:, st:st + 1]),
                    r=["fx"], w=["fs2a%d" % st, "fxn%d" % st, "fhs"])
                b.op("scalar", lambda h, xs=xs, dum=dum, st=st: h.activation(
                    dum, xs[:, 512:1024], AF.Square, accum_out=fstat[:, 4 + st:5 + st]),
                    r=["fx", "fs2a%d" % st], w=["fs2b%d" % st, "fxn%d" % st])
                b.op("vector", lambda h, xs=xs, st=st: h.reduce_sum(
                    fstat[:, 8 + st:9 + st], xs, AX.X), r=["fx"], w=["fs1%d" % st])
                b.op("vector", lambda h, st=st: h.tensor_single_scalar(
                    fstat[:, 8 + st:9 + st], fstat[:, 8 + st:9 + st], -1.0 / D, ALU.mult),
                    r=["fs1%d" % st], w=["fs1%d" % st])
                b.op("vector", lambda h, st=st: h.tensor_tensor(
                    fstat[:, 12 + st:13 + st], fstat[:, st:st + 1],
                    fstat[:, 4 + st:5 + st], ALU.add),
                    r=["fs2a%d" % st, "fs2b%d" % st], w=["fv%d" % st])
                b.op("vector", lambda h, st=st: h.tensor_single_scalar(
                    fstat[:, 12 + st:13 + st], fstat[:, 12 + st:13 + st], 1.0 / D, ALU.mult),
                    r=["fv%d" % st], w=["fv%d" % st])
                b.op("vector", lambda h, st=st: h.scalar_tensor_tensor(
                    fstat[:, 12 + st:13 + st], fstat[:, 8 + st:9 + st],
                    fstat[:, 8 + st:9 + st], fstat[:, 12 + st:13 + st],
                    ALU.mult, ALU.subtract),
                    r=["fs1%d" % st, "fv%d" % st], w=["fv%d" % st])
                b.op("scalar", lambda h, st=st: h.activation(
                    fstat[:, 12 + st:13 + st], fstat[:, 12 + st:13 + st], AF.Sqrt,
                    bias=cst[:, 0:1], scale=-1.0), r=["fv%d" % st], w=["fv%d" % st])
                b.op("vector", lambda h, st=st: h.reciprocal(
                    fstat[:, 12 + st:13 + st], fstat[:, 12 + st:13 + st]),
                    r=["fv%d" % st], w=["fistd%d" % st])
                b.op("vector", lambda h, xs=xs, st=st: h.tensor_scalar(
                    f_q[:, st * 1024:(st + 1) * 1024], xs,
                    fstat[:, 8 + st:9 + st], fstat[:, 12 + st:13 + st], ALU.add, ALU.mult),
                    r=["fx", "fs1%d" % st, "fistd%d" % st],
                    w=["fxn%d" % st, "fhs"])
                quant_tm(f_q[:, st * 1024:(st + 1) * 1024], "fxn%d" % st,
                         f_sgn[:, st * 1024:(st + 1) * 1024], "fsgn%d" % st,
                         f_xf[:, st * 1024:(st + 1) * 1024], "fxq%d" % st,
                         f_scl[:, st * 128:(st + 1) * 128], "fscl%d" % st, 1024,
                         eng="gpsimd" if st % 2 else "vector",
                         claim=["fx", "fhqh0", "fhqh1", "fhql0", "fhql1"],
                         sclaim=["fhqq0", "fhqq1", "fyn"])
                qe = "gpsimd" if st % 2 else "vector"
                b.op(qe, lambda h, st=st: h.tensor_copy(
                    f_qb[:, st * 2048:st * 2048 + 1024],
                    f_xf[:, st * 1024:(st + 1) * 1024]),
                    r=["fxq%d" % st, "fhs"], w=["fxqh%d" % st, "fxn%d" % st])
                b.op(qe, lambda h, st=st: h.tensor_tensor(
                    f_qb[:, st * 2048 + 1024:st * 2048 + 2048],
                    f_xf[:, st * 1024:(st + 1) * 1024],
                    f_qb[:, st * 2048:st * 2048 + 1024], ALU.subtract),
                    r=["fxq%d" % st, "fxqh%d" % st], w=["fxql%d" % st, "fxn%d" % st])

        def emit_XT(t):
            p = t % 2
            for st in range(4):
                for kc in range(8):
                    bank, pk = pb[6 + kc % 2], PB[6 + kc % 2]
                    bkb = bank.bitcast(BF16)
                    for hl in range(2):
                        xk = ("fxqh%d" % st) if hl == 0 else ("fxql%d" % st)
                        dk = ("xqf%d_%d_%d" % (kc, st, p)) if hl == 0 \
                            else ("xqfl%d_%d_%d" % (kc, st, p))
                        dof = hl * 4096
                        b.op("tensor", lambda h, st=st, kc=kc, hl=hl, bkb=bkb: h.transpose(
                            bkb[:, hl * 128:hl * 128 + 128],
                            f_qb[:, st * 2048 + hl * 1024 + kc * 128:
                                 st * 2048 + hl * 1024 + (kc + 1) * 128], identb),
                            r=[xk, "identb", "fhs"], w=[pk])
                        if kc % 2 == 0:
                            b.op("vector", lambda h, kc=kc, st=st, hl=hl, dof=dof,
                                 bkb=bkb, p=p: h.tensor_copy(
                                f_xqf2b[p][:, dof + kc * 512 + st * 128:
                                           dof + kc * 512 + (st + 1) * 128],
                                bkb[:, hl * 128:hl * 128 + 128]), r=[pk], w=[dk])
                        else:
                            b.op("scalar", lambda h, kc=kc, st=st, hl=hl, dof=dof,
                                 bkb=bkb, p=p: h.copy(
                                f_xqf2b[p][:, dof + kc * 512 + st * 128:
                                           dof + kc * 512 + (st + 1) * 128],
                                bkb[:, hl * 128:hl * 128 + 128]), r=[pk], w=[dk])

        def emit_Dpre(t):
            b.op("sync", lambda h, t=t: h.dma_start(
                out=f_sgn.rearrange("p (c f) -> p c f", f=512),
                in_=YnT[:, t * TILE:(t + 1) * TILE].rearrange("(c p) f -> p c f", p=128)),
                w=["fyn"] + ["fsgn%d" % s2 for s2 in range(4)], chan=c_ynr[t % 2])
            for di in range(8):
                ee = "vector" if di % 2 == 0 else "gpsimd"
                b.op(ee, lambda h, di=di: h.tensor_copy(
                    f_wqb[:, di * 512:(di + 1) * 512],
                    f_sgn[:, di * 512:(di + 1) * 512]),
                    r=["fyn"], w=["fynh%d" % di, "fwq"])
                b.op(ee, lambda h, di=di: h.tensor_tensor(
                    f_wqb[:, 4096 + di * 512:4096 + (di + 1) * 512],
                    f_sgn[:, di * 512:(di + 1) * 512],
                    f_wqb[:, di * 512:(di + 1) * 512], ALU.subtract),
                    r=["fyn", "fynh%d" % di], w=["fynl%d" % di, "fwq"])

        def emit_MID(t):
            p = t % 2
            for (soff, w) in SPANS:
                for wi, (wsh, wsl) in enumerate(((wgqT_h, wgqT_l), (wuqT_h, wuqT_l))):
                    b.dma_group("sync", c_wq[wi], [
                        (lambda h, wsh=wsh, soff=soff, w=w: h.dma_start(
                            out=f_wqb[:, 0:8 * w].rearrange("p (c f) -> p c f", f=w),
                            in_=wsh[:, soff:soff + w].rearrange("(c p) f -> p c f", p=128)),
                         [], ["fwq"]),
                        (lambda h, wsl=wsl, soff=soff, w=w: h.dma_start(
                            out=f_wqb[:, 4096:4096 + 8 * w].rearrange("p (c f) -> p c f", f=w),
                            in_=wsl[:, soff:soff + w].rearrange("(c p) f -> p c f", p=128)),
                         [], ["fwq"])])
                    for st in range(4):
                        items = []
                        for pi, (wof, xof) in enumerate(((0, 0), (0, 4096), (4096, 0))):
                            for kc in range(8):
                                first = (pi == 0 and kc == 0)
                                last = (pi == 2 and kc == 7)
                                xk = ("xqf%d_%d_%d" % (kc, st, p)) if xof == 0 \
                                    else ("xqfl%d_%d_%d" % (kc, st, p))
                                items.append((lambda h, st=st, kc=kc, w=w, p=p,
                                              wof=wof, xof=xof, first=first, last=last:
                                              h.matmul(
                                    pb[st][:, 0:w],
                                    f_xqf2b[p][:, xof + kc * 512 + st * 128:
                                               xof + kc * 512 + (st + 1) * 128],
                                    f_wqb[:, wof + kc * w:wof + (kc + 1) * w],
                                    start=first, stop=last), ["fwq", xk]))
                        b.mmg(PB[st], items)
                        if wi == 0:
                            b.op("scalar", lambda h, st=st, w=w: h.activation(
                                f_sil[:, st * 512:st * 512 + w], pb[st][:, 0:w], AF.Silu),
                                r=[PB[st]], w=["fsil%d" % st, "fres"])
                        else:
                            b.op("vector", lambda h, st=st, w=w, soff=soff: h.tensor_tensor(
                                f_h[:, st * 2816 + soff:st * 2816 + soff + w],
                                f_sil[:, st * 512:st * 512 + w], pb[st][:, 0:w], ALU.mult),
                                r=["fsil%d" % st, PB[st]], w=["fh%d" % st])

        def emit_C(t):
            for st in range(4):
                hwin = f_h[:, st * 2816:(st + 1) * 2816]
                for hf in range(2):
                    o = hf * 1408
                    quant_tm(hwin[:, o:o + 1408], "fh%d" % st,
                             f_q[:, o:o + 1408], "fhs",
                             f_sgn[:, o:o + 1408], "fhqq%d" % hf,
                             f_hscl[:, hf * 176:hf * 176 + 176], "fhscl%d" % hf, 1408,
                             eng="vector",
                             claim=["fyn", "fsgn0", "fsgn1", "fsgn2"],
                             sclaim=["fxn0", "fxn1", "fxn2", "fhs",
                                     "fxqh0", "fxqh1", "fxqh2",
                                     "fxql0", "fxql1", "fxql2"],
                             sclclaim=["fres"])
                    b.op("vector", lambda h, o=o, hf=hf: h.tensor_copy(
                        f_xfb[:, o:o + 1408], f_sgn[:, o:o + 1408]),
                        r=["fhqq%d" % hf],
                        w=["fhqh%d" % hf, "fhtmp", "fx", "fxq0", "fxq1", "fxq2",
                           "fxqh0", "fxqh1", "fxql0", "fxql1"])
                    b.op("vector", lambda h, o=o, hf=hf: h.tensor_tensor(
                        f_xfb[:, 2816 + o:2816 + o + 1408], f_sgn[:, o:o + 1408],
                        f_xfb[:, o:o + 1408], ALU.subtract),
                        r=["fhqq%d" % hf, "fhqh%d" % hf],
                        w=["fhql%d" % hf, "fx", "fxq1", "fxq2", "fxq3",
                           "fxqh2", "fxqh3", "fxql2", "fxql3"])
                    for kc in range(11 * hf, 11 * (hf + 1)):
                        bank, pk = pb[4 + kc % 2], PB[4 + kc % 2]
                        bkb = bank.bitcast(BF16)
                        for hl in range(2):
                            hk = ("fhqh%d" % hf) if hl == 0 else ("fhql%d" % hf)
                            dk = ("hqf%d_%d" % (kc, st)) if hl == 0 \
                                else ("hqfl%d_%d" % (kc, st))
                            b.op("tensor", lambda h, kc=kc, hl=hl, bkb=bkb, hk=hk:
                                 h.transpose(
                                bkb[:, hl * 128:hl * 128 + 128],
                                f_xfb[:, hl * 2816 + kc * 128:hl * 2816 + (kc + 1) * 128],
                                identb), r=[hk, "identb"], w=[pk])
                            dof2 = st * 5632 + hf * 2816 + hl * 1408 \
                                + (kc - 11 * hf) * 128
                            if kc % 2 == 0:
                                b.op("vector", lambda h, dof2=dof2, hl=hl, bkb=bkb:
                                     h.tensor_copy(
                                    f_hb[:, dof2:dof2 + 128],
                                    bkb[:, hl * 128:hl * 128 + 128]), r=[pk],
                                    w=[dk, "fh%d" % st])
                            else:
                                b.op("scalar", lambda h, dof2=dof2, hl=hl, bkb=bkb:
                                     h.copy(
                                    f_hb[:, dof2:dof2 + 128],
                                    bkb[:, hl * 128:hl * 128 + 128]), r=[pk],
                                    w=[dk, "fh%d" % st])

        def emit_D(t):
            b.op("gpsimd", lambda h, t=t: h.dma_start(
                out=f_res.rearrange("p (c f) -> p c f", f=512),
                in_=xfT[:, t * TILE:(t + 1) * TILE].rearrange("(c p) f -> p c f", p=128)),
                w=["fres", "fhscl0", "fhscl0i", "fhscl1", "fhscl1i"]
                  + ["fsil%d" % s2 for s2 in range(4)],
                chan=c_res)
            for dm in range(8):
                b.hazards("tensor", w=[PB[dm]])
            for di in range(8):
                wsb, wsk = f_wsb[di % 2], "fws%d" % (di % 2)
                b.dma_group("sync", c_ws[di % 2], [
                    (lambda h, di=di, wsb=wsb: h.dma_start(
                        out=wsb[:, 0:1024], in_=WoT_h[di * 128:(di + 1) * 128, :]),
                     [], [wsk]),
                    (lambda h, di=di, wsb=wsb: h.dma_start(
                        out=wsb[:, 1024:2048], in_=WoT_l[di * 128:(di + 1) * 128, :]),
                     [], [wsk])])
                for pi, (wof, yof) in enumerate(((0, 0), (0, 4096), (1024, 0))):
                    ynk = ("fynh%d" % di) if yof == 0 else ("fynl%d" % di)
                    for dm in range(8):
                        last = (dm == 7 and pi == 2)
                        b.op("tensor", lambda h, di=di, dm=dm, pi=pi, wof=wof, yof=yof,
                             wsb=wsb: h.matmul(
                            pb[dm], wsb[:, wof + dm * 128:wof + (dm + 1) * 128],
                            f_wqb[:, yof + di * 512:yof + (di + 1) * 512],
                            start=(di == 0 and pi == 0), stop=False,
                            skip_group_check=True),
                            r=([wsk, ynk, "fwq"] if last else [wsk, ynk]),
                            raw=not last)
            f_hb4 = f_hb.rearrange("p (st x j) -> p st x j", st=4, j=128)
            for kc in range(22):
                wsb, wsk = f_wsb[kc % 2], "fws%d" % (kc % 2)
                b.dma_group("sync", c_ws[kc % 2], [
                    (lambda h, kc=kc, wsb=wsb: h.dma_start(
                        out=wsb[:, 0:1024], in_=wdqT_h[kc * 128:(kc + 1) * 128, :]),
                     [], [wsk]),
                    (lambda h, kc=kc, wsb=wsb: h.dma_start(
                        out=wsb[:, 1024:2048], in_=wdqT_l[kc * 128:(kc + 1) * 128, :]),
                     [], [wsk])])
                hkeys = (["hqf%d_%d" % (kc, s2) for s2 in range(4)]
                         + ["hqfl%d_%d" % (kc, s2) for s2 in range(4)])
                fin = kc == 21
                xh = (kc // 11) * 22 + kc % 11
                xl = xh + 11
                for dm in range(8):
                    kclast = dm == 7
                    for pi, (wof, xc) in enumerate(((0, xh), (0, xl), (1024, xh))):
                        plast = pi == 2
                        b.op("tensor", lambda h, kc=kc, dm=dm, wof=wof, xc=xc, wsb=wsb,
                             fin=fin, plast=plast: h.matmul(
                            pb[dm],
                            wsb[:, wof + dm * 128:wof + (dm + 1) * 128],
                            f_hb4[:, :, xc, :],
                            start=False, stop=(fin and plast), skip_group_check=True),
                            r=[wsk] + hkeys,
                            w=([PB[dm]] if (fin and plast) else ()),
                            raw=not ((fin or kclast) and plast))

        def emit_TAIL(t):
            for dm in range(8):
                fhk = ["fh0"] if dm < 5 else (["fh0", "fh1"] if dm == 5 else ["fh1"])
                b.op("vector", lambda h, dm=dm: h.tensor_tensor(
                    f_h[:, dm * 512:(dm + 1) * 512], pb[dm],
                    f_res[:, dm * 512:(dm + 1) * 512], ALU.add),
                    r=[PB[dm], PB[7], "fres"], w=["ofm%d" % dm] + fhk)
            for st in range(4):
                bk0 = 4 + 2 * (st % 2)
                for dm in range(8):
                    bank, pk = pb[bk0 + dm // 4], PB[bk0 + dm // 4]
                    fhk2 = ["fh0"] if dm < 5 else (["fh0", "fh1"] if dm == 5 else ["fh1"])
                    b.op("tensor", lambda h, st=st, dm=dm, bank=bank: h.transpose(
                        bank[:, (dm % 4) * 128:(dm % 4) * 128 + 128],
                        f_h[:, dm * 512 + st * 128:dm * 512 + (st + 1) * 128], ident),
                        r=["ofm%d" % dm, "ident"] + fhk2, w=[pk])
                for hb in range(2):
                    bank, pk = pb[bk0 + hb], PB[bk0 + hb]
                    b.op("vector", lambda h, hb=hb, bank=bank: h.tensor_copy(
                        f_h[:, 4096 + hb * 512:4096 + (hb + 1) * 512], bank[:, 0:512]),
                        r=[pk], w=["otm%d" % hb, "fh1"])
                b.op("gpsimd", lambda h, t=t, st=st: h.dma_start(
                    out=out_h[t * TILE + st * 128:t * TILE + (st + 1) * 128, :],
                    in_=f_h[:, 4096:5120]),
                    r=["otm0", "otm1", "fh1"], chan=c_out)

        emit_A(0)
        emit_XT(0)
        for t in range(8):
            if t >= 1:
                emit_TAIL(t - 1)
            if t + 1 <= 7:
                emit_A(t + 1)
            emit_MID(t)
            emit_Dpre(t)
            if t + 1 <= 7:
                emit_XT(t + 1)
            emit_C(t)
            emit_D(t)
        emit_TAIL(7)

    b.barrier()

    with nc.Block() as block:
        b.finalize(block)
    es.close()
    return nc


_NC_CACHE = {}


def _get_nc():
    if "nc" not in _NC_CACHE:
        _NC_CACHE["nc"] = build_nc()
    return _NC_CACHE["nc"]


def make_in_maps(inputs):
    x = np.asarray(inputs["L0_attn_x"], np.float32)
    W_in = np.asarray(inputs["L0_attn_W_in"], np.float32)
    conv_w = np.asarray(inputs["L0_attn_conv_w"], np.float32)
    conv_b = np.asarray(inputs["L0_attn_conv_b"], np.float32)
    A_param = np.asarray(inputs["L0_attn_A_param"], np.float32)
    dt_bias = np.asarray(inputs["L0_attn_dt_bias"], np.float32)
    W_out = np.asarray(inputs["L0_attn_W_out"], np.float32)
    wg = np.asarray(inputs["L0_ffn_w_gate"], np.float32)
    wu = np.asarray(inputs["L0_ffn_w_up"], np.float32)
    wd = np.asarray(inputs["L0_ffn_w_down"], np.float32)

    W_inT = np.ascontiguousarray(W_in.T)
    W_outT = np.ascontiguousarray(W_out.T)
    conv_b2 = np.ascontiguousarray(conv_b[:, None])
    A2 = np.ascontiguousarray(A_param[:, None])
    dtb2 = np.ascontiguousarray(dt_bias[:, None])
    ident = np.eye(128, dtype=np.float32)
    identb = np.eye(128, dtype=ml_dtypes.bfloat16)
    um = np.triu(np.ones((128, 128), np.float32))
    onesc = np.ones((128, 1), np.float32)
    ones1 = np.ones((1, 128), np.float32)
    cstv = np.zeros((128, 2), np.float32)
    cstv[:, 0] = EPS
    cstv[:, 1] = 1.0

    x0 = x[0]
    x0T_full = np.ascontiguousarray(x0.T)
    pad0 = np.zeros((D, HALF), np.float32)
    x0T_sh0 = np.ascontiguousarray(np.concatenate([pad0, x0T_full[:, :HALF]], axis=1))
    m_h0 = np.zeros((NH, S), np.float32)
    m_h0[:, HALF:] = 1.0
    m_h1 = np.ones((NH, S), np.float32)

    in_maps = []
    for c in range(8):
        bb, hh = c // 2, c % 2
        xb = x[bb]
        xbT_full = np.ascontiguousarray(xb.T)
        if hh == 0:
            x0T_c = x0T_sh0
            xbT_c = np.ascontiguousarray(
                np.concatenate([pad0, xbT_full[:, :HALF]], axis=1))
            m_c = m_h0
        else:
            x0T_c = x0T_full
            xbT_c = xbT_full
            m_c = m_h1
        xf = np.ascontiguousarray(xb[hh * HALF:(hh + 1) * HALF])
        xfT = np.ascontiguousarray(xf.T)
        in_maps.append({
            "x0T": x0T_c, "xbT": xbT_c, "xf": xf, "xfT": xfT, "m16": m_c,
            "W_inT": W_inT, "convw": conv_w, "convb": conv_b2,
            "A_param": A2, "dt_bias": dtb2, "W_outT": W_outT,
            "wg": wg, "wu": wu, "wd": wd,
            "ident_in": ident, "identb_in": identb,
            "umask_in": um, "onesc_in": onesc, "ones1_in": ones1,
            "cst_in": cstv,
        })
    return in_maps


def kernel(**inputs):
    nc = _get_nc()
    in_maps = make_in_maps(inputs)
    res = run_bass_kernel_spmd(nc, in_maps, core_ids=list(range(8)))
    out = np.zeros((4, S, D), np.float32)
    for c in range(8):
        bb, hh = c // 2, c % 2
        out[bb, hh * HALF:(hh + 1) * HALF] = res.results[c]["out"]
    return out


# version 13
# speedup vs baseline: 1.1243x; 1.0928x over previous
import contextlib
import numpy as np
import ml_dtypes
import concourse.bass as bass
import concourse.mybir as mybir
from concourse.bass_utils import run_bass_kernel_spmd

F32 = mybir.dt.float32
BF16 = mybir.dt.bfloat16
AF = mybir.ActivationFunctionType
ALU = mybir.AluOpType
AX = mybir.AxisListType

S = 8192
D = 1024
NH = 16
HD = 64
CHK = 64
TILE = 512
NT = S // TILE
NCK = TILE // CHK
HALF = 4096
DI = 2816
EPS = 1e-5

ENGS = ("sync", "tensor", "vector", "scalar", "gpsimd")


class Bld:
    def __init__(self, nc, es):
        self.nc = nc
        self.es = es
        self.esem = {e: es.enter_context(nc.semaphore("se_" + e)) for e in ENGS}
        self.ecnt = {e: 0 for e in ENGS}
        self.ops = {e: [] for e in ENGS}
        self.waited = {e: {} for e in ENGS}
        self.wr = {}
        self.rd = {}
        self.chans = {}

    def chan(self, name):
        if name not in self.chans:
            sem = self.es.enter_context(self.nc.semaphore("ch_" + name))
            self.chans[name] = [sem, 0, "c_" + name]
        return name

    def _wait(self, eng, tok):
        nm, sem, v = tok
        if self.waited[eng].get(nm, 0) >= v:
            return
        self.waited[eng][nm] = v
        self.ops[eng].append(("w", sem, v))

    def hazards(self, eng, r=(), w=()):
        for k in r:
            t = self.wr.get(k)
            if t is not None:
                self._wait(eng, t)
        for k in w:
            t = self.wr.get(k)
            if t is not None:
                self._wait(eng, t)
            for t2 in self.rd.get(k, ()):
                self._wait(eng, t2)

    def op(self, eng, fn, r=(), w=(), chan=None, raw=False, chanwait=True):
        if chan is not None:
            c = self.chans[chan]
            if chanwait and c[1] > 0:
                self._wait(eng, (c[2], c[0], c[1]))
        self.hazards(eng, r, w)
        if raw:
            self.ops[eng].append(("r", fn))
            return None
        if chan is None:
            self.ecnt[eng] += 1
            sem, v, nm, isd = self.esem[eng], self.ecnt[eng], "e_" + eng, False
        else:
            c = self.chans[chan]
            c[1] += 16
            sem, v, nm, isd = c[0], c[1], c[2], True
        tok = (nm, sem, v)
        self.ops[eng].append(("i", fn, sem, isd))
        for k in r:
            self.rd.setdefault(k, []).append(tok)
        for k in w:
            self.wr[k] = tok
            self.rd[k] = []
        return tok

    def settok(self, tok, r=(), w=()):
        for k in r:
            self.rd.setdefault(k, []).append(tok)
        for k in w:
            self.wr[k] = tok
            self.rd[k] = []

    def dma_group(self, eng, chan, items):
        for fn, r, w in items:
            self.hazards(eng, r, w)
        last = None
        for i, (fn, r, w) in enumerate(items):
            last = self.op(eng, fn, chan=chan, chanwait=(i == 0))
        allr = [k for _, r, _ in items for k in r]
        allw = [k for _, _, w in items for k in w]
        self.settok(last, r=allr, w=allw)

    def mmg(self, bank_key, items):
        self.hazards("tensor", w=[bank_key])
        n = len(items)
        union = []
        for _, rk in items:
            for k in rk:
                if k not in union:
                    union.append(k)
        for i, (fn, rk) in enumerate(items):
            if i < n - 1:
                self.op("tensor", fn, r=rk, raw=True)
            else:
                tok = self.op("tensor", fn, r=rk, w=[bank_key])
                extra = [k for k in union if k not in rk]
                self.settok(tok, r=extra)

    def barrier(self):
        toks = [(nm, sem, cnt) for (sem, cnt, nm) in self.chans.values()]
        toks += [("e_" + e, self.esem[e], self.ecnt[e]) for e in ENGS]
        for e in ENGS:
            for tk in toks:
                if tk[0] != "e_" + e and tk[2] > 0:
                    self._wait(e, tk)
        self.wr = {}
        self.rd = {}

    def finalize(self, block):
        for e in ENGS:
            ops = self.ops[e]

            def make(ops):
                def body(h):
                    for it in ops:
                        if it[0] == "w":
                            h.wait_ge(it[1], it[2])
                        elif it[0] == "r":
                            it[1](h)
                        else:
                            it[1](h).then_inc(it[2], 16 if it[3] else 1)
                return body

            getattr(block, e)(make(ops))


def build_nc(upto=4, s3mode=9):
    nc = bass.Bass("TRN2", target_bir_lowering=False)
    es = contextlib.ExitStack()

    def din(name, shape):
        return nc.dram_tensor(name, list(shape), F32, kind="ExternalInput").ap()

    def dsc(name, shape):
        return nc.dram_tensor(name, list(shape), F32, kind="Internal").ap()

    x0T = din("x0T", (D, S))
    xbT = din("xbT", (D, S))
    xf_in = din("xf", (HALF, D))
    xfT = din("xfT", (D, HALF))
    m16 = din("m16", (NH, S))
    W_inT = din("W_inT", (D, 3088))
    convw = din("convw", (3072, 4))
    convb = din("convb", (3072, 1))
    A_param = din("A_param", (NH, 1))
    dt_bias = din("dt_bias", (NH, 1))
    W_outT = din("W_outT", (D, D))
    wg_in = din("wg", (DI, D))
    wu_in = din("wu", (DI, D))
    wd_in = din("wd", (D, DI))
    ident_in = din("ident_in", (128, 128))
    identb_in = nc.dram_tensor("identb_in", [128, 128], BF16, kind="ExternalInput").ap()
    umask_in = din("umask_in", (128, 128))
    onesc_in = din("onesc_in", (128, 1))
    ones1_in = din("ones1_in", (1, 128))
    cst_in = din("cst_in", (128, 2))
    out_h = nc.dram_tensor("out", [HALF, D], F32, kind="ExternalOutput").ap()

    def dscb(name, shape):
        return nc.dram_tensor(name, list(shape), BF16, kind="Internal").ap()

    cvB = dsc("cvB", (D, S))
    cvX = dsc("cvX", (D, S))
    cvC = dsc("cvC", (D, HALF))
    wgqT = dsc("wgqT", (D, DI))
    wuqT = dsc("wuqT", (D, DI))
    wdqT = dsc("wdqT", (DI, D))
    rows_em_d = dsc("rows_em", (NH, S))
    rows_ep_d = dsc("rows_ep", (NH, S))
    rows_nv_d = dsc("rows_nv", (NH, S))
    YnT = dsc("YnT", (D, HALF))

    AR = 42496
    arena = es.enter_context(nc.sbuf_tensor("arena", [128, AR], F32)).ap()
    ident = es.enter_context(nc.sbuf_tensor("ident", [128, 128], F32)).ap()
    identb = es.enter_context(nc.sbuf_tensor("identb", [128, 128], BF16)).ap()
    umask = es.enter_context(nc.sbuf_tensor("umask", [128, 128], F32)).ap()
    onesc = es.enter_context(nc.sbuf_tensor("onesc", [128, 1], F32)).ap()
    ones1 = es.enter_context(nc.sbuf_tensor("ones1", [1, 128], F32)).ap()
    apar = es.enter_context(nc.sbuf_tensor("apar", [NH, 1], F32)).ap()
    dtb = es.enter_context(nc.sbuf_tensor("dtb", [NH, 1], F32)).ap()
    cwb = es.enter_context(nc.sbuf_tensor("cwb", [128, 5 * 24], F32)).ap()
    swr = es.enter_context(nc.sbuf_tensor("swr", [NH, 1], F32)).ap()
    gtab = es.enter_context(nc.sbuf_tensor("gtab", [NH, 160], F32)).ap()
    carry = es.enter_context(nc.sbuf_tensor("carry", [NH, 8], F32)).ap()
    mrow = es.enter_context(nc.sbuf_tensor("mrow", [NH, TILE], F32)).ap()
    srow = es.enter_context(nc.sbuf_tensor("srow", [1, 8 * TILE], F32)).ap()
    rowsb = es.enter_context(nc.sbuf_tensor("rowsb", [NH, 8 * TILE], F32)).ap()
    fstat = es.enter_context(nc.sbuf_tensor("fstat", [128, 16], F32)).ap()
    cst = es.enter_context(nc.sbuf_tensor("cst", [128, 2], F32)).ap()
    wdt_s = es.enter_context(nc.sbuf_tensor("wdt_sb", [128, 128], F32)).ap()

    pb = [es.enter_context(nc.psum_tensor("pbank%d" % i, [128, 512], F32)).ap()
          for i in range(8)]
    PB = ["pb%d" % i for i in range(8)]

    b = Bld(nc, es)

    R_dt, R_A, R_cum, R_em, R_ep, R_S, R_t0, R_t1 = range(8)

    def rslot(i):
        return rowsb[0:16, i * TILE:(i + 1) * TILE]

    def sslot(i):
        return srow[0:1, i * TILE:(i + 1) * TILE]

    ldc = b.chan("const")
    citems = [
        (lambda h: h.dma_start(out=ident, in_=ident_in), (), ["ident"]),
        (lambda h: h.dma_start(out=identb, in_=identb_in), (), ["identb"]),
        (lambda h: h.dma_start(out=umask, in_=umask_in), (), ["umask"]),
        (lambda h: h.dma_start(out=onesc, in_=onesc_in), (), ["onesc"]),
        (lambda h: h.dma_start(out=ones1, in_=ones1_in), (), ["ones1"]),
        (lambda h: h.dma_start(out=apar, in_=A_param), (), ["apar"]),
        (lambda h: h.dma_start(out=dtb, in_=dt_bias), (), ["dtb"]),
        (lambda h: h.dma_start(out=cst, in_=cst_in), (), ["cst"]),
    ]
    for fc in range(24):
        citems.append((lambda h, fc=fc: h.dma_start(
            out=cwb[:, 5 * fc:5 * fc + 4], in_=convw[fc * 128:(fc + 1) * 128, :]),
            (), ["cwb"]))
        citems.append((lambda h, fc=fc: h.dma_start(
            out=cwb[:, 5 * fc + 4:5 * fc + 5], in_=convb[fc * 128:(fc + 1) * 128, :]),
            (), ["cwb"]))
    b.dma_group("sync", ldc, citems)

    wQ = arena[:, 26696:29512]
    wS = arena[:, 29512:32328]
    wq1 = arena[:, 32328:35144]
    wsc = arena[:, 35144:35528]
    wst = arena[:, 35528:38344]

    c_wdt = b.chan("wdtl")
    c_w = b.chan("s0w")
    c_st = b.chan("s0st")

    b.dma_group("sync", c_wdt, [
        (lambda h, kc=kc: h.dma_start(
            out=wdt_s[:, kc * 16:(kc + 1) * 16],
            in_=W_inT[kc * 128:(kc + 1) * 128, 3072:3088]), (), ["wdt_s"])
        for kc in range(8)])
    b.mmg(PB[0], [
        (lambda h, kc=kc: h.matmul(pb[0][0:1, 0:16], onesc,
                                   wdt_s[:, kc * 16:(kc + 1) * 16],
                                   start=(kc == 0), stop=(kc == 7)),
         ["wdt_s", "onesc"]) for kc in range(8)])
    b.op("vector", lambda h: h.tensor_copy(gtab[0:1, 128:144], pb[0][0:1, 0:16]),
         r=[PB[0]], w=["swrow"])
    b.op("tensor", lambda h: h.transpose(pb[1][0:16, 0:1], gtab[0:1, 128:144], ident[0:1, 0:1]),
         r=["swrow", "ident"], w=[PB[1]])
    b.op("vector", lambda h: h.tensor_copy(swr, pb[1][0:16, 0:1]), r=[PB[1]], w=["swr"])

    def ladder(v_ap, q_ap, vkey, qkey, eng, claim=(), tmp_ap=None, tmpkey=None,
               tmpclaim=()):
        b.op(eng, lambda h: h.tensor_single_scalar(q_ap, v_ap, 0.25, ALU.is_gt),
             r=[vkey], w=[qkey] + list(claim))
        if eng == "vector":
            for c in (0.75, 1.25, 1.75):
                b.op(eng, lambda h, c=c: h.scalar_tensor_tensor(
                    q_ap, v_ap, c, q_ap, ALU.is_gt, ALU.add), r=[vkey, qkey], w=[qkey])
            b.op(eng, lambda h: h.tensor_single_scalar(q_ap, q_ap, 0.5, ALU.mult),
                 r=[qkey], w=[qkey])
            for c in (2.5, 3.5, 5.0, 5.0):
                b.op(eng, lambda h, c=c: h.scalar_tensor_tensor(
                    q_ap, v_ap, c, q_ap, ALU.is_gt, ALU.add), r=[vkey, qkey], w=[qkey])
        else:
            first = True
            for c in (0.75, 1.25, 1.75):
                b.op(eng, lambda h, c=c: h.tensor_single_scalar(
                    tmp_ap, v_ap, c, ALU.is_gt),
                    r=[vkey], w=[tmpkey] + (list(tmpclaim) if first else []))
                first = False
                b.op(eng, lambda h: h.tensor_tensor(q_ap, q_ap, tmp_ap, ALU.add),
                     r=[qkey, tmpkey], w=[qkey])
            b.op(eng, lambda h: h.tensor_single_scalar(q_ap, q_ap, 0.5, ALU.mult),
                 r=[qkey], w=[qkey])
            for c in (2.5, 3.5, 5.0, 5.0):
                b.op(eng, lambda h, c=c: h.tensor_single_scalar(
                    tmp_ap, v_ap, c, ALU.is_gt), r=[vkey], w=[tmpkey])
                b.op(eng, lambda h: h.tensor_tensor(q_ap, q_ap, tmp_ap, ALU.add),
                     r=[qkey, tmpkey], w=[qkey])

    def quant_tm(x_ap, xkey, sgn_ap, skey, q_ap, qkey, scl_ap, sclkey, n, eng="vector",
                 claim=(), sclaim=(), sclclaim=(), tmp_ap=None, tmpkey=None,
                 tmpclaim=()):
        nb = n // 16
        if eng == "gpsimd" and tmp_ap is None:
            eng = "vector"
        b.op("vector", lambda h: h.tensor_reduce(
            scl_ap[:, 0:nb], x_ap.rearrange("p (bb k) -> p bb k", k=16),
            AX.X, ALU.max, apply_absolute_value=True),
            r=[xkey], w=[sclkey] + list(sclclaim))
        b.op(eng, lambda h: h.tensor_scalar(
            scl_ap[:, 0:nb], scl_ap[:, 0:nb], 1.0 / 6.0, 1e-12, ALU.mult, ALU.max),
            r=[sclkey], w=[sclkey])
        b.op("vector", lambda h: h.reciprocal(scl_ap[:, nb:2 * nb], scl_ap[:, 0:nb]),
             r=[sclkey], w=[sclkey + "i"])
        b.op("scalar", lambda h: h.activation(sgn_ap, x_ap, AF.Sign),
             r=[xkey], w=[skey] + list(sclaim))
        b.op("scalar", lambda h: h.activation(x_ap, x_ap, AF.Abs), r=[xkey, skey], w=[xkey])
        b.op(eng, lambda h: h.tensor_tensor(
            x_ap.rearrange("p (bb k) -> p bb k", k=16),
            x_ap.rearrange("p (bb k) -> p bb k", k=16),
            scl_ap[:, nb:2 * nb].unsqueeze(2).to_broadcast([128, nb, 16]), ALU.mult),
            r=[xkey, sclkey + "i"], w=[xkey])
        ladder(x_ap, q_ap, xkey, qkey, eng, claim=claim,
               tmp_ap=tmp_ap, tmpkey=tmpkey, tmpclaim=tmpclaim)
        b.op(eng, lambda h: h.tensor_tensor(q_ap, q_ap, sgn_ap, ALU.mult),
             r=[qkey, skey], w=[qkey])
        b.op(eng, lambda h: h.tensor_tensor(
            q_ap.rearrange("p (bb k) -> p bb k", k=16),
            q_ap.rearrange("p (bb k) -> p bb k", k=16),
            scl_ap[:, 0:nb].unsqueeze(2).to_broadcast([128, nb, 16]), ALU.mult),
            r=[qkey, sclkey], w=[qkey])

    def s0_weight(w_in_ap, wT_out_ap, nrows, ncols):
        nblk = ncols // 128
        for rc in range(nrows // 128):
            yield
            b.op("sync", lambda h, rc=rc: h.dma_start(
                out=wQ[:, 0:ncols], in_=w_in_ap[rc * 128:(rc + 1) * 128, :]),
                w=["wQ"], chan=c_w)
            quant_tm(wQ[:, 0:ncols], "wQ", wS[:, 0:ncols], "wS",
                     wq1[:, 0:ncols], "wq1", wsc, "wsc", ncols)
            yield
            for tb in range(nblk):
                bank, pk = pb[tb % 2], PB[tb % 2]
                b.op("tensor", lambda h, tb=tb, bank=bank: h.transpose(
                    bank[:, 0:128], wq1[:, tb * 128:(tb + 1) * 128], ident),
                    r=["wq1", "ident"], w=[pk])
                if tb % 2 == 0:
                    b.op("vector", lambda h, tb=tb, bank=bank: h.tensor_copy(
                        wst[:, tb * 128:(tb + 1) * 128], bank[:, 0:128]),
                        r=[pk], w=["wst%d" % tb])
                else:
                    b.op("scalar", lambda h, tb=tb, bank=bank: h.copy(
                        wst[:, tb * 128:(tb + 1) * 128], bank[:, 0:128]),
                        r=[pk], w=["wst%d" % tb])
            for tb in range(nblk):
                b.op("scalar", lambda h, tb=tb, rc=rc, dst_d=wT_out_ap: h.dma_start(
                    out=dst_d[tb * 128:(tb + 1) * 128, rc * 128:(rc + 1) * 128],
                    in_=wst[:, tb * 128:(tb + 1) * 128]),
                    r=["wst%d" % tb], chan=c_st)

    def s0_steps():
        yield from s0_weight(wg_in, wgqT, DI, D)
        yield from s0_weight(wu_in, wuqT, DI, D)
        yield from s0_weight(wd_in, wdqT, D, DI)
    s0_gen = s0_steps()

    a_x = arena[:, 0:4096]
    a_sq = arena[:, 4096:4608]
    a_cv = arena[:, 4608:4608 + 16 * 515]
    a_cvC = arena[:, 12848:12848 + 8 * 515]
    a_cd = [arena[:, 16968:17480], arena[:, 17480:17992]]
    a_wb = [arena[:, 17992:19016], arena[:, 19016:20040]]
    a_xn = arena[:, 20040:24136]
    a_bc = arena[:, 24136:25160]
    a_ct = arena[:, 25160:26696]

    c_x = [b.chan("xs0"), b.chan("xs1")]
    c_wp = [b.chan("wp0"), b.chan("wp1")]
    c_m = b.chan("mrowc")
    c_sp = [b.chan("sp0"), b.chan("sp1")]
    c_rw = b.chan("rowsw")

    def stats_and_dt(key):
        b.mmg(PB[6], [
            (lambda h, fc=fc: h.matmul(pb[6][0:1, 0:512], onesc,
                                       a_x[:, fc * 512:(fc + 1) * 512],
                                       start=(fc == 0), stop=(fc == 7)),
             [key + "x", "onesc"]) for fc in range(8)])
        b.hazards("tensor", w=[PB[7]])
        for fc in range(8):
            b.op("scalar", lambda h, fc=fc: h.activation(
                a_sq[:, 0:512], a_x[:, fc * 512:(fc + 1) * 512], AF.Square),
                r=[key + "x"], w=["sqs"])
            b.op("tensor", lambda h, fc=fc: h.matmul(
                pb[7][0:1, 0:512], onesc, a_sq[:, 0:512],
                start=(fc == 0), stop=(fc == 7)),
                r=["sqs", "onesc"], w=([PB[7]] if fc == 7 else ()))
        b.op("vector", lambda h: h.tensor_single_scalar(
            sslot(0), pb[6][0:1, 0:512], 1.0 / D, ALU.mult), r=[PB[6]], w=["S0k"])
        b.op("scalar", lambda h: h.activation(sslot(1), sslot(0), AF.Square),
             r=["S0k"], w=["S1k"])
        b.op("vector", lambda h: h.scalar_tensor_tensor(
            sslot(1), pb[7][0:1, 0:512], 1.0 / D, sslot(1), ALU.mult, ALU.subtract),
            r=[PB[7], "S1k"], w=["S1k"])
        b.op("scalar", lambda h: h.activation(sslot(1), sslot(1), AF.Sqrt, bias=cst[0:1, 0:1]),
             r=["S1k"], w=["S1k"])
        b.op("vector", lambda h: h.reciprocal(sslot(2), sslot(1)), r=["S1k"], w=["S2k"])
        b.op("vector", lambda h: h.tensor_tensor(sslot(3), sslot(0), sslot(2), ALU.mult),
             r=["S0k", "S2k"], w=["S3k"])
        b.mmg(PB[5], [
            (lambda h, kc=kc: h.matmul(pb[5][0:16, 0:512], wdt_s[:, kc * 16:(kc + 1) * 16],
                                       a_x[:, kc * 512:(kc + 1) * 512],
                                       start=(kc == 0), stop=(kc == 7)),
             ["wdt_s", key + "x"]) for kc in range(8)])
        b.op("tensor", lambda h: h.matmul(pb[4][0:16, 0:512], ones1[0:1, 0:16], sslot(2),
                                          start=True, stop=True),
             r=["S2k", "ones1"], w=[PB[4]])
        b.op("scalar", lambda h: h.copy(rslot(R_t1), pb[5][0:16, 0:512]),
             r=[PB[5]], w=["Rt1"])
        b.op("vector", lambda h: h.tensor_tensor(
            rslot(R_t0), rslot(R_t1), pb[4][0:16, 0:512], ALU.mult),
            r=["Rt1", PB[4]], w=["Rt0"])
        b.op("tensor", lambda h: h.matmul(pb[4][0:16, 0:512], ones1[0:1, 0:16], sslot(3),
                                          start=True, stop=True),
             r=["S3k", "ones1"], w=[PB[4]])
        b.op("vector", lambda h: h.scalar_tensor_tensor(
            rslot(R_dt), pb[4][0:16, 0:512], swr[:, 0:1], rslot(R_t0), ALU.mult, ALU.subtract),
            r=[PB[4], "swr", "Rt0"], w=["Rdt"])
        b.op("scalar", lambda h: h.activation(rslot(R_t1), rslot(R_dt), AF.Exp,
                                              bias=dtb[:, 0:1], scale=-1.0),
             r=["Rdt", "dtb"], w=["Rt1"])
        b.op("scalar", lambda h: h.activation(rslot(R_dt), rslot(R_t1), AF.Ln, bias=cst[0:16, 1:2]),
             r=["Rt1"], w=["Rdt"])
        b.op("vector", lambda h: h.tensor_scalar_mul(rslot(R_A), rslot(R_dt), apar[:, 0:1]),
             r=["Rdt", "apar"], w=["RA"])
        b.op("vector", lambda h: h.tensor_tensor(rslot(R_A), rslot(R_A), mrow, ALU.mult),
             r=["RA", "mrowk"], w=["RA"])

    if upto >= 1:
        b.op("vector", lambda h: h.memset(carry[:, 2:3], 0.0), w=["c_D"])
        b.op("vector", lambda h: h.memset(carry[:, 3:4], 1.0), w=["c_u"])
        for t in range(NT):
            b.op("sync", lambda h, t=t: h.dma_start(
                out=a_x.rearrange("p (c f) -> p c f", f=512),
                in_=xbT[:, t * TILE:(t + 1) * TILE].rearrange("(c p) f -> p c f", p=128)),
                w=["s1x"], chan=c_x[t % 2])
            b.op("sync", lambda h, t=t: h.dma_start(
                out=mrow, in_=m16[:, t * TILE:(t + 1) * TILE]), w=["mrowk"], chan=c_m)
            stats_and_dt("s1")
            b.op("scalar", lambda h: h.activation(rslot(R_em), rslot(R_A), AF.Exp),
                 r=["RA"], w=["Rem"])
            b.op("vector", lambda h: h.memset(rslot(R_t1), 1.0), w=["Rt1"])
            b.op("vector", lambda h: h.tensor_scalar_mul(rslot(R_t0), rslot(R_t1), carry[:, 3:4]),
                 r=["Rt1", "c_u"], w=["Rt0"])
            b.op("vector", lambda h: h.tensor_tensor(rslot(R_t0), rslot(R_t0), mrow, ALU.mult),
                 r=["Rt0", "mrowk"], w=["Rt0"])
            b.op("vector", lambda h: h.tensor_tensor_scan(
                rslot(R_S), rslot(R_em), rslot(R_t0), carry[:, 2:3], ALU.mult, ALU.add),
                r=["Rem", "Rt0", "c_D"], w=["RS"])
            b.op("vector", lambda h: h.tensor_single_scalar(
                rslot(R_t1), rslot(R_S), 1e-12, ALU.max), r=["RS"], w=["Rt1"])
            b.op("vector", lambda h: h.reciprocal(rslot(R_dt), rslot(R_t1)),
                 r=["Rt1"], w=["Rdt"])
            b.op("vector", lambda h: h.tensor_scalar_mul(rslot(R_dt), rslot(R_dt), carry[:, 3:4]),
                 r=["Rdt", "c_u"], w=["Rdt"])
            b.op("scalar", lambda h, t=t: h.dma_start(
                out=rows_nv_d[:, t * TILE:(t + 1) * TILE], in_=rslot(R_dt)),
                r=["Rdt"], chan=c_rw)
            b.op("vector", lambda h: h.tensor_copy(carry[:, 4:5], rslot(R_S)[:, TILE - 1:TILE]),
                 r=["RS"], w=["c_t"])
            b.op("vector", lambda h: h.tensor_single_scalar(
                carry[:, 5:6], carry[:, 4:5], 1e-12, ALU.max), r=["c_t"], w=["c_r"])
            b.op("vector", lambda h: h.reciprocal(carry[:, 5:6], carry[:, 5:6]),
                 r=["c_r"], w=["c_r"])
            b.op("vector", lambda h: h.reduce_max(carry[:, 6:7], mrow, AX.X),
                 r=["mrowk"], w=["c_m1"])
            b.op("vector", lambda h: h.tensor_tensor(carry[:, 5:6], carry[:, 5:6],
                                                     carry[:, 6:7], ALU.mult),
                 r=["c_r", "c_m1"], w=["c_r"])
            b.op("vector", lambda h: h.tensor_scalar(carry[:, 6:7], carry[:, 6:7],
                                                     -1.0, 1.0, ALU.mult, ALU.add),
                 r=["c_m1"], w=["c_m1"])
            b.op("vector", lambda h: h.tensor_tensor(carry[:, 5:6], carry[:, 5:6],
                                                     carry[:, 6:7], ALU.add),
                 r=["c_r", "c_m1"], w=["c_r"])
            b.op("vector", lambda h: h.tensor_tensor(carry[:, 2:3], carry[:, 4:5],
                                                     carry[:, 5:6], ALU.mult),
                 r=["c_t", "c_r"], w=["c_D"])
            b.op("vector", lambda h: h.tensor_tensor(carry[:, 3:4], carry[:, 3:4],
                                                     carry[:, 5:6], ALU.mult),
                 r=["c_u", "c_r"], w=["c_u"])
        b.barrier()

    def proj_conv(spill_col, ofs, winf, wkeyf, cwcf, spill):
        for j, o in enumerate(ofs):
            wb, wk = a_wb[j % 2], "wb%d" % (j % 2)
            b.op("sync", lambda h, o=o, wb=wb: h.dma_start(
                out=wb.rearrange("p (c f) -> p c f", f=128),
                in_=W_inT[:, o * 128:(o + 1) * 128].rearrange("(c p) f -> p c f", p=128)),
                w=[wk], chan=c_wp[j % 2])
            bank, pk = pb[j % 2], PB[j % 2]
            b.mmg(pk, [
                (lambda h, kc=kc, wb=wb, bank=bank: h.matmul(
                    bank, wb[:, kc * 128:(kc + 1) * 128],
                    a_xn[:, kc * 512:(kc + 1) * 512],
                    start=(kc == 0), stop=(kc == 7)),
                 [wk, "xn%d" % kc]) for kc in range(8)])
            win, wkey = winf(j), wkeyf(j)
            if j % 2 == 0:
                b.op("vector", lambda h, win=win, bank=bank: h.tensor_copy(win[:, 3:515], bank),
                     r=[pk], w=[wkey])
            else:
                b.op("scalar", lambda h, win=win, bank=bank: h.copy(win[:, 3:515], bank),
                     r=[pk], w=[wkey])
            cd, cdk = a_cd[j % 2], "cd%d" % (j % 2)
            cc = cwcf(j)
            b.op("scalar", lambda h, win=win, cd=cd, cc=cc: h.activation(
                cd, win[:, 0:512], AF.Copy, scale=cwb[:, 5 * cc:5 * cc + 1]),
                r=[wkey, "cwb"], w=[cdk])
            for tap in range(1, 4):
                b.op("scalar", lambda h, win=win, cc=cc, tap=tap: h.activation(
                    a_ct[:, (tap - 1) * 512:tap * 512], win[:, tap:tap + 512], AF.Copy,
                    scale=cwb[:, 5 * cc + tap:5 * cc + tap + 1]),
                    r=[wkey, "cwb"], w=["ctmp%d" % tap])
            b.op("vector", lambda h, cd=cd, cc=cc: h.scalar_tensor_tensor(
                a_ct[:, 0:512], a_ct[:, 0:512], cwb[:, 5 * cc + 4:5 * cc + 5],
                a_ct[:, 512:1024], ALU.add, ALU.add),
                r=["ctmp1", "ctmp2", "cwb"], w=["ctmp1"])
            b.op("vector", lambda h, cd=cd: h.tensor_tensor(
                cd, cd, a_ct[:, 1024:1536], ALU.add), r=[cdk, "ctmp3"], w=[cdk])
            b.op("vector", lambda h, cd=cd: h.tensor_tensor(
                cd, cd, a_ct[:, 0:512], ALU.add), r=[cdk, "ctmp1"], w=[cdk])
            b.op("gpsimd", lambda h, win=win: h.tensor_copy(win[:, 0:3], win[:, 512:515]),
                 r=[wkey], w=[wkey])
            b.op("scalar", lambda h, cd=cd, j=j, spill=spill, spill_col=spill_col: h.dma_start(
                out=spill[j * 128:(j + 1) * 128, spill_col * TILE:(spill_col + 1) * TILE],
                in_=cd), r=[cdk], chan=c_sp[j % 2])

    if upto >= 2:
        b.op("vector", lambda h: h.memset(carry[:, 0:1], 0.0), w=["c_cum"])
        for fc in range(16):
            b.op("gpsimd", lambda h, fc=fc: h.memset(a_cv[:, fc * 515:fc * 515 + 3], 0.0),
                 w=["cw%d" % fc])
        for fc in range(8):
            b.op("gpsimd", lambda h, fc=fc: h.memset(a_cvC[:, fc * 515:fc * 515 + 3], 0.0),
                 w=["cwC%d" % fc])
        for t in range(NT):
            b.op("sync", lambda h, t=t: h.dma_start(
                out=a_x.rearrange("p (c f) -> p c f", f=512),
                in_=x0T[:, t * TILE:(t + 1) * TILE].rearrange("(c p) f -> p c f", p=128)),
                w=["s2x"], chan=c_x[t % 2])
            b.op("sync", lambda h, t=t: h.dma_start(
                out=mrow, in_=m16[:, t * TILE:(t + 1) * TILE]), w=["mrowk"], chan=c_m)
            stats_and_dt("s2")
            b.op("tensor", lambda h: h.matmul(pb[2], ones1, sslot(0), start=True, stop=True),
                 r=["S0k", "ones1"], w=[PB[2]])
            b.op("tensor", lambda h: h.matmul(pb[3], ones1, sslot(2), start=True, stop=True),
                 r=["S2k", "ones1"], w=[PB[3]])
            b.op("scalar", lambda h: h.copy(a_bc[:, 0:512], pb[2]), r=[PB[2]], w=["bcmu"])
            b.op("scalar", lambda h: h.copy(a_bc[:, 512:1024], pb[3]), r=[PB[3]], w=["bcis"])
            for fc in range(8):
                ee = "vector" if fc % 2 == 0 else "gpsimd"
                b.op(ee, lambda h, fc=fc: h.tensor_tensor(
                    a_xn[:, fc * 512:(fc + 1) * 512], a_x[:, fc * 512:(fc + 1) * 512],
                    a_bc[:, 0:512], ALU.subtract), r=["s2x", "bcmu"], w=["xn%d" % fc])
                b.op(ee, lambda h, fc=fc: h.tensor_tensor(
                    a_xn[:, fc * 512:(fc + 1) * 512], a_xn[:, fc * 512:(fc + 1) * 512],
                    a_bc[:, 512:1024], ALU.mult), r=["xn%d" % fc, "bcis"], w=["xn%d" % fc])
            proj_conv(t, list(range(8, 16)),
                      lambda j: a_cv[:, j * 515:(j + 1) * 515], lambda j: "cw%d" % j,
                      lambda j: 8 + j, cvB)
            for _ in range(3):
                next(s0_gen, None)
            proj_conv(t, list(range(16, 24)),
                      lambda j: a_cv[:, (8 + j) * 515:(9 + j) * 515],
                      lambda j: "cw%d" % (8 + j), lambda j: 16 + j, cvX)
            if t == 7:
                for j, o in enumerate(range(0, 8)):
                    wb, wk = a_wb[j % 2], "wb%d" % (j % 2)
                    b.op("sync", lambda h, o=o, wb=wb: h.dma_start(
                        out=wb.rearrange("p (c f) -> p c f", f=128),
                        in_=W_inT[:, o * 128:(o + 1) * 128].rearrange(
                            "(c p) f -> p c f", p=128)),
                        w=[wk], chan=c_wp[j % 2])
                    bank, pk = pb[j % 2], PB[j % 2]
                    b.mmg(pk, [
                        (lambda h, kc=kc, wb=wb, bank=bank: h.matmul(
                            bank[:, 0:3], wb[:, kc * 128:(kc + 1) * 128],
                            a_xn[:, kc * 512 + 509:kc * 512 + 512],
                            start=(kc == 0), stop=(kc == 7)),
                         [wk, "xn%d" % kc]) for kc in range(8)])
                    b.op("vector", lambda h, j=j, bank=bank: h.tensor_copy(
                        a_cvC[:, j * 515:j * 515 + 3], bank[:, 0:3]),
                        r=[pk], w=["cwC%d" % j])
            for _ in range(3):
                next(s0_gen, None)
            if t >= 8:
                proj_conv(t - 8, list(range(0, 8)),
                          lambda j: a_cvC[:, j * 515:(j + 1) * 515],
                          lambda j: "cwC%d" % j, lambda j: j, cvC)
            b.op("vector", lambda h: h.tensor_tensor_scan(
                rslot(R_cum), rslot(R_A), rslot(R_A), carry[:, 0:1], ALU.add, ALU.bypass),
                r=["RA", "c_cum"], w=["Rcum"])
            b.op("vector", lambda h: h.tensor_copy(carry[:, 0:1], rslot(R_cum)[:, TILE - 1:TILE]),
                 r=["Rcum"], w=["c_cum"])
            b.op("scalar", lambda h: h.activation(rslot(R_em), rslot(R_cum), AF.Exp, scale=-1.0),
                 r=["Rcum"], w=["Rem"])
            b.op("vector", lambda h: h.tensor_tensor(rslot(R_em), rslot(R_em), mrow, ALU.mult),
                 r=["Rem", "mrowk"], w=["Rem"])
            b.op("scalar", lambda h: h.activation(rslot(R_ep), rslot(R_cum), AF.Exp),
                 r=["Rcum"], w=["Rep"])
            b.op("scalar", lambda h, t=t: h.dma_start(
                out=rows_em_d[:, t * TILE:(t + 1) * TILE], in_=rslot(R_em)),
                r=["Rem"], chan=c_rw)
            b.op("scalar", lambda h, t=t: h.dma_start(
                out=rows_ep_d[:, t * TILE:(t + 1) * TILE], in_=rslot(R_ep)),
                r=["Rep"], chan=c_rw)
            for _ in range(2):
                next(s0_gen, None)
        for _ in s0_gen:
            pass
        b.barrier()

    if upto < 2:
        for _ in s0_gen:
            pass
        b.barrier()
    if upto >= 3:
        CH2 = 128
        NCK2 = TILE // CH2
        s_B64 = arena[0:64, 0:8192]
        s_X64 = arena[0:64, 8192:16384]
        s_C64 = arena[0:64, 16384:24576]
        s_ms = arena[:, 24576:26624]
        s_xt = arena[:, 26624:27648]
        s_Bt = arena[:, 27648:28672]
        s_ACC = arena[0:64, 28672:29696]
        s_Y = arena[:, 29696:30720]
        s_emt = arena[:, 30720:30784]
        s_ft = arena[:, 30784:30848]
        s_Yne = arena[0:64, 30848:34944]
        s_Yno = arena[0:64, 34944:39040]

        c_b64 = [b.chan("b64_0"), b.chan("b64_1")]
        c_x64 = [b.chan("x64_0"), b.chan("x64_1")]
        c_c64 = [b.chan("c64_0"), b.chan("c64_1")]
        c_rem = b.chan("rowem")
        c_rep = b.chan("rowep")
        c_rnv = b.chan("rownv")
        c_yn = [b.chan("yn0"), b.chan("yn1")]

        b.op("vector", lambda h: h.memset(s_ACC[:, 0:512], 0.0), w=["ACC0"])
        b.op("vector", lambda h: h.memset(s_ACC[:, 512:1024], 0.0), w=["ACC1"])

        YnT4 = YnT.rearrange("(g two p) f -> two p g f", two=2, p=64)

        for t in range(NT):
            isY = t >= 8
            b.op("sync", lambda h, t=t: h.dma_start(
                out=s_B64.rearrange("p (g f) -> p g f", f=512),
                in_=cvB[:, t * TILE:(t + 1) * TILE].rearrange("(g p) f -> p g f", p=64)),
                w=["B64"], chan=c_b64[t % 2])
            b.op("sync", lambda h, t=t: h.dma_start(
                out=s_X64.rearrange("p (g f) -> p g f", f=512),
                in_=cvX[:, t * TILE:(t + 1) * TILE].rearrange("(g p) f -> p g f", p=64)),
                w=["X64"], chan=c_x64[t % 2])
            if isY:
                b.op("sync", lambda h, t=t: h.dma_start(
                    out=s_C64.rearrange("p (g f) -> p g f", f=512),
                    in_=cvC[:, (t - 8) * TILE:(t - 7) * TILE].rearrange("(g p) f -> p g f", p=64)),
                    w=["C64"], chan=c_c64[t % 2])
            b.op("sync", lambda h, t=t: h.dma_start(
                out=rslot(R_em), in_=rows_em_d[:, t * TILE:(t + 1) * TILE]),
                w=["Rem"], chan=c_rem)
            if isY:
                b.op("sync", lambda h, t=t: h.dma_start(
                    out=rslot(R_ep), in_=rows_ep_d[:, t * TILE:(t + 1) * TILE]),
                    w=["Rep"], chan=c_rep)
                b.op("sync", lambda h, t=t: h.dma_start(
                    out=rslot(R_t0), in_=rows_nv_d[:, t * TILE:(t + 1) * TILE]),
                    w=["Rt0"], chan=c_rnv)
                b.op("vector", lambda h: h.tensor_tensor(
                    rslot(R_ep), rslot(R_ep), rslot(R_t0), ALU.mult),
                    r=["Rep", "Rt0"], w=["Rep"])
            for blk in range(4):
                bank, pk = pb[6 + blk % 2], PB[6 + blk % 2]
                b.op("tensor", lambda h, blk=blk, bank=bank: h.transpose(
                    bank[:, 0:16], rslot(R_em)[:, blk * 128:(blk + 1) * 128],
                    ident[0:16, 0:16]), r=["Rem", "ident"], w=[pk])
                b.op("vector", lambda h, blk=blk, bank=bank: h.tensor_copy(
                    s_emt[:, blk * 16:(blk + 1) * 16], bank[:, 0:16]),
                    r=[pk], w=["emt"])
                if isY:
                    b.op("tensor", lambda h, blk=blk, bank=bank: h.transpose(
                        bank[:, 0:16], rslot(R_ep)[:, blk * 128:(blk + 1) * 128],
                        ident[0:16, 0:16]), r=["Rep", "ident"], w=[pk])
                    b.op("vector", lambda h, blk=blk, bank=bank: h.tensor_copy(
                        s_ft[:, blk * 16:(blk + 1) * 16], bank[:, 0:16]),
                        r=[pk], w=["ft"])
            for ci in range(NCK2):
                for hh in range(NH):
                    bank, pk = pb[6 + hh % 2], PB[6 + hh % 2]
                    xsl = s_X64[:, hh * 512 + ci * 128:hh * 512 + (ci + 1) * 128]
                    b.op("tensor", lambda h, xsl=xsl, bank=bank: h.transpose(
                        bank[:, 0:64], xsl, ident[0:64, 0:64]),
                        r=["X64", "ident"], w=[pk])
                    if hh % 2 == 0:
                        b.op("vector", lambda h, hh=hh, bank=bank, ci=ci: h.tensor_scalar_mul(
                            s_xt[:, hh * 64:(hh + 1) * 64], bank[:, 0:64],
                            s_emt[:, ci * 16 + hh:ci * 16 + hh + 1]),
                            r=[pk, "emt"], w=["xt%d" % hh])
                    else:
                        b.op("scalar", lambda h, hh=hh, bank=bank, ci=ci: h.activation(
                            s_xt[:, hh * 64:(hh + 1) * 64], bank[:, 0:64], AF.Copy,
                            scale=s_emt[:, ci * 16 + hh:ci * 16 + hh + 1]),
                            r=[pk, "emt"], w=["xt%d" % hh])
                    bsl = s_B64[:, hh * 512 + ci * 128:hh * 512 + (ci + 1) * 128]
                    b.op("tensor", lambda h, bsl=bsl, bank=bank: h.transpose(
                        bank[:, 0:64], bsl, ident[0:64, 0:64]),
                        r=["B64", "ident"], w=[pk])
                    if hh % 2 == 0:
                        b.op("vector", lambda h, hh=hh, bank=bank: h.tensor_copy(
                            s_Bt[:, hh * 64:(hh + 1) * 64], bank[:, 0:64]),
                            r=[pk], w=["Bt%d" % hh])
                    else:
                        b.op("scalar", lambda h, hh=hh, bank=bank: h.copy(
                            s_Bt[:, hh * 64:(hh + 1) * 64], bank[:, 0:64]),
                            r=[pk], w=["Bt%d" % hh])
                if isY:
                    for bk in range(4):
                        items = []
                        for j in range(4):
                            hh = bk * 4 + j
                            items.append((lambda h, hh=hh, j=j, bk=bk, ci=ci,
                                          first=(j == 0), lastf=(j == 3): h.matmul(
                                pb[bk][:, j * 128:(j + 1) * 128],
                                s_B64[:, hh * 512 + ci * 128:hh * 512 + (ci + 1) * 128],
                                s_C64[:, hh * 512 + ci * 128:hh * 512 + (ci + 1) * 128],
                                start=first, stop=lastf), ["B64", "C64"]))
                        b.mmg(PB[bk], items)
                    for bk in range(4):
                        b.op("vector", lambda h, bk=bk: h.tensor_tensor(
                            s_ms[:, bk * 512:(bk + 1) * 512].rearrange("p (g f) -> p g f", f=128),
                            pb[bk].rearrange("p (g f) -> p g f", f=128),
                            umask.unsqueeze(1).to_broadcast([128, 4, 128]), ALU.mult),
                            r=[PB[bk], "umask"], w=["ms%d" % bk])
                    for half in range(2):
                        items = []
                        for j in range(8):
                            hh = half * 8 + j
                            off = j * 64
                            items.append((lambda h, hh=hh, off=off, half=half,
                                          first=(j == 0): h.matmul(
                                pb[4 + half][:, off:off + 64],
                                s_ms[:, hh * 128:(hh + 1) * 128],
                                s_xt[:, hh * 64:(hh + 1) * 64],
                                start=first, stop=False),
                                ["ms%d" % (hh // 4), "xt%d" % hh]))
                            items.append((lambda h, hh=hh, off=off, ci=ci, half=half,
                                          lastf=(j == 7): h.matmul(
                                pb[4 + half][:, off:off + 64],
                                s_C64[:, hh * 512 + ci * 128:hh * 512 + (ci + 1) * 128],
                                s_ACC[:, hh * 64:(hh + 1) * 64],
                                start=False, stop=lastf),
                                ["C64", "ACC%d" % half]))
                        b.mmg(PB[4 + half], items)
                    for hh in range(NH):
                        half, off = hh // 8, (hh % 8) * 64
                        if half == 0:
                            b.op("vector", lambda h, hh=hh, half=half, off=off, ci=ci: h.tensor_scalar_mul(
                                s_Y[:, hh * 64:(hh + 1) * 64], pb[4 + half][:, off:off + 64],
                                s_ft[:, ci * 16 + hh:ci * 16 + hh + 1]),
                                r=[PB[4 + half], "ft"], w=["sy%d" % hh])
                        else:
                            b.op("scalar", lambda h, hh=hh, half=half, off=off, ci=ci: h.activation(
                                s_Y[:, hh * 64:(hh + 1) * 64], pb[4 + half][:, off:off + 64],
                                AF.Copy, scale=s_ft[:, ci * 16 + hh:ci * 16 + hh + 1]),
                                r=[PB[4 + half], "ft"], w=["sy%d" % hh])
                    for hh in range(NH):
                        bank, pk = pb[6 + hh % 2], PB[6 + hh % 2]
                        b.op("tensor", lambda h, hh=hh, bank=bank: h.transpose(
                            bank[0:64, 0:128], s_Y[:, hh * 64:(hh + 1) * 64],
                            ident), r=["sy%d" % hh, "ident"], w=[pk])
                        dst_t = s_Yne if hh % 2 == 0 else s_Yno
                        fcq = hh // 2
                        dst = dst_t[:, fcq * 512 + ci * 128:fcq * 512 + (ci + 1) * 128]
                        if hh % 2 == 0:
                            b.op("vector", lambda h, dst=dst, bank=bank: h.tensor_copy(
                                dst, bank[0:64, 0:128]), r=[pk], w=["yn%d" % hh])
                        else:
                            b.op("scalar", lambda h, dst=dst, bank=bank: h.copy(
                                dst, bank[0:64, 0:128]), r=[pk], w=["yn%d" % hh])
                for half in range(2):
                    items = []
                    for j in range(8):
                        hh = half * 8 + j
                        off = j * 64
                        items.append((lambda h, hh=hh, off=off, half=half,
                                      first=(j == 0), lastf=(j == 7): h.matmul(
                            pb[2 + half][0:64, off:off + 64],
                            s_Bt[:, hh * 64:(hh + 1) * 64],
                            s_xt[:, hh * 64:(hh + 1) * 64],
                            start=first, stop=lastf),
                            ["Bt%d" % hh, "xt%d" % hh]))
                    b.mmg(PB[2 + half], items)
                for half in range(2):
                    b.op("vector", lambda h, half=half: h.tensor_tensor(
                        s_ACC[:, half * 512:(half + 1) * 512],
                        s_ACC[:, half * 512:(half + 1) * 512],
                        pb[2 + half][0:64, 0:512], ALU.add),
                        r=[PB[2 + half], "ACC%d" % half], w=["ACC%d" % half])
            if isY:
                b.op("scalar", lambda h, t=t: h.dma_start(
                    out=YnT4[0][:, :, (t - 8) * TILE:(t - 7) * TILE],
                    in_=s_Yne.rearrange("p (g f) -> p g f", f=512)),
                    r=["yn%d" % i2 for i2 in range(0, 16, 2)], chan=c_yn[0])
                b.op("scalar", lambda h, t=t: h.dma_start(
                    out=YnT4[1][:, :, (t - 8) * TILE:(t - 7) * TILE],
                    in_=s_Yno.rearrange("p (g f) -> p g f", f=512)),
                    r=["yn%d" % i2 for i2 in range(1, 16, 2)], chan=c_yn[1])
        b.barrier()

    if upto >= 4:
        f_xf = arena[:, 0:4096]
        f_q = arena[:, 4096:8192]
        f_sgn = arena[:, 8192:12288]
        f_scl = arena[:, 12288:12800]
        f_xqf2 = [arena[:, 12800:16896], arena[:, 16896:20992]]
        f_h = arena[:, 20992:32256]
        f_wq = arena[:, 32256:36352]
        f_ws = [arena[:, 36352:37376], arena[:, 37376:38400]]
        f_res = arena[:, 38400:42496]
        f_hscl = f_res[:, 0:352]
        f_sil = f_res[:, 352:2400]

        c_xf = [b.chan("xfc0"), b.chan("xfc1")]
        c_wq = [b.chan("wqc0"), b.chan("wqc1")]
        c_ynr = [b.chan("ynr0"), b.chan("ynr1")]
        c_ws = [b.chan("wsc0"), b.chan("wsc1")]
        c_out = b.chan("outc")
        c_res = b.chan("resc")

        SPANS = [(0, 512), (512, 512), (1024, 512), (1536, 512), (2048, 512), (2560, 256)]

        def emit_A(t):
            b.op("sync", lambda h, t=t: h.dma_start(
                out=f_xf.rearrange("p (s d) -> p s d", d=1024),
                in_=xf_in[t * TILE:(t + 1) * TILE, :].rearrange("(s p) d -> p s d", p=128)),
                w=["fx"] + ["fxq%d" % s2 for s2 in range(4)], chan=c_xf[t % 2])
            for st in range(4):
                xs = f_xf[:, st * 1024:(st + 1) * 1024]
                dum = f_q[:, st * 1024:st * 1024 + 512]
                b.op("scalar", lambda h, xs=xs, dum=dum, st=st: h.activation(
                    dum, xs[:, 0:512], AF.Square, accum_out=fstat[:, st:st + 1]),
                    r=["fx"], w=["fs2a%d" % st, "fxn%d" % st, "fhs"])
                b.op("scalar", lambda h, xs=xs, dum=dum, st=st: h.activation(
                    dum, xs[:, 512:1024], AF.Square, accum_out=fstat[:, 4 + st:5 + st]),
                    r=["fx", "fs2a%d" % st], w=["fs2b%d" % st, "fxn%d" % st])
                b.op("vector", lambda h, xs=xs, st=st: h.reduce_sum(
                    fstat[:, 8 + st:9 + st], xs, AX.X), r=["fx"], w=["fs1%d" % st])
                b.op("vector", lambda h, st=st: h.tensor_single_scalar(
                    fstat[:, 8 + st:9 + st], fstat[:, 8 + st:9 + st], -1.0 / D, ALU.mult),
                    r=["fs1%d" % st], w=["fs1%d" % st])
                b.op("vector", lambda h, st=st: h.tensor_tensor(
                    fstat[:, 12 + st:13 + st], fstat[:, st:st + 1],
                    fstat[:, 4 + st:5 + st], ALU.add),
                    r=["fs2a%d" % st, "fs2b%d" % st], w=["fv%d" % st])
                b.op("vector", lambda h, st=st: h.tensor_single_scalar(
                    fstat[:, 12 + st:13 + st], fstat[:, 12 + st:13 + st], 1.0 / D, ALU.mult),
                    r=["fv%d" % st], w=["fv%d" % st])
                b.op("vector", lambda h, st=st: h.scalar_tensor_tensor(
                    fstat[:, 12 + st:13 + st], fstat[:, 8 + st:9 + st],
                    fstat[:, 8 + st:9 + st], fstat[:, 12 + st:13 + st],
                    ALU.mult, ALU.subtract),
                    r=["fs1%d" % st, "fv%d" % st], w=["fv%d" % st])
                b.op("scalar", lambda h, st=st: h.activation(
                    fstat[:, 12 + st:13 + st], fstat[:, 12 + st:13 + st], AF.Sqrt,
                    bias=cst[:, 0:1], scale=-1.0), r=["fv%d" % st], w=["fv%d" % st])
                b.op("vector", lambda h, st=st: h.reciprocal(
                    fstat[:, 12 + st:13 + st], fstat[:, 12 + st:13 + st]),
                    r=["fv%d" % st], w=["fistd%d" % st])
                b.op("vector", lambda h, xs=xs, st=st: h.tensor_scalar(
                    f_q[:, st * 1024:(st + 1) * 1024], xs,
                    fstat[:, 8 + st:9 + st], fstat[:, 12 + st:13 + st], ALU.add, ALU.mult),
                    r=["fx", "fs1%d" % st, "fistd%d" % st],
                    w=["fxn%d" % st, "fhs"])
                quant_tm(f_q[:, st * 1024:(st + 1) * 1024], "fxn%d" % st,
                         f_sgn[:, st * 1024:(st + 1) * 1024], "fsgn%d" % st,
                         f_xf[:, st * 1024:(st + 1) * 1024], "fxq%d" % st,
                         f_scl[:, st * 128:(st + 1) * 128], "fscl%d" % st, 1024,
                         eng="gpsimd" if st % 2 else "vector",
                         claim=["fx"], sclaim=["fhqq0", "fhqq1"])

        def emit_XT(t):
            p = t % 2
            for st in range(4):
                for kc in range(8):
                    bank, pk = pb[6 + kc % 2], PB[6 + kc % 2]
                    dk = "xqf%d_%d_%d" % (kc, st, p)
                    b.op("tensor", lambda h, st=st, kc=kc, bank=bank: h.transpose(
                        bank[:, 0:128],
                        f_xf[:, st * 1024 + kc * 128:st * 1024 + (kc + 1) * 128],
                        ident), r=["fxq%d" % st, "ident"], w=[pk])
                    if kc % 2 == 0:
                        b.op("vector", lambda h, kc=kc, st=st, bank=bank, p=p:
                             h.tensor_copy(
                            f_xqf2[p][:, kc * 512 + st * 128:kc * 512 + (st + 1) * 128],
                            bank[:, 0:128]), r=[pk], w=[dk])
                    else:
                        b.op("scalar", lambda h, kc=kc, st=st, bank=bank, p=p: h.copy(
                            f_xqf2[p][:, kc * 512 + st * 128:kc * 512 + (st + 1) * 128],
                            bank[:, 0:128]), r=[pk], w=[dk])

        def emit_Dpre(t):
            b.op("sync", lambda h, t=t: h.dma_start(
                out=f_wq.rearrange("p (c f) -> p c f", f=512),
                in_=YnT[:, t * TILE:(t + 1) * TILE].rearrange("(c p) f -> p c f", p=128)),
                w=["fyn", "fwq"], chan=c_ynr[t % 2])

        def emit_MID(t):
            p = t % 2
            for (soff, w) in SPANS:
                for wi, wsrc in enumerate((wgqT, wuqT)):
                    b.op("sync", lambda h, wsrc=wsrc, soff=soff, w=w: h.dma_start(
                        out=f_wq[:, 0:8 * w].rearrange("p (c f) -> p c f", f=w),
                        in_=wsrc[:, soff:soff + w].rearrange("(c p) f -> p c f", p=128)),
                        w=["fwq"], chan=c_wq[wi])
                    for st in range(4):
                        b.mmg(PB[st], [
                            (lambda h, st=st, kc=kc, w=w, p=p: h.matmul(
                                pb[st][:, 0:w],
                                f_xqf2[p][:, kc * 512 + st * 128:kc * 512 + (st + 1) * 128],
                                f_wq[:, kc * w:(kc + 1) * w],
                                start=(kc == 0), stop=(kc == 7)),
                             ["fwq", "xqf%d_%d_%d" % (kc, st, p)]) for kc in range(8)])
                        if wi == 0:
                            b.op("scalar", lambda h, st=st, w=w: h.activation(
                                f_sil[:, st * 512:st * 512 + w], pb[st][:, 0:w], AF.Silu),
                                r=[PB[st]], w=["fsil%d" % st, "fres"])
                        else:
                            b.op("vector", lambda h, st=st, w=w, soff=soff: h.tensor_tensor(
                                f_h[:, st * 2816 + soff:st * 2816 + soff + w],
                                f_sil[:, st * 512:st * 512 + w], pb[st][:, 0:w], ALU.mult),
                                r=["fsil%d" % st, PB[st]], w=["fh%d" % st])

        def emit_C(t):
            for st in range(4):
                hwin = f_h[:, st * 2816:(st + 1) * 2816]
                for hf in range(2):
                    o = hf * 1408
                    quant_tm(hwin[:, o:o + 1408], "fh%d" % st,
                             f_q[:, o:o + 1408], "fhs",
                             f_sgn[:, o:o + 1408], "fhqq%d" % hf,
                             f_hscl[:, hf * 176:hf * 176 + 176], "fhscl%d" % hf, 1408,
                             eng="vector",
                             claim=["fsgn0", "fsgn1", "fsgn2", "fsgn3"],
                             sclaim=["fxn0", "fxn1", "fxn2", "fhs"],
                             sclclaim=["fres"])
                    for kc in range(11 * hf, 11 * (hf + 1)):
                        bank, pk = pb[4 + kc % 2], PB[4 + kc % 2]
                        dk = "hqf%d_%d" % (kc, st)
                        b.op("tensor", lambda h, kc=kc, hf=hf, bank=bank: h.transpose(
                            bank[:, 0:128], f_sgn[:, kc * 128:(kc + 1) * 128], ident),
                            r=["fhqq%d" % hf, "ident"], w=[pk])
                        if kc % 2 == 0:
                            b.op("vector", lambda h, st=st, kc=kc, bank=bank:
                                 h.tensor_copy(
                                f_h[:, st * 2816 + kc * 128:st * 2816 + (kc + 1) * 128],
                                bank[:, 0:128]), r=[pk], w=[dk, "fh%d" % st])
                        else:
                            b.op("scalar", lambda h, st=st, kc=kc, bank=bank: h.copy(
                                f_h[:, st * 2816 + kc * 128:st * 2816 + (kc + 1) * 128],
                                bank[:, 0:128]), r=[pk], w=[dk, "fh%d" % st])

        def emit_D(t):
            b.op("gpsimd", lambda h, t=t: h.dma_start(
                out=f_res.rearrange("p (c f) -> p c f", f=512),
                in_=xfT[:, t * TILE:(t + 1) * TILE].rearrange("(c p) f -> p c f", p=128)),
                w=["fres", "fhscl0", "fhscl0i", "fhscl1", "fhscl1i"]
                  + ["fsil%d" % s2 for s2 in range(4)],
                chan=c_res)
            for dm in range(8):
                b.hazards("tensor", w=[PB[dm]])
            for di in range(8):
                ws, wsk = f_ws[di % 2], "fws%d" % (di % 2)
                b.op("sync", lambda h, di=di, ws=ws: h.dma_start(
                    out=ws, in_=W_outT[di * 128:(di + 1) * 128, :]),
                    w=[wsk], chan=c_ws[di % 2])
                for dm in range(8):
                    last = dm == 7
                    b.op("tensor", lambda h, di=di, dm=dm, ws=ws: h.matmul(
                        pb[dm], ws[:, dm * 128:(dm + 1) * 128],
                        f_wq[:, di * 512:(di + 1) * 512],
                        start=(di == 0), stop=False, skip_group_check=True),
                        r=([wsk, "fyn", "fwq"] if last else [wsk, "fyn"]),
                        raw=not last)
            f_h4 = f_h.rearrange("p (st x j) -> p st x j", st=4, j=128)
            for kc in range(22):
                ws, wsk = f_ws[kc % 2], "fws%d" % (kc % 2)
                b.op("sync", lambda h, kc=kc, ws=ws: h.dma_start(
                    out=ws, in_=wdqT[kc * 128:(kc + 1) * 128, :]),
                    w=[wsk], chan=c_ws[kc % 2])
                hkeys = ["hqf%d_%d" % (kc, s2) for s2 in range(4)]
                fin = kc == 21
                for dm in range(8):
                    kclast = dm == 7
                    b.op("tensor", lambda h, kc=kc, dm=dm, ws=ws, fin=fin: h.matmul(
                        pb[dm],
                        ws[:, dm * 128:(dm + 1) * 128],
                        f_h4[:, :, kc, :],
                        start=False, stop=fin, skip_group_check=True),
                        r=[wsk] + hkeys,
                        w=([PB[dm]] if fin else ()),
                        raw=not (fin or kclast))

        def emit_TAIL(t):
            for dm in range(8):
                fhk = ["fh0"] if dm < 5 else (["fh0", "fh1"] if dm == 5 else ["fh1"])
                b.op("vector", lambda h, dm=dm: h.tensor_tensor(
                    f_h[:, dm * 512:(dm + 1) * 512], pb[dm],
                    f_res[:, dm * 512:(dm + 1) * 512], ALU.add),
                    r=[PB[dm], PB[7], "fres"], w=["ofm%d" % dm] + fhk)
            for st in range(4):
                bk0 = 4 + 2 * (st % 2)
                for dm in range(8):
                    bank, pk = pb[bk0 + dm // 4], PB[bk0 + dm // 4]
                    fhk2 = ["fh0"] if dm < 5 else (["fh0", "fh1"] if dm == 5 else ["fh1"])
                    b.op("tensor", lambda h, st=st, dm=dm, bank=bank: h.transpose(
                        bank[:, (dm % 4) * 128:(dm % 4) * 128 + 128],
                        f_h[:, dm * 512 + st * 128:dm * 512 + (st + 1) * 128], ident),
                        r=["ofm%d" % dm, "ident"] + fhk2, w=[pk])
                for hb in range(2):
                    bank, pk = pb[bk0 + hb], PB[bk0 + hb]
                    b.op("vector", lambda h, hb=hb, bank=bank: h.tensor_copy(
                        f_h[:, 4096 + hb * 512:4096 + (hb + 1) * 512], bank[:, 0:512]),
                        r=[pk], w=["otm%d" % hb, "fh1"])
                b.op("gpsimd", lambda h, t=t, st=st: h.dma_start(
                    out=out_h[t * TILE + st * 128:t * TILE + (st + 1) * 128, :],
                    in_=f_h[:, 4096:5120]),
                    r=["otm0", "otm1", "fh1"], chan=c_out)

        emit_A(0)
        emit_XT(0)
        for t in range(8):
            if t >= 1:
                emit_TAIL(t - 1)
            if t + 1 <= 7:
                emit_A(t + 1)
            emit_MID(t)
            emit_Dpre(t)
            if t + 1 <= 7:
                emit_XT(t + 1)
            emit_C(t)
            emit_D(t)
        emit_TAIL(7)

    b.barrier()

    with nc.Block() as block:
        b.finalize(block)
    es.close()
    return nc


_NC_CACHE = {}


def _get_nc():
    if "nc" not in _NC_CACHE:
        _NC_CACHE["nc"] = build_nc()
    return _NC_CACHE["nc"]


def make_in_maps(inputs):
    x = np.asarray(inputs["L0_attn_x"], np.float32)
    W_in = np.asarray(inputs["L0_attn_W_in"], np.float32)
    conv_w = np.asarray(inputs["L0_attn_conv_w"], np.float32)
    conv_b = np.asarray(inputs["L0_attn_conv_b"], np.float32)
    A_param = np.asarray(inputs["L0_attn_A_param"], np.float32)
    dt_bias = np.asarray(inputs["L0_attn_dt_bias"], np.float32)
    W_out = np.asarray(inputs["L0_attn_W_out"], np.float32)
    wg = np.asarray(inputs["L0_ffn_w_gate"], np.float32)
    wu = np.asarray(inputs["L0_ffn_w_up"], np.float32)
    wd = np.asarray(inputs["L0_ffn_w_down"], np.float32)

    W_inT = np.ascontiguousarray(W_in.T)
    W_outT = np.ascontiguousarray(W_out.T)
    conv_b2 = np.ascontiguousarray(conv_b[:, None])
    A2 = np.ascontiguousarray(A_param[:, None])
    dtb2 = np.ascontiguousarray(dt_bias[:, None])
    ident = np.eye(128, dtype=np.float32)
    identb = np.eye(128, dtype=ml_dtypes.bfloat16)
    um = np.triu(np.ones((128, 128), np.float32))
    onesc = np.ones((128, 1), np.float32)
    ones1 = np.ones((1, 128), np.float32)
    cstv = np.zeros((128, 2), np.float32)
    cstv[:, 0] = EPS
    cstv[:, 1] = 1.0

    x0 = x[0]
    x0T_full = np.ascontiguousarray(x0.T)
    pad0 = np.zeros((D, HALF), np.float32)
    x0T_sh0 = np.ascontiguousarray(np.concatenate([pad0, x0T_full[:, :HALF]], axis=1))
    m_h0 = np.zeros((NH, S), np.float32)
    m_h0[:, HALF:] = 1.0
    m_h1 = np.ones((NH, S), np.float32)

    in_maps = []
    for c in range(8):
        bb, hh = c // 2, c % 2
        xb = x[bb]
        xbT_full = np.ascontiguousarray(xb.T)
        if hh == 0:
            x0T_c = x0T_sh0
            xbT_c = np.ascontiguousarray(
                np.concatenate([pad0, xbT_full[:, :HALF]], axis=1))
            m_c = m_h0
        else:
            x0T_c = x0T_full
            xbT_c = xbT_full
            m_c = m_h1
        xf = np.ascontiguousarray(xb[hh * HALF:(hh + 1) * HALF])
        xfT = np.ascontiguousarray(xf.T)
        in_maps.append({
            "x0T": x0T_c, "xbT": xbT_c, "xf": xf, "xfT": xfT, "m16": m_c,
            "W_inT": W_inT, "convw": conv_w, "convb": conv_b2,
            "A_param": A2, "dt_bias": dtb2, "W_outT": W_outT,
            "wg": wg, "wu": wu, "wd": wd,
            "ident_in": ident, "identb_in": identb,
            "umask_in": um, "onesc_in": onesc, "ones1_in": ones1,
            "cst_in": cstv,
        })
    return in_maps


def kernel(**inputs):
    nc = _get_nc()
    in_maps = make_in_maps(inputs)
    res = run_bass_kernel_spmd(nc, in_maps, core_ids=list(range(8)))
    out = np.zeros((4, S, D), np.float32)
    for c in range(8):
        bb, hh = c // 2, c % 2
        out[bb, hh * HALF:(hh + 1) * HALF] = res.results[c]["out"]
    return out


# version 14
# speedup vs baseline: 1.1271x; 1.0025x over previous
import contextlib
import numpy as np
import ml_dtypes
import concourse.bass as bass
import concourse.mybir as mybir
from concourse.bass_utils import run_bass_kernel_spmd

F32 = mybir.dt.float32
BF16 = mybir.dt.bfloat16
AF = mybir.ActivationFunctionType
ALU = mybir.AluOpType
AX = mybir.AxisListType

S = 8192
D = 1024
NH = 16
HD = 64
CHK = 64
TILE = 512
NT = S // TILE
NCK = TILE // CHK
HALF = 4096
DI = 2816
EPS = 1e-5

ENGS = ("sync", "tensor", "vector", "scalar", "gpsimd")


class Bld:
    def __init__(self, nc, es):
        self.nc = nc
        self.es = es
        self.esem = {e: es.enter_context(nc.semaphore("se_" + e)) for e in ENGS}
        self.ecnt = {e: 0 for e in ENGS}
        self.ops = {e: [] for e in ENGS}
        self.waited = {e: {} for e in ENGS}
        self.wr = {}
        self.rd = {}
        self.chans = {}

    def chan(self, name):
        if name not in self.chans:
            sem = self.es.enter_context(self.nc.semaphore("ch_" + name))
            self.chans[name] = [sem, 0, "c_" + name]
        return name

    def _wait(self, eng, tok):
        nm, sem, v = tok
        if self.waited[eng].get(nm, 0) >= v:
            return
        self.waited[eng][nm] = v
        self.ops[eng].append(("w", sem, v))

    def hazards(self, eng, r=(), w=()):
        for k in r:
            t = self.wr.get(k)
            if t is not None:
                self._wait(eng, t)
        for k in w:
            t = self.wr.get(k)
            if t is not None:
                self._wait(eng, t)
            for t2 in self.rd.get(k, ()):
                self._wait(eng, t2)

    def op(self, eng, fn, r=(), w=(), chan=None, raw=False, chanwait=True):
        if chan is not None:
            c = self.chans[chan]
            if chanwait and c[1] > 0:
                self._wait(eng, (c[2], c[0], c[1]))
        self.hazards(eng, r, w)
        if raw:
            self.ops[eng].append(("r", fn))
            return None
        if chan is None:
            self.ecnt[eng] += 1
            sem, v, nm, isd = self.esem[eng], self.ecnt[eng], "e_" + eng, False
        else:
            c = self.chans[chan]
            c[1] += 16
            sem, v, nm, isd = c[0], c[1], c[2], True
        tok = (nm, sem, v)
        self.ops[eng].append(("i", fn, sem, isd))
        for k in r:
            self.rd.setdefault(k, []).append(tok)
        for k in w:
            self.wr[k] = tok
            self.rd[k] = []
        return tok

    def settok(self, tok, r=(), w=()):
        for k in r:
            self.rd.setdefault(k, []).append(tok)
        for k in w:
            self.wr[k] = tok
            self.rd[k] = []

    def dma_group(self, eng, chan, items):
        for fn, r, w in items:
            self.hazards(eng, r, w)
        last = None
        for i, (fn, r, w) in enumerate(items):
            last = self.op(eng, fn, chan=chan, chanwait=(i == 0))
        allr = [k for _, r, _ in items for k in r]
        allw = [k for _, _, w in items for k in w]
        self.settok(last, r=allr, w=allw)

    def mmg(self, bank_key, items):
        self.hazards("tensor", w=[bank_key])
        n = len(items)
        union = []
        for _, rk in items:
            for k in rk:
                if k not in union:
                    union.append(k)
        for i, (fn, rk) in enumerate(items):
            if i < n - 1:
                self.op("tensor", fn, r=rk, raw=True)
            else:
                tok = self.op("tensor", fn, r=rk, w=[bank_key])
                extra = [k for k in union if k not in rk]
                self.settok(tok, r=extra)

    def barrier(self):
        toks = [(nm, sem, cnt) for (sem, cnt, nm) in self.chans.values()]
        toks += [("e_" + e, self.esem[e], self.ecnt[e]) for e in ENGS]
        for e in ENGS:
            for tk in toks:
                if tk[0] != "e_" + e and tk[2] > 0:
                    self._wait(e, tk)
        self.wr = {}
        self.rd = {}

    def finalize(self, block):
        for e in ENGS:
            ops = self.ops[e]

            def make(ops):
                def body(h):
                    for it in ops:
                        if it[0] == "w":
                            h.wait_ge(it[1], it[2])
                        elif it[0] == "r":
                            it[1](h)
                        else:
                            it[1](h).then_inc(it[2], 16 if it[3] else 1)
                return body

            getattr(block, e)(make(ops))


def build_nc(upto=4, s3mode=9):
    nc = bass.Bass("TRN2", target_bir_lowering=False)
    es = contextlib.ExitStack()

    def din(name, shape):
        return nc.dram_tensor(name, list(shape), F32, kind="ExternalInput").ap()

    def dsc(name, shape):
        return nc.dram_tensor(name, list(shape), F32, kind="Internal").ap()

    x0T = din("x0T", (D, S))
    xbT = din("xbT", (D, S))
    xf_in = din("xf", (HALF, D))
    xfT = din("xfT", (D, HALF))
    m16 = din("m16", (NH, S))
    W_inT = din("W_inT", (D, 3088))
    convw = din("convw", (3072, 4))
    convb = din("convb", (3072, 1))
    A_param = din("A_param", (NH, 1))
    dt_bias = din("dt_bias", (NH, 1))
    W_outT = din("W_outT", (D, D))
    wg_in = din("wg", (DI, D))
    wu_in = din("wu", (DI, D))
    wd_in = din("wd", (D, DI))
    ident_in = din("ident_in", (128, 128))
    identb_in = nc.dram_tensor("identb_in", [128, 128], BF16, kind="ExternalInput").ap()
    umask_in = din("umask_in", (128, 128))
    onesc_in = din("onesc_in", (128, 1))
    ones1_in = din("ones1_in", (1, 128))
    cst_in = din("cst_in", (128, 2))
    out_h = nc.dram_tensor("out", [HALF, D], F32, kind="ExternalOutput").ap()

    def dscb(name, shape):
        return nc.dram_tensor(name, list(shape), BF16, kind="Internal").ap()

    cvB = dsc("cvB", (D, S))
    cvX = dsc("cvX", (D, S))
    cvC = dsc("cvC", (D, HALF))
    wgqT = dsc("wgqT", (D, DI))
    wuqT = dsc("wuqT", (D, DI))
    wdqT = dsc("wdqT", (DI, D))
    rows_em_d = dsc("rows_em", (NH, S))
    rows_ep_d = dsc("rows_ep", (NH, S))
    rows_nv_d = dsc("rows_nv", (NH, S))
    YnT = dsc("YnT", (D, HALF))

    AR = 42496
    arena = es.enter_context(nc.sbuf_tensor("arena", [128, AR], F32)).ap()
    ident = es.enter_context(nc.sbuf_tensor("ident", [128, 128], F32)).ap()
    identb = es.enter_context(nc.sbuf_tensor("identb", [128, 128], BF16)).ap()
    umask = es.enter_context(nc.sbuf_tensor("umask", [128, 128], F32)).ap()
    onesc = es.enter_context(nc.sbuf_tensor("onesc", [128, 1], F32)).ap()
    ones1 = es.enter_context(nc.sbuf_tensor("ones1", [1, 128], F32)).ap()
    apar = es.enter_context(nc.sbuf_tensor("apar", [NH, 1], F32)).ap()
    dtb = es.enter_context(nc.sbuf_tensor("dtb", [NH, 1], F32)).ap()
    cwb = es.enter_context(nc.sbuf_tensor("cwb", [128, 5 * 24], F32)).ap()
    swr = es.enter_context(nc.sbuf_tensor("swr", [NH, 1], F32)).ap()
    gtab = es.enter_context(nc.sbuf_tensor("gtab", [NH, 160], F32)).ap()
    carry = es.enter_context(nc.sbuf_tensor("carry", [NH, 8], F32)).ap()
    mrow = es.enter_context(nc.sbuf_tensor("mrow", [NH, TILE], F32)).ap()
    srow = es.enter_context(nc.sbuf_tensor("srow", [1, 8 * TILE], F32)).ap()
    rowsb = es.enter_context(nc.sbuf_tensor("rowsb", [NH, 8 * TILE], F32)).ap()
    fstat = es.enter_context(nc.sbuf_tensor("fstat", [128, 16], F32)).ap()
    cst = es.enter_context(nc.sbuf_tensor("cst", [128, 2], F32)).ap()
    wdt_s = es.enter_context(nc.sbuf_tensor("wdt_sb", [128, 128], F32)).ap()

    pb = [es.enter_context(nc.psum_tensor("pbank%d" % i, [128, 512], F32)).ap()
          for i in range(8)]
    PB = ["pb%d" % i for i in range(8)]

    b = Bld(nc, es)

    R_dt, R_A, R_cum, R_em, R_ep, R_S, R_t0, R_t1 = range(8)

    def rslot(i):
        return rowsb[0:16, i * TILE:(i + 1) * TILE]

    def sslot(i):
        return srow[0:1, i * TILE:(i + 1) * TILE]

    ldc = b.chan("const")
    citems = [
        (lambda h: h.dma_start(out=ident, in_=ident_in), (), ["ident"]),
        (lambda h: h.dma_start(out=identb, in_=identb_in), (), ["identb"]),
        (lambda h: h.dma_start(out=umask, in_=umask_in), (), ["umask"]),
        (lambda h: h.dma_start(out=onesc, in_=onesc_in), (), ["onesc"]),
        (lambda h: h.dma_start(out=ones1, in_=ones1_in), (), ["ones1"]),
        (lambda h: h.dma_start(out=apar, in_=A_param), (), ["apar"]),
        (lambda h: h.dma_start(out=dtb, in_=dt_bias), (), ["dtb"]),
        (lambda h: h.dma_start(out=cst, in_=cst_in), (), ["cst"]),
    ]
    for fc in range(24):
        citems.append((lambda h, fc=fc: h.dma_start(
            out=cwb[:, 5 * fc:5 * fc + 4], in_=convw[fc * 128:(fc + 1) * 128, :]),
            (), ["cwb"]))
        citems.append((lambda h, fc=fc: h.dma_start(
            out=cwb[:, 5 * fc + 4:5 * fc + 5], in_=convb[fc * 128:(fc + 1) * 128, :]),
            (), ["cwb"]))
    b.dma_group("sync", ldc, citems)

    wQ = arena[:, 26696:29512]
    wS = arena[:, 29512:32328]
    wq1b = [arena[:, 32328:35144], arena[:, 38344:41160]]
    wsc = arena[:, 35144:35528]
    wst = arena[:, 35528:38344]

    c_wdt = b.chan("wdtl")
    c_w = b.chan("s0w")
    c_st = b.chan("s0st")

    b.dma_group("sync", c_wdt, [
        (lambda h, kc=kc: h.dma_start(
            out=wdt_s[:, kc * 16:(kc + 1) * 16],
            in_=W_inT[kc * 128:(kc + 1) * 128, 3072:3088]), (), ["wdt_s"])
        for kc in range(8)])
    b.mmg(PB[0], [
        (lambda h, kc=kc: h.matmul(pb[0][0:1, 0:16], onesc,
                                   wdt_s[:, kc * 16:(kc + 1) * 16],
                                   start=(kc == 0), stop=(kc == 7)),
         ["wdt_s", "onesc"]) for kc in range(8)])
    b.op("vector", lambda h: h.tensor_copy(gtab[0:1, 128:144], pb[0][0:1, 0:16]),
         r=[PB[0]], w=["swrow"])
    b.op("tensor", lambda h: h.transpose(pb[1][0:16, 0:1], gtab[0:1, 128:144], ident[0:1, 0:1]),
         r=["swrow", "ident"], w=[PB[1]])
    b.op("vector", lambda h: h.tensor_copy(swr, pb[1][0:16, 0:1]), r=[PB[1]], w=["swr"])

    def ladder(v_ap, q_ap, vkey, qkey, eng, claim=(), tmp_ap=None, tmpkey=None,
               tmpclaim=()):
        b.op(eng, lambda h: h.tensor_single_scalar(q_ap, v_ap, 0.25, ALU.is_gt),
             r=[vkey], w=[qkey] + list(claim))
        if eng == "vector":
            for c in (0.75, 1.25, 1.75):
                b.op(eng, lambda h, c=c: h.scalar_tensor_tensor(
                    q_ap, v_ap, c, q_ap, ALU.is_gt, ALU.add), r=[vkey, qkey], w=[qkey])
            b.op(eng, lambda h: h.tensor_single_scalar(q_ap, q_ap, 0.5, ALU.mult),
                 r=[qkey], w=[qkey])
            for c in (2.5, 3.5, 5.0, 5.0):
                b.op(eng, lambda h, c=c: h.scalar_tensor_tensor(
                    q_ap, v_ap, c, q_ap, ALU.is_gt, ALU.add), r=[vkey, qkey], w=[qkey])
        else:
            first = True
            for c in (0.75, 1.25, 1.75):
                b.op(eng, lambda h, c=c: h.tensor_single_scalar(
                    tmp_ap, v_ap, c, ALU.is_gt),
                    r=[vkey], w=[tmpkey] + (list(tmpclaim) if first else []))
                first = False
                b.op(eng, lambda h: h.tensor_tensor(q_ap, q_ap, tmp_ap, ALU.add),
                     r=[qkey, tmpkey], w=[qkey])
            b.op(eng, lambda h: h.tensor_single_scalar(q_ap, q_ap, 0.5, ALU.mult),
                 r=[qkey], w=[qkey])
            for c in (2.5, 3.5, 5.0, 5.0):
                b.op(eng, lambda h, c=c: h.tensor_single_scalar(
                    tmp_ap, v_ap, c, ALU.is_gt), r=[vkey], w=[tmpkey])
                b.op(eng, lambda h: h.tensor_tensor(q_ap, q_ap, tmp_ap, ALU.add),
                     r=[qkey, tmpkey], w=[qkey])

    def quant_tm(x_ap, xkey, sgn_ap, skey, q_ap, qkey, scl_ap, sclkey, n, eng="vector",
                 claim=(), sclaim=(), sclclaim=(), tmp_ap=None, tmpkey=None,
                 tmpclaim=()):
        nb = n // 16
        if eng == "gpsimd" and tmp_ap is None:
            eng = "vector"
        b.op("vector", lambda h: h.tensor_reduce(
            scl_ap[:, 0:nb], x_ap.rearrange("p (bb k) -> p bb k", k=16),
            AX.X, ALU.max, apply_absolute_value=True),
            r=[xkey], w=[sclkey] + list(sclclaim))
        b.op(eng, lambda h: h.tensor_scalar(
            scl_ap[:, 0:nb], scl_ap[:, 0:nb], 1.0 / 6.0, 1e-12, ALU.mult, ALU.max),
            r=[sclkey], w=[sclkey])
        b.op("vector", lambda h: h.reciprocal(scl_ap[:, nb:2 * nb], scl_ap[:, 0:nb]),
             r=[sclkey], w=[sclkey + "i"])
        b.op("scalar", lambda h: h.activation(sgn_ap, x_ap, AF.Sign),
             r=[xkey], w=[skey] + list(sclaim))
        b.op("scalar", lambda h: h.activation(x_ap, x_ap, AF.Abs), r=[xkey, skey], w=[xkey])
        b.op(eng, lambda h: h.tensor_tensor(
            x_ap.rearrange("p (bb k) -> p bb k", k=16),
            x_ap.rearrange("p (bb k) -> p bb k", k=16),
            scl_ap[:, nb:2 * nb].unsqueeze(2).to_broadcast([128, nb, 16]), ALU.mult),
            r=[xkey, sclkey + "i"], w=[xkey])
        ladder(x_ap, q_ap, xkey, qkey, eng, claim=claim,
               tmp_ap=tmp_ap, tmpkey=tmpkey, tmpclaim=tmpclaim)
        b.op(eng, lambda h: h.tensor_tensor(q_ap, q_ap, sgn_ap, ALU.mult),
             r=[qkey, skey], w=[qkey])
        b.op(eng, lambda h: h.tensor_tensor(
            q_ap.rearrange("p (bb k) -> p bb k", k=16),
            q_ap.rearrange("p (bb k) -> p bb k", k=16),
            scl_ap[:, 0:nb].unsqueeze(2).to_broadcast([128, nb, 16]), ALU.mult),
            r=[qkey, sclkey], w=[qkey])

    def s0_weight(w_in_ap, wT_out_ap, nrows, ncols):
        nblk = ncols // 128

        def s0_T(rc):
            wq1, wqk = wq1b[rc % 2], "wq1_%d" % (rc % 2)
            for tb in range(nblk):
                bank, pk = pb[tb % 2], PB[tb % 2]
                b.op("tensor", lambda h, tb=tb, bank=bank, wq1=wq1: h.transpose(
                    bank[:, 0:128], wq1[:, tb * 128:(tb + 1) * 128], ident),
                    r=[wqk, "ident"], w=[pk])
                if tb % 2 == 0:
                    b.op("vector", lambda h, tb=tb, bank=bank: h.tensor_copy(
                        wst[:, tb * 128:(tb + 1) * 128], bank[:, 0:128]),
                        r=[pk], w=["wst%d" % tb])
                else:
                    b.op("scalar", lambda h, tb=tb, bank=bank: h.copy(
                        wst[:, tb * 128:(tb + 1) * 128], bank[:, 0:128]),
                        r=[pk], w=["wst%d" % tb])
            for tb in range(nblk):
                b.op("scalar", lambda h, tb=tb, rc=rc, dst_d=wT_out_ap: h.dma_start(
                    out=dst_d[tb * 128:(tb + 1) * 128, rc * 128:(rc + 1) * 128],
                    in_=wst[:, tb * 128:(tb + 1) * 128]),
                    r=["wst%d" % tb], chan=c_st)

        prev = None
        for rc in range(nrows // 128):
            yield
            b.op("sync", lambda h, rc=rc: h.dma_start(
                out=wQ[:, 0:ncols], in_=w_in_ap[rc * 128:(rc + 1) * 128, :]),
                w=["wQ"], chan=c_w)
            quant_tm(wQ[:, 0:ncols], "wQ", wS[:, 0:ncols], "wS",
                     wq1b[rc % 2][:, 0:ncols], "wq1_%d" % (rc % 2), wsc, "wsc", ncols)
            if prev is not None:
                yield
                s0_T(prev)
            prev = rc
        yield
        s0_T(prev)

    def s0_steps():
        yield from s0_weight(wg_in, wgqT, DI, D)
        yield from s0_weight(wu_in, wuqT, DI, D)
        yield from s0_weight(wd_in, wdqT, D, DI)
    s0_gen = s0_steps()

    a_x = arena[:, 0:4096]
    a_sq = arena[:, 4096:4608]
    a_cv = arena[:, 4608:4608 + 16 * 515]
    a_cvC = arena[:, 12848:12848 + 8 * 515]
    a_cd = [arena[:, 16968:17480], arena[:, 17480:17992]]
    a_wb = [arena[:, 17992:19016], arena[:, 19016:20040]]
    a_xn = arena[:, 20040:24136]
    a_bc = arena[:, 24136:25160]
    a_ct = arena[:, 25160:26696]

    c_x = [b.chan("xs0"), b.chan("xs1")]
    c_wp = [b.chan("wp0"), b.chan("wp1")]
    c_m = b.chan("mrowc")
    c_sp = [b.chan("sp0"), b.chan("sp1")]
    c_rw = b.chan("rowsw")

    def stats_and_dt(key):
        b.mmg(PB[6], [
            (lambda h, fc=fc: h.matmul(pb[6][0:1, 0:512], onesc,
                                       a_x[:, fc * 512:(fc + 1) * 512],
                                       start=(fc == 0), stop=(fc == 7)),
             [key + "x", "onesc"]) for fc in range(8)])
        b.hazards("tensor", w=[PB[7]])
        for fc in range(8):
            b.op("scalar", lambda h, fc=fc: h.activation(
                a_sq[:, 0:512], a_x[:, fc * 512:(fc + 1) * 512], AF.Square),
                r=[key + "x"], w=["sqs"])
            b.op("tensor", lambda h, fc=fc: h.matmul(
                pb[7][0:1, 0:512], onesc, a_sq[:, 0:512],
                start=(fc == 0), stop=(fc == 7)),
                r=["sqs", "onesc"], w=([PB[7]] if fc == 7 else ()))
        b.op("vector", lambda h: h.tensor_single_scalar(
            sslot(0), pb[6][0:1, 0:512], 1.0 / D, ALU.mult), r=[PB[6]], w=["S0k"])
        b.op("scalar", lambda h: h.activation(sslot(1), sslot(0), AF.Square),
             r=["S0k"], w=["S1k"])
        b.op("vector", lambda h: h.scalar_tensor_tensor(
            sslot(1), pb[7][0:1, 0:512], 1.0 / D, sslot(1), ALU.mult, ALU.subtract),
            r=[PB[7], "S1k"], w=["S1k"])
        b.op("scalar", lambda h: h.activation(sslot(1), sslot(1), AF.Sqrt, bias=cst[0:1, 0:1]),
             r=["S1k"], w=["S1k"])
        b.op("vector", lambda h: h.reciprocal(sslot(2), sslot(1)), r=["S1k"], w=["S2k"])
        b.op("vector", lambda h: h.tensor_tensor(sslot(3), sslot(0), sslot(2), ALU.mult),
             r=["S0k", "S2k"], w=["S3k"])
        b.mmg(PB[5], [
            (lambda h, kc=kc: h.matmul(pb[5][0:16, 0:512], wdt_s[:, kc * 16:(kc + 1) * 16],
                                       a_x[:, kc * 512:(kc + 1) * 512],
                                       start=(kc == 0), stop=(kc == 7)),
             ["wdt_s", key + "x"]) for kc in range(8)])
        b.op("tensor", lambda h: h.matmul(pb[4][0:16, 0:512], ones1[0:1, 0:16], sslot(2),
                                          start=True, stop=True),
             r=["S2k", "ones1"], w=[PB[4]])
        b.op("scalar", lambda h: h.copy(rslot(R_t1), pb[5][0:16, 0:512]),
             r=[PB[5]], w=["Rt1"])
        b.op("vector", lambda h: h.tensor_tensor(
            rslot(R_t0), rslot(R_t1), pb[4][0:16, 0:512], ALU.mult),
            r=["Rt1", PB[4]], w=["Rt0"])
        b.op("tensor", lambda h: h.matmul(pb[4][0:16, 0:512], ones1[0:1, 0:16], sslot(3),
                                          start=True, stop=True),
             r=["S3k", "ones1"], w=[PB[4]])
        b.op("vector", lambda h: h.scalar_tensor_tensor(
            rslot(R_dt), pb[4][0:16, 0:512], swr[:, 0:1], rslot(R_t0), ALU.mult, ALU.subtract),
            r=[PB[4], "swr", "Rt0"], w=["Rdt"])
        b.op("scalar", lambda h: h.activation(rslot(R_t1), rslot(R_dt), AF.Exp,
                                              bias=dtb[:, 0:1], scale=-1.0),
             r=["Rdt", "dtb"], w=["Rt1"])
        b.op("scalar", lambda h: h.activation(rslot(R_dt), rslot(R_t1), AF.Ln, bias=cst[0:16, 1:2]),
             r=["Rt1"], w=["Rdt"])
        b.op("vector", lambda h: h.tensor_scalar_mul(rslot(R_A), rslot(R_dt), apar[:, 0:1]),
             r=["Rdt", "apar"], w=["RA"])
        b.op("vector", lambda h: h.tensor_tensor(rslot(R_A), rslot(R_A), mrow, ALU.mult),
             r=["RA", "mrowk"], w=["RA"])

    if upto >= 1:
        b.op("vector", lambda h: h.memset(carry[:, 2:3], 0.0), w=["c_D"])
        b.op("vector", lambda h: h.memset(carry[:, 3:4], 1.0), w=["c_u"])
        for t in range(NT):
            b.op("sync", lambda h, t=t: h.dma_start(
                out=a_x.rearrange("p (c f) -> p c f", f=512),
                in_=xbT[:, t * TILE:(t + 1) * TILE].rearrange("(c p) f -> p c f", p=128)),
                w=["s1x"], chan=c_x[t % 2])
            b.op("sync", lambda h, t=t: h.dma_start(
                out=mrow, in_=m16[:, t * TILE:(t + 1) * TILE]), w=["mrowk"], chan=c_m)
            stats_and_dt("s1")
            b.op("scalar", lambda h: h.activation(rslot(R_em), rslot(R_A), AF.Exp),
                 r=["RA"], w=["Rem"])
            b.op("vector", lambda h: h.memset(rslot(R_t1), 1.0), w=["Rt1"])
            b.op("vector", lambda h: h.tensor_scalar_mul(rslot(R_t0), rslot(R_t1), carry[:, 3:4]),
                 r=["Rt1", "c_u"], w=["Rt0"])
            b.op("vector", lambda h: h.tensor_tensor(rslot(R_t0), rslot(R_t0), mrow, ALU.mult),
                 r=["Rt0", "mrowk"], w=["Rt0"])
            b.op("vector", lambda h: h.tensor_tensor_scan(
                rslot(R_S), rslot(R_em), rslot(R_t0), carry[:, 2:3], ALU.mult, ALU.add),
                r=["Rem", "Rt0", "c_D"], w=["RS"])
            b.op("vector", lambda h: h.tensor_single_scalar(
                rslot(R_t1), rslot(R_S), 1e-12, ALU.max), r=["RS"], w=["Rt1"])
            b.op("vector", lambda h: h.reciprocal(rslot(R_dt), rslot(R_t1)),
                 r=["Rt1"], w=["Rdt"])
            b.op("vector", lambda h: h.tensor_scalar_mul(rslot(R_dt), rslot(R_dt), carry[:, 3:4]),
                 r=["Rdt", "c_u"], w=["Rdt"])
            b.op("scalar", lambda h, t=t: h.dma_start(
                out=rows_nv_d[:, t * TILE:(t + 1) * TILE], in_=rslot(R_dt)),
                r=["Rdt"], chan=c_rw)
            b.op("vector", lambda h: h.tensor_copy(carry[:, 4:5], rslot(R_S)[:, TILE - 1:TILE]),
                 r=["RS"], w=["c_t"])
            b.op("vector", lambda h: h.tensor_single_scalar(
                carry[:, 5:6], carry[:, 4:5], 1e-12, ALU.max), r=["c_t"], w=["c_r"])
            b.op("vector", lambda h: h.reciprocal(carry[:, 5:6], carry[:, 5:6]),
                 r=["c_r"], w=["c_r"])
            b.op("vector", lambda h: h.reduce_max(carry[:, 6:7], mrow, AX.X),
                 r=["mrowk"], w=["c_m1"])
            b.op("vector", lambda h: h.tensor_tensor(carry[:, 5:6], carry[:, 5:6],
                                                     carry[:, 6:7], ALU.mult),
                 r=["c_r", "c_m1"], w=["c_r"])
            b.op("vector", lambda h: h.tensor_scalar(carry[:, 6:7], carry[:, 6:7],
                                                     -1.0, 1.0, ALU.mult, ALU.add),
                 r=["c_m1"], w=["c_m1"])
            b.op("vector", lambda h: h.tensor_tensor(carry[:, 5:6], carry[:, 5:6],
                                                     carry[:, 6:7], ALU.add),
                 r=["c_r", "c_m1"], w=["c_r"])
            b.op("vector", lambda h: h.tensor_tensor(carry[:, 2:3], carry[:, 4:5],
                                                     carry[:, 5:6], ALU.mult),
                 r=["c_t", "c_r"], w=["c_D"])
            b.op("vector", lambda h: h.tensor_tensor(carry[:, 3:4], carry[:, 3:4],
                                                     carry[:, 5:6], ALU.mult),
                 r=["c_u", "c_r"], w=["c_u"])
        b.barrier()

    def proj_conv(spill_col, ofs, winf, wkeyf, cwcf, spill):
        for j, o in enumerate(ofs):
            wb, wk = a_wb[j % 2], "wb%d" % (j % 2)
            b.op("sync", lambda h, o=o, wb=wb: h.dma_start(
                out=wb.rearrange("p (c f) -> p c f", f=128),
                in_=W_inT[:, o * 128:(o + 1) * 128].rearrange("(c p) f -> p c f", p=128)),
                w=[wk], chan=c_wp[j % 2])
            bank, pk = pb[j % 2], PB[j % 2]
            b.mmg(pk, [
                (lambda h, kc=kc, wb=wb, bank=bank: h.matmul(
                    bank, wb[:, kc * 128:(kc + 1) * 128],
                    a_xn[:, kc * 512:(kc + 1) * 512],
                    start=(kc == 0), stop=(kc == 7)),
                 [wk, "xn%d" % kc]) for kc in range(8)])
            win, wkey = winf(j), wkeyf(j)
            if j % 2 == 0:
                b.op("vector", lambda h, win=win, bank=bank: h.tensor_copy(win[:, 3:515], bank),
                     r=[pk], w=[wkey])
            else:
                b.op("scalar", lambda h, win=win, bank=bank: h.copy(win[:, 3:515], bank),
                     r=[pk], w=[wkey])
            cd, cdk = a_cd[j % 2], "cd%d" % (j % 2)
            cc = cwcf(j)
            b.op("scalar", lambda h, win=win, cd=cd, cc=cc: h.activation(
                cd, win[:, 0:512], AF.Copy, scale=cwb[:, 5 * cc:5 * cc + 1]),
                r=[wkey, "cwb"], w=[cdk])
            for tap in range(1, 4):
                b.op("scalar", lambda h, win=win, cc=cc, tap=tap: h.activation(
                    a_ct[:, (tap - 1) * 512:tap * 512], win[:, tap:tap + 512], AF.Copy,
                    scale=cwb[:, 5 * cc + tap:5 * cc + tap + 1]),
                    r=[wkey, "cwb"], w=["ctmp%d" % tap])
            b.op("vector", lambda h, cd=cd, cc=cc: h.scalar_tensor_tensor(
                a_ct[:, 0:512], a_ct[:, 0:512], cwb[:, 5 * cc + 4:5 * cc + 5],
                a_ct[:, 512:1024], ALU.add, ALU.add),
                r=["ctmp1", "ctmp2", "cwb"], w=["ctmp1"])
            b.op("vector", lambda h, cd=cd: h.tensor_tensor(
                cd, cd, a_ct[:, 1024:1536], ALU.add), r=[cdk, "ctmp3"], w=[cdk])
            b.op("vector", lambda h, cd=cd: h.tensor_tensor(
                cd, cd, a_ct[:, 0:512], ALU.add), r=[cdk, "ctmp1"], w=[cdk])
            b.op("gpsimd", lambda h, win=win: h.tensor_copy(win[:, 0:3], win[:, 512:515]),
                 r=[wkey], w=[wkey])
            b.op("scalar", lambda h, cd=cd, j=j, spill=spill, spill_col=spill_col: h.dma_start(
                out=spill[j * 128:(j + 1) * 128, spill_col * TILE:(spill_col + 1) * TILE],
                in_=cd), r=[cdk], chan=c_sp[j % 2])

    if upto >= 2:
        b.op("vector", lambda h: h.memset(carry[:, 0:1], 0.0), w=["c_cum"])
        for fc in range(16):
            b.op("gpsimd", lambda h, fc=fc: h.memset(a_cv[:, fc * 515:fc * 515 + 3], 0.0),
                 w=["cw%d" % fc])
        for fc in range(8):
            b.op("gpsimd", lambda h, fc=fc: h.memset(a_cvC[:, fc * 515:fc * 515 + 3], 0.0),
                 w=["cwC%d" % fc])
        for t in range(NT):
            b.op("sync", lambda h, t=t: h.dma_start(
                out=a_x.rearrange("p (c f) -> p c f", f=512),
                in_=x0T[:, t * TILE:(t + 1) * TILE].rearrange("(c p) f -> p c f", p=128)),
                w=["s2x"], chan=c_x[t % 2])
            b.op("sync", lambda h, t=t: h.dma_start(
                out=mrow, in_=m16[:, t * TILE:(t + 1) * TILE]), w=["mrowk"], chan=c_m)
            stats_and_dt("s2")
            b.op("tensor", lambda h: h.matmul(pb[2], ones1, sslot(0), start=True, stop=True),
                 r=["S0k", "ones1"], w=[PB[2]])
            b.op("tensor", lambda h: h.matmul(pb[3], ones1, sslot(2), start=True, stop=True),
                 r=["S2k", "ones1"], w=[PB[3]])
            b.op("scalar", lambda h: h.copy(a_bc[:, 0:512], pb[2]), r=[PB[2]], w=["bcmu"])
            b.op("scalar", lambda h: h.copy(a_bc[:, 512:1024], pb[3]), r=[PB[3]], w=["bcis"])
            for fc in range(8):
                ee = "vector" if fc % 2 == 0 else "gpsimd"
                b.op(ee, lambda h, fc=fc: h.tensor_tensor(
                    a_xn[:, fc * 512:(fc + 1) * 512], a_x[:, fc * 512:(fc + 1) * 512],
                    a_bc[:, 0:512], ALU.subtract), r=["s2x", "bcmu"], w=["xn%d" % fc])
                b.op(ee, lambda h, fc=fc: h.tensor_tensor(
                    a_xn[:, fc * 512:(fc + 1) * 512], a_xn[:, fc * 512:(fc + 1) * 512],
                    a_bc[:, 512:1024], ALU.mult), r=["xn%d" % fc, "bcis"], w=["xn%d" % fc])
            proj_conv(t, list(range(8, 16)),
                      lambda j: a_cv[:, j * 515:(j + 1) * 515], lambda j: "cw%d" % j,
                      lambda j: 8 + j, cvB)
            for _ in range(3):
                next(s0_gen, None)
            proj_conv(t, list(range(16, 24)),
                      lambda j: a_cv[:, (8 + j) * 515:(9 + j) * 515],
                      lambda j: "cw%d" % (8 + j), lambda j: 16 + j, cvX)
            if t == 7:
                for j, o in enumerate(range(0, 8)):
                    wb, wk = a_wb[j % 2], "wb%d" % (j % 2)
                    b.op("sync", lambda h, o=o, wb=wb: h.dma_start(
                        out=wb.rearrange("p (c f) -> p c f", f=128),
                        in_=W_inT[:, o * 128:(o + 1) * 128].rearrange(
                            "(c p) f -> p c f", p=128)),
                        w=[wk], chan=c_wp[j % 2])
                    bank, pk = pb[j % 2], PB[j % 2]
                    b.mmg(pk, [
                        (lambda h, kc=kc, wb=wb, bank=bank: h.matmul(
                            bank[:, 0:3], wb[:, kc * 128:(kc + 1) * 128],
                            a_xn[:, kc * 512 + 509:kc * 512 + 512],
                            start=(kc == 0), stop=(kc == 7)),
                         [wk, "xn%d" % kc]) for kc in range(8)])
                    b.op("vector", lambda h, j=j, bank=bank: h.tensor_copy(
                        a_cvC[:, j * 515:j * 515 + 3], bank[:, 0:3]),
                        r=[pk], w=["cwC%d" % j])
            for _ in range(3):
                next(s0_gen, None)
            if t >= 8:
                proj_conv(t - 8, list(range(0, 8)),
                          lambda j: a_cvC[:, j * 515:(j + 1) * 515],
                          lambda j: "cwC%d" % j, lambda j: j, cvC)
            b.op("vector", lambda h: h.tensor_tensor_scan(
                rslot(R_cum), rslot(R_A), rslot(R_A), carry[:, 0:1], ALU.add, ALU.bypass),
                r=["RA", "c_cum"], w=["Rcum"])
            b.op("vector", lambda h: h.tensor_copy(carry[:, 0:1], rslot(R_cum)[:, TILE - 1:TILE]),
                 r=["Rcum"], w=["c_cum"])
            b.op("scalar", lambda h: h.activation(rslot(R_em), rslot(R_cum), AF.Exp, scale=-1.0),
                 r=["Rcum"], w=["Rem"])
            b.op("vector", lambda h: h.tensor_tensor(rslot(R_em), rslot(R_em), mrow, ALU.mult),
                 r=["Rem", "mrowk"], w=["Rem"])
            b.op("scalar", lambda h: h.activation(rslot(R_ep), rslot(R_cum), AF.Exp),
                 r=["Rcum"], w=["Rep"])
            b.op("scalar", lambda h, t=t: h.dma_start(
                out=rows_em_d[:, t * TILE:(t + 1) * TILE], in_=rslot(R_em)),
                r=["Rem"], chan=c_rw)
            b.op("scalar", lambda h, t=t: h.dma_start(
                out=rows_ep_d[:, t * TILE:(t + 1) * TILE], in_=rslot(R_ep)),
                r=["Rep"], chan=c_rw)
            for _ in range(2):
                next(s0_gen, None)
        for _ in s0_gen:
            pass
        b.barrier()

    if upto < 2:
        for _ in s0_gen:
            pass
        b.barrier()
    if upto >= 3:
        CH2 = 128
        NCK2 = TILE // CH2
        s_B64 = arena[0:64, 0:8192]
        s_X64 = arena[0:64, 8192:16384]
        s_C64 = arena[0:64, 16384:24576]
        s_ms = arena[:, 24576:26624]
        s_xt = arena[:, 26624:27648]
        s_Bt = arena[:, 27648:28672]
        s_ACC = arena[0:64, 28672:29696]
        s_Y = arena[:, 29696:30720]
        s_emt = arena[:, 30720:30784]
        s_ft = arena[:, 30784:30848]
        s_Yne = arena[0:64, 30848:34944]
        s_Yno = arena[0:64, 34944:39040]

        c_b64 = [b.chan("b64_0"), b.chan("b64_1")]
        c_x64 = [b.chan("x64_0"), b.chan("x64_1")]
        c_c64 = [b.chan("c64_0"), b.chan("c64_1")]
        c_rem = b.chan("rowem")
        c_rep = b.chan("rowep")
        c_rnv = b.chan("rownv")
        c_yn = [b.chan("yn0"), b.chan("yn1")]

        b.op("vector", lambda h: h.memset(s_ACC[:, 0:512], 0.0), w=["ACC0"])
        b.op("vector", lambda h: h.memset(s_ACC[:, 512:1024], 0.0), w=["ACC1"])

        YnT4 = YnT.rearrange("(g two p) f -> two p g f", two=2, p=64)

        for t in range(NT):
            isY = t >= 8
            b.op("sync", lambda h, t=t: h.dma_start(
                out=s_B64.rearrange("p (g f) -> p g f", f=512),
                in_=cvB[:, t * TILE:(t + 1) * TILE].rearrange("(g p) f -> p g f", p=64)),
                w=["B64"], chan=c_b64[t % 2])
            b.op("sync", lambda h, t=t: h.dma_start(
                out=s_X64.rearrange("p (g f) -> p g f", f=512),
                in_=cvX[:, t * TILE:(t + 1) * TILE].rearrange("(g p) f -> p g f", p=64)),
                w=["X64"], chan=c_x64[t % 2])
            if isY:
                b.op("sync", lambda h, t=t: h.dma_start(
                    out=s_C64.rearrange("p (g f) -> p g f", f=512),
                    in_=cvC[:, (t - 8) * TILE:(t - 7) * TILE].rearrange("(g p) f -> p g f", p=64)),
                    w=["C64"], chan=c_c64[t % 2])
            b.op("sync", lambda h, t=t: h.dma_start(
                out=rslot(R_em), in_=rows_em_d[:, t * TILE:(t + 1) * TILE]),
                w=["Rem"], chan=c_rem)
            if isY:
                b.op("sync", lambda h, t=t: h.dma_start(
                    out=rslot(R_ep), in_=rows_ep_d[:, t * TILE:(t + 1) * TILE]),
                    w=["Rep"], chan=c_rep)
                b.op("sync", lambda h, t=t: h.dma_start(
                    out=rslot(R_t0), in_=rows_nv_d[:, t * TILE:(t + 1) * TILE]),
                    w=["Rt0"], chan=c_rnv)
                b.op("vector", lambda h: h.tensor_tensor(
                    rslot(R_ep), rslot(R_ep), rslot(R_t0), ALU.mult),
                    r=["Rep", "Rt0"], w=["Rep"])
            for blk in range(4):
                bank, pk = pb[6 + blk % 2], PB[6 + blk % 2]
                b.op("tensor", lambda h, blk=blk, bank=bank: h.transpose(
                    bank[:, 0:16], rslot(R_em)[:, blk * 128:(blk + 1) * 128],
                    ident[0:16, 0:16]), r=["Rem", "ident"], w=[pk])
                b.op("vector", lambda h, blk=blk, bank=bank: h.tensor_copy(
                    s_emt[:, blk * 16:(blk + 1) * 16], bank[:, 0:16]),
                    r=[pk], w=["emt"])
                if isY:
                    b.op("tensor", lambda h, blk=blk, bank=bank: h.transpose(
                        bank[:, 0:16], rslot(R_ep)[:, blk * 128:(blk + 1) * 128],
                        ident[0:16, 0:16]), r=["Rep", "ident"], w=[pk])
                    b.op("vector", lambda h, blk=blk, bank=bank: h.tensor_copy(
                        s_ft[:, blk * 16:(blk + 1) * 16], bank[:, 0:16]),
                        r=[pk], w=["ft"])
            for ci in range(NCK2):
                for hh in range(NH):
                    bank, pk = pb[6 + hh % 2], PB[6 + hh % 2]
                    xsl = s_X64[:, hh * 512 + ci * 128:hh * 512 + (ci + 1) * 128]
                    b.op("tensor", lambda h, xsl=xsl, bank=bank: h.transpose(
                        bank[:, 0:64], xsl, ident[0:64, 0:64]),
                        r=["X64", "ident"], w=[pk])
                    if hh % 2 == 0:
                        b.op("vector", lambda h, hh=hh, bank=bank, ci=ci: h.tensor_scalar_mul(
                            s_xt[:, hh * 64:(hh + 1) * 64], bank[:, 0:64],
                            s_emt[:, ci * 16 + hh:ci * 16 + hh + 1]),
                            r=[pk, "emt"], w=["xt%d" % hh])
                    else:
                        b.op("scalar", lambda h, hh=hh, bank=bank, ci=ci: h.activation(
                            s_xt[:, hh * 64:(hh + 1) * 64], bank[:, 0:64], AF.Copy,
                            scale=s_emt[:, ci * 16 + hh:ci * 16 + hh + 1]),
                            r=[pk, "emt"], w=["xt%d" % hh])
                    bsl = s_B64[:, hh * 512 + ci * 128:hh * 512 + (ci + 1) * 128]
                    b.op("tensor", lambda h, bsl=bsl, bank=bank: h.transpose(
                        bank[:, 0:64], bsl, ident[0:64, 0:64]),
                        r=["B64", "ident"], w=[pk])
                    if hh % 2 == 0:
                        b.op("vector", lambda h, hh=hh, bank=bank: h.tensor_copy(
                            s_Bt[:, hh * 64:(hh + 1) * 64], bank[:, 0:64]),
                            r=[pk], w=["Bt%d" % hh])
                    else:
                        b.op("scalar", lambda h, hh=hh, bank=bank: h.copy(
                            s_Bt[:, hh * 64:(hh + 1) * 64], bank[:, 0:64]),
                            r=[pk], w=["Bt%d" % hh])
                if isY:
                    for bk in range(4):
                        items = []
                        for j in range(4):
                            hh = bk * 4 + j
                            items.append((lambda h, hh=hh, j=j, bk=bk, ci=ci,
                                          first=(j == 0), lastf=(j == 3): h.matmul(
                                pb[bk][:, j * 128:(j + 1) * 128],
                                s_B64[:, hh * 512 + ci * 128:hh * 512 + (ci + 1) * 128],
                                s_C64[:, hh * 512 + ci * 128:hh * 512 + (ci + 1) * 128],
                                start=first, stop=lastf), ["B64", "C64"]))
                        b.mmg(PB[bk], items)
                    for bk in range(4):
                        b.op("vector", lambda h, bk=bk: h.tensor_tensor(
                            s_ms[:, bk * 512:(bk + 1) * 512].rearrange("p (g f) -> p g f", f=128),
                            pb[bk].rearrange("p (g f) -> p g f", f=128),
                            umask.unsqueeze(1).to_broadcast([128, 4, 128]), ALU.mult),
                            r=[PB[bk], "umask"], w=["ms%d" % bk])
                    for half in range(2):
                        items = []
                        for j in range(8):
                            hh = half * 8 + j
                            off = j * 64
                            items.append((lambda h, hh=hh, off=off, half=half,
                                          first=(j == 0): h.matmul(
                                pb[4 + half][:, off:off + 64],
                                s_ms[:, hh * 128:(hh + 1) * 128],
                                s_xt[:, hh * 64:(hh + 1) * 64],
                                start=first, stop=False),
                                ["ms%d" % (hh // 4), "xt%d" % hh]))
                            items.append((lambda h, hh=hh, off=off, ci=ci, half=half,
                                          lastf=(j == 7): h.matmul(
                                pb[4 + half][:, off:off + 64],
                                s_C64[:, hh * 512 + ci * 128:hh * 512 + (ci + 1) * 128],
                                s_ACC[:, hh * 64:(hh + 1) * 64],
                                start=False, stop=lastf),
                                ["C64", "ACC%d" % half]))
                        b.mmg(PB[4 + half], items)
                    for hh in range(NH):
                        half, off = hh // 8, (hh % 8) * 64
                        if half == 0:
                            b.op("vector", lambda h, hh=hh, half=half, off=off, ci=ci: h.tensor_scalar_mul(
                                s_Y[:, hh * 64:(hh + 1) * 64], pb[4 + half][:, off:off + 64],
                                s_ft[:, ci * 16 + hh:ci * 16 + hh + 1]),
                                r=[PB[4 + half], "ft"], w=["sy%d" % hh])
                        else:
                            b.op("scalar", lambda h, hh=hh, half=half, off=off, ci=ci: h.activation(
                                s_Y[:, hh * 64:(hh + 1) * 64], pb[4 + half][:, off:off + 64],
                                AF.Copy, scale=s_ft[:, ci * 16 + hh:ci * 16 + hh + 1]),
                                r=[PB[4 + half], "ft"], w=["sy%d" % hh])
                    for hh in range(NH):
                        bank, pk = pb[6 + hh % 2], PB[6 + hh % 2]
                        b.op("tensor", lambda h, hh=hh, bank=bank: h.transpose(
                            bank[0:64, 0:128], s_Y[:, hh * 64:(hh + 1) * 64],
                            ident), r=["sy%d" % hh, "ident"], w=[pk])
                        dst_t = s_Yne if hh % 2 == 0 else s_Yno
                        fcq = hh // 2
                        dst = dst_t[:, fcq * 512 + ci * 128:fcq * 512 + (ci + 1) * 128]
                        if hh % 2 == 0:
                            b.op("vector", lambda h, dst=dst, bank=bank: h.tensor_copy(
                                dst, bank[0:64, 0:128]), r=[pk], w=["yn%d" % hh])
                        else:
                            b.op("scalar", lambda h, dst=dst, bank=bank: h.copy(
                                dst, bank[0:64, 0:128]), r=[pk], w=["yn%d" % hh])
                for half in range(2):
                    items = []
                    for j in range(8):
                        hh = half * 8 + j
                        off = j * 64
                        items.append((lambda h, hh=hh, off=off, half=half,
                                      first=(j == 0), lastf=(j == 7): h.matmul(
                            pb[2 + half][0:64, off:off + 64],
                            s_Bt[:, hh * 64:(hh + 1) * 64],
                            s_xt[:, hh * 64:(hh + 1) * 64],
                            start=first, stop=lastf),
                            ["Bt%d" % hh, "xt%d" % hh]))
                    b.mmg(PB[2 + half], items)
                for half in range(2):
                    b.op("vector", lambda h, half=half: h.tensor_tensor(
                        s_ACC[:, half * 512:(half + 1) * 512],
                        s_ACC[:, half * 512:(half + 1) * 512],
                        pb[2 + half][0:64, 0:512], ALU.add),
                        r=[PB[2 + half], "ACC%d" % half], w=["ACC%d" % half])
            if isY:
                b.op("scalar", lambda h, t=t: h.dma_start(
                    out=YnT4[0][:, :, (t - 8) * TILE:(t - 7) * TILE],
                    in_=s_Yne.rearrange("p (g f) -> p g f", f=512)),
                    r=["yn%d" % i2 for i2 in range(0, 16, 2)], chan=c_yn[0])
                b.op("scalar", lambda h, t=t: h.dma_start(
                    out=YnT4[1][:, :, (t - 8) * TILE:(t - 7) * TILE],
                    in_=s_Yno.rearrange("p (g f) -> p g f", f=512)),
                    r=["yn%d" % i2 for i2 in range(1, 16, 2)], chan=c_yn[1])
        b.barrier()

    if upto >= 4:
        f_xf = arena[:, 0:4096]
        f_q = arena[:, 4096:8192]
        f_sgn = arena[:, 8192:12288]
        f_scl = arena[:, 12288:12800]
        f_xqf2 = [arena[:, 12800:16896], arena[:, 16896:20992]]
        f_h = arena[:, 20992:32256]
        f_wq = arena[:, 32256:36352]
        f_ws = [arena[:, 36352:37376], arena[:, 37376:38400]]
        f_res = arena[:, 38400:42496]
        f_hscl = f_res[:, 0:352]
        f_sil = f_res[:, 352:2400]

        c_xf = [b.chan("xfc0"), b.chan("xfc1")]
        c_wq = [b.chan("wqc0"), b.chan("wqc1")]
        c_ynr = [b.chan("ynr0"), b.chan("ynr1")]
        c_ws = [b.chan("wsc0"), b.chan("wsc1")]
        c_out = b.chan("outc")
        c_res = b.chan("resc")

        SPANS = [(0, 512), (512, 512), (1024, 512), (1536, 512), (2048, 512), (2560, 256)]

        def emit_A(t):
            b.op("sync", lambda h, t=t: h.dma_start(
                out=f_xf.rearrange("p (s d) -> p s d", d=1024),
                in_=xf_in[t * TILE:(t + 1) * TILE, :].rearrange("(s p) d -> p s d", p=128)),
                w=["fx"] + ["fxq%d" % s2 for s2 in range(4)], chan=c_xf[t % 2])
            for st in range(4):
                xs = f_xf[:, st * 1024:(st + 1) * 1024]
                dum = f_q[:, st * 1024:st * 1024 + 512]
                b.op("scalar", lambda h, xs=xs, dum=dum, st=st: h.activation(
                    dum, xs[:, 0:512], AF.Square, accum_out=fstat[:, st:st + 1]),
                    r=["fx"], w=["fs2a%d" % st, "fxn%d" % st, "fhs"])
                b.op("scalar", lambda h, xs=xs, dum=dum, st=st: h.activation(
                    dum, xs[:, 512:1024], AF.Square, accum_out=fstat[:, 4 + st:5 + st]),
                    r=["fx", "fs2a%d" % st], w=["fs2b%d" % st, "fxn%d" % st])
                b.op("vector", lambda h, xs=xs, st=st: h.reduce_sum(
                    fstat[:, 8 + st:9 + st], xs, AX.X), r=["fx"], w=["fs1%d" % st])
                b.op("vector", lambda h, st=st: h.tensor_single_scalar(
                    fstat[:, 8 + st:9 + st], fstat[:, 8 + st:9 + st], -1.0 / D, ALU.mult),
                    r=["fs1%d" % st], w=["fs1%d" % st])
                b.op("vector", lambda h, st=st: h.tensor_tensor(
                    fstat[:, 12 + st:13 + st], fstat[:, st:st + 1],
                    fstat[:, 4 + st:5 + st], ALU.add),
                    r=["fs2a%d" % st, "fs2b%d" % st], w=["fv%d" % st])
                b.op("vector", lambda h, st=st: h.tensor_single_scalar(
                    fstat[:, 12 + st:13 + st], fstat[:, 12 + st:13 + st], 1.0 / D, ALU.mult),
                    r=["fv%d" % st], w=["fv%d" % st])
                b.op("vector", lambda h, st=st: h.scalar_tensor_tensor(
                    fstat[:, 12 + st:13 + st], fstat[:, 8 + st:9 + st],
                    fstat[:, 8 + st:9 + st], fstat[:, 12 + st:13 + st],
                    ALU.mult, ALU.subtract),
                    r=["fs1%d" % st, "fv%d" % st], w=["fv%d" % st])
                b.op("scalar", lambda h, st=st: h.activation(
                    fstat[:, 12 + st:13 + st], fstat[:, 12 + st:13 + st], AF.Sqrt,
                    bias=cst[:, 0:1], scale=-1.0), r=["fv%d" % st], w=["fv%d" % st])
                b.op("vector", lambda h, st=st: h.reciprocal(
                    fstat[:, 12 + st:13 + st], fstat[:, 12 + st:13 + st]),
                    r=["fv%d" % st], w=["fistd%d" % st])
                b.op("vector", lambda h, xs=xs, st=st: h.tensor_scalar(
                    f_q[:, st * 1024:(st + 1) * 1024], xs,
                    fstat[:, 8 + st:9 + st], fstat[:, 12 + st:13 + st], ALU.add, ALU.mult),
                    r=["fx", "fs1%d" % st, "fistd%d" % st],
                    w=["fxn%d" % st, "fhs"])
                quant_tm(f_q[:, st * 1024:(st + 1) * 1024], "fxn%d" % st,
                         f_sgn[:, st * 1024:(st + 1) * 1024], "fsgn%d" % st,
                         f_xf[:, st * 1024:(st + 1) * 1024], "fxq%d" % st,
                         f_scl[:, st * 128:(st + 1) * 128], "fscl%d" % st, 1024,
                         eng="gpsimd" if st % 2 else "vector",
                         claim=["fx"], sclaim=["fhqq0", "fhqq1"])

        def emit_XT(t):
            p = t % 2
            for st in range(4):
                for kc in range(8):
                    bank, pk = pb[6 + kc % 2], PB[6 + kc % 2]
                    dk = "xqf%d_%d_%d" % (kc, st, p)
                    b.op("tensor", lambda h, st=st, kc=kc, bank=bank: h.transpose(
                        bank[:, 0:128],
                        f_xf[:, st * 1024 + kc * 128:st * 1024 + (kc + 1) * 128],
                        ident), r=["fxq%d" % st, "ident"], w=[pk])
                    if kc % 2 == 0:
                        b.op("vector", lambda h, kc=kc, st=st, bank=bank, p=p:
                             h.tensor_copy(
                            f_xqf2[p][:, kc * 512 + st * 128:kc * 512 + (st + 1) * 128],
                            bank[:, 0:128]), r=[pk], w=[dk])
                    else:
                        b.op("scalar", lambda h, kc=kc, st=st, bank=bank, p=p: h.copy(
                            f_xqf2[p][:, kc * 512 + st * 128:kc * 512 + (st + 1) * 128],
                            bank[:, 0:128]), r=[pk], w=[dk])

        def emit_Dpre(t):
            b.op("sync", lambda h, t=t: h.dma_start(
                out=f_wq.rearrange("p (c f) -> p c f", f=512),
                in_=YnT[:, t * TILE:(t + 1) * TILE].rearrange("(c p) f -> p c f", p=128)),
                w=["fyn", "fwq"], chan=c_ynr[t % 2])

        def emit_MID(t):
            p = t % 2
            for (soff, w) in SPANS:
                for wi, wsrc in enumerate((wgqT, wuqT)):
                    b.op("sync", lambda h, wsrc=wsrc, soff=soff, w=w: h.dma_start(
                        out=f_wq[:, 0:8 * w].rearrange("p (c f) -> p c f", f=w),
                        in_=wsrc[:, soff:soff + w].rearrange("(c p) f -> p c f", p=128)),
                        w=["fwq"], chan=c_wq[wi])
                    for st in range(4):
                        b.mmg(PB[st], [
                            (lambda h, st=st, kc=kc, w=w, p=p: h.matmul(
                                pb[st][:, 0:w],
                                f_xqf2[p][:, kc * 512 + st * 128:kc * 512 + (st + 1) * 128],
                                f_wq[:, kc * w:(kc + 1) * w],
                                start=(kc == 0), stop=(kc == 7)),
                             ["fwq", "xqf%d_%d_%d" % (kc, st, p)]) for kc in range(8)])
                        if wi == 0:
                            b.op("scalar", lambda h, st=st, w=w: h.activation(
                                f_sil[:, st * 512:st * 512 + w], pb[st][:, 0:w], AF.Silu),
                                r=[PB[st]], w=["fsil%d" % st, "fres"])
                        else:
                            b.op("vector", lambda h, st=st, w=w, soff=soff: h.tensor_tensor(
                                f_h[:, st * 2816 + soff:st * 2816 + soff + w],
                                f_sil[:, st * 512:st * 512 + w], pb[st][:, 0:w], ALU.mult),
                                r=["fsil%d" % st, PB[st]], w=["fh%d" % st])

        def emit_C(t):
            for st in range(4):
                hwin = f_h[:, st * 2816:(st + 1) * 2816]
                for hf in range(2):
                    o = hf * 1408
                    quant_tm(hwin[:, o:o + 1408], "fh%d" % st,
                             f_q[:, o:o + 1408], "fhs",
                             f_sgn[:, o:o + 1408], "fhqq%d" % hf,
                             f_hscl[:, hf * 176:hf * 176 + 176], "fhscl%d" % hf, 1408,
                             eng="vector",
                             claim=["fsgn0", "fsgn1", "fsgn2", "fsgn3"],
                             sclaim=["fxn0", "fxn1", "fxn2", "fhs"],
                             sclclaim=["fres"])
                    for kc in range(11 * hf, 11 * (hf + 1)):
                        bank, pk = pb[4 + kc % 2], PB[4 + kc % 2]
                        dk = "hqf%d_%d" % (kc, st)
                        b.op("tensor", lambda h, kc=kc, hf=hf, bank=bank: h.transpose(
                            bank[:, 0:128], f_sgn[:, kc * 128:(kc + 1) * 128], ident),
                            r=["fhqq%d" % hf, "ident"], w=[pk])
                        if kc % 2 == 0:
                            b.op("vector", lambda h, st=st, kc=kc, bank=bank:
                                 h.tensor_copy(
                                f_h[:, st * 2816 + kc * 128:st * 2816 + (kc + 1) * 128],
                                bank[:, 0:128]), r=[pk], w=[dk, "fh%d" % st])
                        else:
                            b.op("scalar", lambda h, st=st, kc=kc, bank=bank: h.copy(
                                f_h[:, st * 2816 + kc * 128:st * 2816 + (kc + 1) * 128],
                                bank[:, 0:128]), r=[pk], w=[dk, "fh%d" % st])

        def emit_D(t):
            b.op("gpsimd", lambda h, t=t: h.dma_start(
                out=f_res.rearrange("p (c f) -> p c f", f=512),
                in_=xfT[:, t * TILE:(t + 1) * TILE].rearrange("(c p) f -> p c f", p=128)),
                w=["fres", "fhscl0", "fhscl0i", "fhscl1", "fhscl1i"]
                  + ["fsil%d" % s2 for s2 in range(4)],
                chan=c_res)
            for dm in range(8):
                b.hazards("tensor", w=[PB[dm]])
            for di in range(8):
                ws, wsk = f_ws[di % 2], "fws%d" % (di % 2)
                b.op("sync", lambda h, di=di, ws=ws: h.dma_start(
                    out=ws, in_=W_outT[di * 128:(di + 1) * 128, :]),
                    w=[wsk], chan=c_ws[di % 2])
                for dm in range(8):
                    last = dm == 7
                    b.op("tensor", lambda h, di=di, dm=dm, ws=ws: h.matmul(
                        pb[dm], ws[:, dm * 128:(dm + 1) * 128],
                        f_wq[:, di * 512:(di + 1) * 512],
                        start=(di == 0), stop=False, skip_group_check=True),
                        r=([wsk, "fyn", "fwq"] if last else [wsk, "fyn"]),
                        raw=not last)
            f_h4 = f_h.rearrange("p (st x j) -> p st x j", st=4, j=128)
            for kc in range(22):
                ws, wsk = f_ws[kc % 2], "fws%d" % (kc % 2)
                b.op("sync", lambda h, kc=kc, ws=ws: h.dma_start(
                    out=ws, in_=wdqT[kc * 128:(kc + 1) * 128, :]),
                    w=[wsk], chan=c_ws[kc % 2])
                hkeys = ["hqf%d_%d" % (kc, s2) for s2 in range(4)]
                fin = kc == 21
                for dm in range(8):
                    kclast = dm == 7
                    b.op("tensor", lambda h, kc=kc, dm=dm, ws=ws, fin=fin: h.matmul(
                        pb[dm],
                        ws[:, dm * 128:(dm + 1) * 128],
                        f_h4[:, :, kc, :],
                        start=False, stop=fin, skip_group_check=True),
                        r=[wsk] + hkeys,
                        w=([PB[dm]] if fin else ()),
                        raw=not (fin or kclast))

        def emit_TAIL(t):
            for dm in range(8):
                fhk = ["fh0"] if dm < 5 else (["fh0", "fh1"] if dm == 5 else ["fh1"])
                b.op("vector", lambda h, dm=dm: h.tensor_tensor(
                    f_h[:, dm * 512:(dm + 1) * 512], pb[dm],
                    f_res[:, dm * 512:(dm + 1) * 512], ALU.add),
                    r=[PB[dm], PB[7], "fres"], w=["ofm%d" % dm] + fhk)
            for st in range(4):
                bk0 = 4 + 2 * (st % 2)
                for dm in range(8):
                    bank, pk = pb[bk0 + dm // 4], PB[bk0 + dm // 4]
                    fhk2 = ["fh0"] if dm < 5 else (["fh0", "fh1"] if dm == 5 else ["fh1"])
                    b.op("tensor", lambda h, st=st, dm=dm, bank=bank: h.transpose(
                        bank[:, (dm % 4) * 128:(dm % 4) * 128 + 128],
                        f_h[:, dm * 512 + st * 128:dm * 512 + (st + 1) * 128], ident),
                        r=["ofm%d" % dm, "ident"] + fhk2, w=[pk])
                for hb in range(2):
                    bank, pk = pb[bk0 + hb], PB[bk0 + hb]
                    b.op("vector", lambda h, hb=hb, bank=bank: h.tensor_copy(
                        f_h[:, 4096 + hb * 512:4096 + (hb + 1) * 512], bank[:, 0:512]),
                        r=[pk], w=["otm%d" % hb, "fh1"])
                b.op("gpsimd", lambda h, t=t, st=st: h.dma_start(
                    out=out_h[t * TILE + st * 128:t * TILE + (st + 1) * 128, :],
                    in_=f_h[:, 4096:5120]),
                    r=["otm0", "otm1", "fh1"], chan=c_out)

        emit_A(0)
        emit_XT(0)
        for t in range(8):
            if t >= 1:
                emit_TAIL(t - 1)
            if t + 1 <= 7:
                emit_A(t + 1)
            emit_MID(t)
            emit_Dpre(t)
            if t + 1 <= 7:
                emit_XT(t + 1)
            emit_C(t)
            emit_D(t)
        emit_TAIL(7)

    b.barrier()

    with nc.Block() as block:
        b.finalize(block)
    es.close()
    return nc


_NC_CACHE = {}


def _get_nc():
    if "nc" not in _NC_CACHE:
        _NC_CACHE["nc"] = build_nc()
    return _NC_CACHE["nc"]


def make_in_maps(inputs):
    x = np.asarray(inputs["L0_attn_x"], np.float32)
    W_in = np.asarray(inputs["L0_attn_W_in"], np.float32)
    conv_w = np.asarray(inputs["L0_attn_conv_w"], np.float32)
    conv_b = np.asarray(inputs["L0_attn_conv_b"], np.float32)
    A_param = np.asarray(inputs["L0_attn_A_param"], np.float32)
    dt_bias = np.asarray(inputs["L0_attn_dt_bias"], np.float32)
    W_out = np.asarray(inputs["L0_attn_W_out"], np.float32)
    wg = np.asarray(inputs["L0_ffn_w_gate"], np.float32)
    wu = np.asarray(inputs["L0_ffn_w_up"], np.float32)
    wd = np.asarray(inputs["L0_ffn_w_down"], np.float32)

    W_inT = np.ascontiguousarray(W_in.T)
    W_outT = np.ascontiguousarray(W_out.T)
    conv_b2 = np.ascontiguousarray(conv_b[:, None])
    A2 = np.ascontiguousarray(A_param[:, None])
    dtb2 = np.ascontiguousarray(dt_bias[:, None])
    ident = np.eye(128, dtype=np.float32)
    identb = np.eye(128, dtype=ml_dtypes.bfloat16)
    um = np.triu(np.ones((128, 128), np.float32))
    onesc = np.ones((128, 1), np.float32)
    ones1 = np.ones((1, 128), np.float32)
    cstv = np.zeros((128, 2), np.float32)
    cstv[:, 0] = EPS
    cstv[:, 1] = 1.0

    x0 = x[0]
    x0T_full = np.ascontiguousarray(x0.T)
    pad0 = np.zeros((D, HALF), np.float32)
    x0T_sh0 = np.ascontiguousarray(np.concatenate([pad0, x0T_full[:, :HALF]], axis=1))
    m_h0 = np.zeros((NH, S), np.float32)
    m_h0[:, HALF:] = 1.0
    m_h1 = np.ones((NH, S), np.float32)

    in_maps = []
    for c in range(8):
        bb, hh = c // 2, c % 2
        xb = x[bb]
        xbT_full = np.ascontiguousarray(xb.T)
        if hh == 0:
            x0T_c = x0T_sh0
            xbT_c = np.ascontiguousarray(
                np.concatenate([pad0, xbT_full[:, :HALF]], axis=1))
            m_c = m_h0
        else:
            x0T_c = x0T_full
            xbT_c = xbT_full
            m_c = m_h1
        xf = np.ascontiguousarray(xb[hh * HALF:(hh + 1) * HALF])
        xfT = np.ascontiguousarray(xf.T)
        in_maps.append({
            "x0T": x0T_c, "xbT": xbT_c, "xf": xf, "xfT": xfT, "m16": m_c,
            "W_inT": W_inT, "convw": conv_w, "convb": conv_b2,
            "A_param": A2, "dt_bias": dtb2, "W_outT": W_outT,
            "wg": wg, "wu": wu, "wd": wd,
            "ident_in": ident, "identb_in": identb,
            "umask_in": um, "onesc_in": onesc, "ones1_in": ones1,
            "cst_in": cstv,
        })
    return in_maps


def kernel(**inputs):
    nc = _get_nc()
    in_maps = make_in_maps(inputs)
    res = run_bass_kernel_spmd(nc, in_maps, core_ids=list(range(8)))
    out = np.zeros((4, S, D), np.float32)
    for c in range(8):
        bb, hh = c // 2, c % 2
        out[bb, hh * HALF:(hh + 1) * HALF] = res.results[c]["out"]
    return out
